# version 25
# speedup vs baseline: 1.0286x; 1.0055x over previous
import numpy as np
import ml_dtypes
from contextlib import ExitStack
import concourse.bass as bass
import concourse.mybir as mybir
from concourse.bass_utils import run_bass_kernel_spmd

F32 = mybir.dt.float32
BF16 = mybir.dt.bfloat16
AF = mybir.ActivationFunctionType
ALU = mybir.AluOpType
AXX = mybir.AxisListType.X
MAGIC = 12582912.0
NEG_FILL = float(np.finfo(np.float32).min)
D = 1024
NF = 640
BF = ml_dtypes.bfloat16

ENGS = ("sync", "tensor", "scalar", "vector", "gpsimd")


def _dft_consts():
    f = np.arange(D // 2 + 1, dtype=np.float64)
    d = np.arange(D, dtype=np.float64)
    ang = 2.0 * np.pi * np.outer(d, f) / D
    Fpad = np.zeros((D, 2 * NF), np.float64)
    Fpad[:, 0:D // 2 + 1] = np.cos(ang)
    Fpad[:, NF:NF + D // 2 + 1] = -np.sin(ang)
    w = np.full(D // 2 + 1, 2.0)
    w[0] = 1.0
    w[-1] = 1.0
    ang2 = 2.0 * np.pi * np.outer(f, d) / D
    C = np.zeros((NF, 2 * D), np.float64)
    C[0:D // 2 + 1, 0:D] = (w[:, None] / D) * np.cos(ang2)
    C[0:D // 2 + 1, D:2 * D] = -(w[:, None] / D) * np.sin(ang2)
    return np.ascontiguousarray(Fpad.astype(BF)), np.ascontiguousarray(C.astype(BF))


def _col(v):
    v = np.asarray(v, np.float32)
    return np.ascontiguousarray(v.reshape(-1, 128).T)


class Prog:

    NSD = 4
    NGD = 4

    def __init__(self, nc, es):
        self.nc = nc
        names = ["te", "sc", "ve", "gp", "sy", "cc"]
        names += ["sd%d" % i for i in range(self.NSD)]
        names += ["gd%d" % i for i in range(self.NGD)]
        self.sems = {n: es.enter_context(nc.semaphore("S_" + n)) for n in names}
        self.cnt = {n: 0 for n in names}
        self.ops = {e: [] for e in ENGS}
        self.lastwait = {e: {n: 0 for n in names} for e in ENGS}
        self.chan_cons = {n: [] for n in names if n[:2] in ("sd", "gd") or n == "cc"}
        self.engsem = {"sync": "sy", "tensor": "te", "scalar": "sc",
                       "vector": "ve", "gpsimd": "gp"}

    def _sched(self, eng, waits, emit, inc_sem, inc_by):
        ws = []
        chans_waited = []
        seen = {}
        for (sn, v) in waits:
            if sn in self.chan_cons:
                v = self.cnt[sn]
                chans_waited.append(sn)
            seen[sn] = max(seen.get(sn, 0), v)
        for sn, v in seen.items():
            if v > self.lastwait[eng][sn]:
                ws.append((sn, v))
                self.lastwait[eng][sn] = v
        tok = None
        if inc_sem is not None:
            self.cnt[inc_sem] += inc_by
            tok = (inc_sem, self.cnt[inc_sem])
        my_tok = tok if tok is not None else \
            (self.engsem[eng], self.cnt[self.engsem[eng]])
        for sn in chans_waited:
            if my_tok[1] > 0:
                self.chan_cons[sn].append(my_tok)
        sems = self.sems

        def f(E, ws=tuple(ws), emit=emit, inc=inc_sem, by=inc_by):
            for (sn, v) in ws:
                E.wait_ge(sems[sn], v)
            ins = emit(E)
            if inc is not None and ins is not None:
                ins.then_inc(sems[inc], by)
        self.ops[eng].append(f)
        return tok

    def _dma(self, eng, sem, dst, src, waits):
        gate = self.chan_cons[sem]
        self.chan_cons[sem] = []
        return self._sched(eng, list(waits) + gate,
                           lambda E: E.dma_start(dst, src), sem, 16)

    def sdma(self, dst, src, waits=(), ch=0):
        return self._dma("sync", "sd%d" % ch, dst, src, waits)

    def gdma(self, dst, src, waits=(), ch=0):
        return self._dma("gpsimd", "gd%d" % ch, dst, src, waits)

    def t(self, emit, waits=()):
        return self._sched("tensor", waits, emit, "te", 1)

    def s(self, emit, waits=()):
        return self._sched("scalar", waits, emit, "sc", 1)

    def v(self, emit, waits=()):
        return self._sched("vector", waits, emit, "ve", 1)

    def g(self, emit, waits=()):
        return self._sched("gpsimd", waits, emit, "gp", 1)

    def wait_on(self, eng, waits):
        self._sched(eng, waits, lambda E: None, None, 0)

    def collective_max(self, cin, cout, n_cores, waits=()):
        rg = [list(range(n_cores))]
        gate = self.chan_cons["cc"]
        self.chan_cons["cc"] = []

        def emit(E):
            return E.collective_compute("AllReduce", ALU.max, rg, [cin], [cout])
        return self._sched("gpsimd", list(waits) + gate, emit, "cc", 16)

    def now(self, sem):
        return (sem, self.cnt[sem])

    def alldma(self):
        out = [("sd%d" % i, self.cnt["sd%d" % i]) for i in range(self.NSD)]
        out += [("gd%d" % i, self.cnt["gd%d" % i]) for i in range(self.NGD)]
        return [t for t in out if t[1] > 0]

    def flush(self):
        if not any(self.ops[e] for e in ENGS):
            return
        with self.nc.Block() as block:
            for eng in ENGS:
                if self.ops[eng]:
                    getattr(block, eng)(
                        lambda E, fs=tuple(self.ops[eng]): [f(E) for f in fs])
        self.ops = {e: [] for e in ENGS}
        for e in ENGS:
            for n in ("te", "sc", "ve", "gp", "sy"):
                self.lastwait[e][n] = self.cnt[n]


def em_ln_stats(p, bufs, x_hbm, S):
    TC = S // 512
    xt2, sq = bufs["xt2"], bufs["sq"]
    onec = bufs["onec"]
    mu_row, sq_row = bufs["mu_row"], bufs["sq_row"]
    a_row = bufs["a_row"]
    pC, pD = bufs["pC"], bufs["pD"]
    cons = {0: (), 1: ()}
    for tc in range(TC):
        for dc in range(8):
            h = dc % 2
            buf = xt2[:, h * 512:h * 512 + 512]
            ld = p.sdma(buf, x_hbm[dc * 128:(dc + 1) * 128,
                                   tc * 512:(tc + 1) * 512], waits=cons[h],
                        ch=1 + h)
            m1 = p.t(lambda E, b=buf, dc=dc: E.matmul(
                pC[0:1, 0:512], onec[:], b, start=(dc == 0), stop=(dc == 7)),
                waits=[ld])
            sv = p.v(lambda E, b=buf: E.tensor_tensor(sq[:], b, b, ALU.mult),
                     waits=[ld, p.now("te")])
            m2 = p.t(lambda E, dc=dc: E.matmul(
                pD[0:1, 0:512], onec[:], sq[:], start=(dc == 0), stop=(dc == 7)),
                waits=[sv])
            cons[h] = (m1, sv)
        tmm = p.now("te")
        e1 = p.v(lambda E, tc=tc: E.tensor_copy(
            mu_row[0:1, tc * 512:(tc + 1) * 512], pC[0:1, 0:512]), waits=[tmm])
        e2 = p.v(lambda E, tc=tc: E.tensor_copy(
            sq_row[0:1, tc * 512:(tc + 1) * 512], pD[0:1, 0:512]), waits=[tmm])
        p.wait_on("tensor", [e1, e2])
        last_ev = (e1, e2)
    t1 = p.v(lambda E: E.tensor_scalar(mu_row[:], mu_row[:], 1.0 / D, None,
                                       ALU.mult), waits=list(last_ev))
    t2 = p.v(lambda E: E.tensor_scalar(sq_row[:], sq_row[:], 1.0 / D, None,
                                       ALU.mult), waits=[t1])
    t3 = p.v(lambda E: E.scalar_tensor_tensor(a_row[:], mu_row[:], 1.0,
                                              mu_row[:], ALU.mult, ALU.mult),
             waits=[t2])
    t4 = p.v(lambda E: E.tensor_tensor(sq_row[:], sq_row[:], a_row[:],
                                       ALU.subtract), waits=[t3])
    te = p.v(lambda E: E.memset(bufs["eps1"][:], 1e-5))
    t5 = p.s(lambda E: E.activation(sq_row[:], sq_row[:], AF.Ln,
                                    bias=bufs["eps1"][0:1, 0:1]), waits=[t4, te])
    t6 = p.s(lambda E: E.activation(a_row[:], sq_row[:], AF.Exp, scale=-0.5),
             waits=[t5])
    t7 = p.v(lambda E: E.scalar_tensor_tensor(bufs["b_row"][:], mu_row[:],
                                              -1.0, a_row[:], ALU.mult,
                                              ALU.mult), waits=[t6])
    return t7


def em_ln_bcast(p, bufs, S, waits=()):
    a_row, b_row = bufs["a_row"], bufs["b_row"]
    a_bc, b_bc = bufs["a_bc"], bufs["b_bc"]
    oner = bufs["oner"]
    pC, pD = bufs["pC"], bufs["pD"]
    last = None
    for tc in range(S // 512):
        ta = p.t(lambda E, tc=tc: E.matmul(
            pC[:, 0:512], oner[:], a_row[0:1, tc * 512:(tc + 1) * 512],
            start=True, stop=True), waits=list(waits) + [p.now("ve")])
        va = p.v(lambda E, tc=tc: E.tensor_copy(
            a_bc[:, tc * 512:(tc + 1) * 512], pC[:, 0:512]), waits=[ta])
        tb = p.t(lambda E, tc=tc: E.matmul(
            pD[:, 0:512], oner[:], b_row[0:1, tc * 512:(tc + 1) * 512],
            start=True, stop=True), waits=waits)
        vb = p.v(lambda E, tc=tc: E.tensor_copy(
            b_bc[:, tc * 512:(tc + 1) * 512], pD[:, 0:512]), waits=[tb])
        p.wait_on("tensor", [va, vb])
        last = vb
    return last


def em_xn(p, bufs, x_hbm, S, xn_bf=None, xnq_bf=None, q_invs=None, q_s=None,
          max_col=None, waits=()):
    a_bc, b_bc = bufs["a_bc"], bufs["b_bc"]
    dbl = bufs.get("xrow_b") is not None
    xr2 = [bufs["xrow"], bufs.get("xrow_b", bufs["xrow"])]
    tp2 = [bufs["tmprow"], bufs.get("tmprow_b", bufs["tmprow"])]
    mc2 = [bufs["mcol1"], bufs.get("mcol_b", bufs["mcol1"])]
    rd = [dict(), dict()]
    acc_tok = ()
    last = None
    for dc in range(8):
        bq = (dc % 2) if dbl else 0
        xr = xr2[bq]
        ld = p.sdma(xr[:, 0:S], x_hbm[dc * 128:(dc + 1) * 128, :],
                    waits=list(rd[bq].get("xr", ())) + list(waits), ch=1)
        t1 = p.v(lambda E, xr=xr: E.tensor_tensor(
            xr[:, 0:S], xr[:, 0:S], a_bc[:], ALU.mult),
            waits=[ld] + list(waits))
        t2 = p.v(lambda E, xr=xr: E.tensor_tensor(
            xr[:, 0:S], xr[:, 0:S], b_bc[:], ALU.add), waits=[t1])
        readers = []
        if max_col is not None:
            t3 = p.v(lambda E, xr=xr, bq=bq: E.tensor_reduce(
                mc2[bq][:], xr[:, 0:S], AXX, ALU.max,
                apply_absolute_value=True),
                waits=[t2] + list(rd[bq].get("mc", ())))
            if dc == 0:
                acc = p.v(lambda E, bq=bq: E.tensor_copy(
                    max_col[:], mc2[bq][:]), waits=[t3])
            else:
                acc = p.v(lambda E, bq=bq: E.tensor_tensor(
                    max_col[:], max_col[:], mc2[bq][:], ALU.max),
                    waits=[t3] + list(acc_tok))
            acc_tok = (acc,)
            rd[bq]["mc"] = (acc,)
            readers.append(t3)
            last = acc
        if xn_bf is not None:
            cp = p.v(lambda E, dc=dc, xr=xr: E.tensor_copy(
                xn_bf[:, dc * S:(dc + 1) * S], xr[:, 0:S]), waits=[t2])
            readers.append(cp)
            last = cp
        if xnq_bf is not None:
            mgc = bufs.get("magicc")
            if mgc is not None:
                q1 = p.s(lambda E, xr=xr, bq=bq: E.activation(
                    tp2[bq][:, 0:S], xr[:, 0:S], AF.Identity, scale=q_invs,
                    bias=mgc[:, 0:1]),
                    waits=[t2] + list(rd[bq].get("tp", ())))
            else:
                q1 = p.v(lambda E, xr=xr, bq=bq: E.tensor_scalar(
                    tp2[bq][:, 0:S], xr[:, 0:S], q_invs, MAGIC, ALU.mult,
                    ALU.add), waits=[t2] + list(rd[bq].get("tp", ())))
            q2 = p.v(lambda E, dc=dc, bq=bq: E.tensor_scalar(
                xnq_bf[:, dc * S:(dc + 1) * S], tp2[bq][:, 0:S], -MAGIC, q_s,
                ALU.add, ALU.mult), waits=[q1])
            rd[bq]["tp"] = (q2,)
            readers.append(q1)
            last = q2
        rd[bq]["xr"] = tuple(readers) if readers else (t2,)
    return last


def em_fold(p, bufs, wT_bf, out_bf, bcol_bf, bFcol, waits=()):
    Fpad = bufs["Fpad"]
    pA, pC = bufs["pA"], bufs["pC"]
    for dm in range(8):
        for fc in range(5):
            mm = None
            for kc in range(8):
                w = (list(waits) + [p.now("ve")]) if kc == 0 else ()
                mm = p.t(lambda E, dm=dm, fc=fc, kc=kc: E.matmul(
                    pA[:, 0:256],
                    wT_bf[:, kc * 1024 + dm * 128:kc * 1024 + (dm + 1) * 128],
                    Fpad[:, kc * 1280 + fc * 256:kc * 1280 + (fc + 1) * 256],
                    start=(kc == 0), stop=(kc == 7)), waits=w)
            p.v(lambda E, dm=dm, fc=fc: E.tensor_copy(
                out_bf[:, dm * 1280 + fc * 256:dm * 1280 + (fc + 1) * 256],
                pA[:, 0:256]), waits=[mm])
    for ft in range(10):
        mm = None
        for kc in range(8):
            w = [p.now("ve")] if kc == 0 else ()
            mm = p.t(lambda E, ft=ft, kc=kc: E.matmul(
                pC[:, 0:1],
                Fpad[:, kc * 1280 + ft * 128:kc * 1280 + (ft + 1) * 128],
                bcol_bf[:, kc:kc + 1],
                start=(kc == 0), stop=(kc == 7)), waits=w)
        p.v(lambda E, ft=ft: E.tensor_copy(bFcol[:, ft:ft + 1], pC[:, 0:1]),
            waits=[mm])
    return p.now("ve")


def em_rfft(p, bufs, WxF, bFcol, src_bf, tc, S, out32, waits=()):
    pA, pB = bufs["pA"], bufs["pB"]
    last = None
    for ft in range(10):
        ps = pA if ft % 2 == 0 else pB
        mm = None
        for kc in range(8):
            w = (list(waits) + [p.now("sc")]) if kc == 0 else ()
            mm = p.t(lambda E, ft=ft, kc=kc, ps=ps: E.matmul(
                ps[:, 0:512],
                WxF[:, kc * 1280 + ft * 128:kc * 1280 + (ft + 1) * 128],
                src_bf[:, kc * S + tc * 512:kc * S + (tc + 1) * 512],
                start=(kc == 0), stop=(kc == 7)), waits=w)
        last = p.s(lambda E, ft=ft, ps=ps: E.activation(
            out32[:, ft * 512:(ft + 1) * 512], ps[:, 0:512], AF.Identity,
            bias=bFcol[:, ft:ft + 1]), waits=[mm])
    return last


def em_scalar_to_col(p, bufs, scal_ap, col_ap, waits=()):
    pD = bufs["pD"]
    oner = bufs["oner"]
    mm = p.t(lambda E: E.matmul(pD[:, 0:1], oner[:], scal_ap, start=True,
                                stop=True), waits=list(waits) + [p.now("ve")])
    return p.v(lambda E: E.tensor_copy(col_ap, pD[:, 0:1]), waits=[mm])


def em_global_max(p, bufs, max_col, m1, waits=()):
    mrow = bufs["mrow"]
    d1 = p.gdma(mrow[0:1, 0:128], max_col, waits=waits)
    r1 = p.v(lambda E: E.tensor_reduce(m1[0:1, 0:1], mrow[0:1, 0:128], AXX,
                                       ALU.max), waits=[d1])
    return r1


def em_recv_max(p, m8_hbm, m8row, m1col_ap, waits=()):
    ld = p.gdma(m8row[0:1, :], m8_hbm[0:1, :], ch=3, waits=waits)
    return p.v(lambda E: E.tensor_reduce(m1col_ap, m8row[0:1, :], AXX,
                                         ALU.max), waits=[ld])


def common_bufs(nc, es, S, stats=True, big=True):
    sb = lambda n, sh, dt=F32: es.enter_context(nc.sbuf_tensor(n, list(sh), dt))
    ps = lambda n, sh, dt=F32: es.enter_context(nc.psum_tensor(n, list(sh), dt))
    bufs = {}
    if big:
        bufs["a_bc"] = sb("a_bc", [128, S])
        bufs["b_bc"] = sb("b_bc", [128, S])
        bufs["xrow"] = sb("xrow", [128, S])
        bufs["xn32"] = sb("xn32", [128, S])
        bufs["tmprow"] = sb("tmprow", [128, S])
    bufs["xt2"] = sb("xt2", [128, 1024])
    bufs["sq"] = sb("sq", [128, 512])
    if stats:
        bufs["mu_row"] = sb("mu_row", [1, S])
        bufs["sq_row"] = sb("sq_row", [1, S])
        bufs["b_row"] = bufs["mu_row"]
    else:
        bufs["b_row"] = sb("b_row", [1, S])
    bufs["a_row"] = sb("a_row", [1, S])
    bufs["onec"] = sb("onec", [128, 1])
    bufs["oner"] = sb("oner", [1, 128])
    bufs["mcol1"] = sb("mcol1", [128, 1])
    bufs["mrow"] = sb("mrow", [1, 128])
    bufs["eps1"] = sb("eps1", [1, 1])
    bufs["pA"] = ps("pA", [128, 1024])
    bufs["pB"] = ps("pB", [128, 1024])
    bufs["pC"] = ps("pC", [128, 512])
    bufs["pD"] = ps("pD", [128, 512])
    return bufs, sb, ps


def build_A(S, n_cores):
    nc = bass.Bass("TRN2", target_bir_lowering=False)
    TC = S // 512
    x = nc.dram_tensor("x", [D, S], F32, kind="ExternalInput")
    wkT = nc.dram_tensor("wkT", [D, D], F32, kind="ExternalInput")
    wvT = nc.dram_tensor("wvT", [D, D], F32, kind="ExternalInput")
    bkc = nc.dram_tensor("bkc", [128, 8], F32, kind="ExternalInput")
    bvc = nc.dram_tensor("bvc", [128, 8], F32, kind="ExternalInput")
    fpad_h = nc.dram_tensor("fpad", [D, 2 * NF], BF16, kind="ExternalInput")
    o_ab = nc.dram_tensor("o_ab", [2, S], F32, kind="ExternalOutput")
    o_shat = nc.dram_tensor("o_shat", [128, 10], F32, kind="ExternalOutput")
    o_mxn = nc.dram_tensor("o_mxn", [1, 1], F32, kind="ExternalOutput")

    es = ExitStack()
    bufs, sb, ps = common_bufs(nc, es, S, stats=True, big=False)
    xn_bf = sb("xn_bf", [128, 8 * S], BF16)
    bc_bf = sb("bc_bf", [128, 8], BF16)
    bkF = sb("bkF", [128, 10])
    bvF = sb("bvF", [128, 10])
    Sacc = sb("Sacc", [128, 10])
    mx = sb("mx", [128, 1])
    m1 = sb("m1", [1, 1])

    p = Prog(nc, es)
    p.v(lambda E: E.memset(bufs["onec"][:], 1.0))
    p.v(lambda E: E.memset(bufs["oner"][:], 1.0))
    p.v(lambda E: E.memset(Sacc[:], 0.0))
    p.flush()
    with ExitStack() as e1:
        sb1 = lambda n, sh, dt=F32: e1.enter_context(
            nc.sbuf_tensor(n, list(sh), dt))
        bufs["a_bc"] = sb1("a_bc", [128, S])
        bufs["b_bc"] = sb1("b_bc", [128, S])
        bufs["xrow"] = sb1("xrow", [128, S])
        bufs["xrow_b"] = sb1("xrow_b", [128, S])
        bufs["tmprow"] = sb1("tmprow", [128, S])
        bufs["mcol_b"] = sb1("mcol_b", [128, 1])
        em_ln_stats(p, bufs, x, S)
        em_ln_bcast(p, bufs, S, waits=[p.now("ve"), p.now("sc")])
        em_xn(p, bufs, x, S, xn_bf=xn_bf, max_col=mx, waits=[p.now("ve")])
        em_global_max(p, bufs, mx[:, 0:1], m1, waits=[p.now("ve")])
        p.flush()
    e23 = es.enter_context(ExitStack())
    WkF = e23.enter_context(nc.sbuf_tensor("WkF", [128, 8 * 2 * NF], BF16))
    WvF = e23.enter_context(nc.sbuf_tensor("WvF", [128, 8 * 2 * NF], BF16))
    with ExitStack() as e2:
        sb2 = lambda n, sh, dt=F32: e2.enter_context(
            nc.sbuf_tensor(n, list(sh), dt))
        bufs["Fpad"] = sb2("Fpad", [128, 8 * 2 * NF], BF16)
        wT_bf = sb2("wT_bf", [128, 8 * D], BF16)
        lf = p.sdma(bufs["Fpad"][:].rearrange("p (c f) -> p c f", c=8),
                    fpad_h[:].rearrange("(c p) f -> p c f", p=128))
        ldk = p.gdma(wT_bf[:].rearrange("p (c f) -> p c f", c=8),
                     wkT[:].rearrange("(c p) f -> p c f", p=128))
        ldbk = p.gdma(bc_bf[:], bkc[:])
        em_fold(p, bufs, wT_bf, WkF, bc_bf, bkF, waits=[lf, ldk, ldbk])
        ldv = p.gdma(wT_bf[:].rearrange("p (c f) -> p c f", c=8),
                     wvT[:].rearrange("(c p) f -> p c f", p=128),
                     waits=[p.now("te")])
        ldbv = p.gdma(bc_bf[:], bvc[:], waits=[p.now("te")])
        em_fold(p, bufs, wT_bf, WvF, bc_bf, bvF, waits=[ldv, ldbv])
        p.flush()
    with ExitStack() as e3:
        sb3 = lambda n, sh, dt=F32: e3.enter_context(
            nc.sbuf_tensor(n, list(sh), dt))
        Kbuf = sb3("Kbuf", [128, 10 * 512])
        Vbuf = sb3("Vbuf", [128, 10 * 512])
        prod = sb3("prod", [128, 512])
        pc1 = sb3("pc1", [128, 1])
        pc2 = sb3("pc2", [128, 1])
        for tc in range(TC):
            tk = em_rfft(p, bufs, WkF, bkF, xn_bf, tc, S, Kbuf,
                         waits=[p.now("ve")])
            tv = em_rfft(p, bufs, WvF, bvF, xn_bf, tc, S, Vbuf,
                         waits=[p.now("ve")])
            for m in range(5):
                Kr = Kbuf[:, m * 512:(m + 1) * 512]
                Ki = Kbuf[:, (m + 5) * 512:(m + 6) * 512]
                Vr = Vbuf[:, m * 512:(m + 1) * 512]
                Vi = Vbuf[:, (m + 5) * 512:(m + 6) * 512]
                t1 = p.v(lambda E, Kr=Kr, Vr=Vr: E.scalar_tensor_tensor(
                    prod[:], Kr, 1.0, Vr, ALU.mult, ALU.mult,
                    accum_out=pc1[:]), waits=[tk, tv, p.now("ve")])
                t2 = p.v(lambda E, Ki=Ki, Vi=Vi: E.scalar_tensor_tensor(
                    prod[:], Ki, 1.0, Vi, ALU.mult, ALU.mult,
                    accum_out=pc2[:]), waits=[t1])
                t3 = p.v(lambda E: E.tensor_tensor(pc1[:], pc1[:], pc2[:],
                                                   ALU.subtract), waits=[t2])
                t4 = p.v(lambda E, m=m: E.tensor_tensor(
                    Sacc[:, m:m + 1], Sacc[:, m:m + 1], pc1[:], ALU.add),
                    waits=[t3])
                t5 = p.v(lambda E, Kr=Kr, Vi=Vi: E.scalar_tensor_tensor(
                    prod[:], Kr, 1.0, Vi, ALU.mult, ALU.mult,
                    accum_out=pc1[:]), waits=[t4])
                t6 = p.v(lambda E, Ki=Ki, Vr=Vr: E.scalar_tensor_tensor(
                    prod[:], Ki, 1.0, Vr, ALU.mult, ALU.mult,
                    accum_out=pc2[:]), waits=[t5])
                t7 = p.v(lambda E: E.tensor_tensor(pc1[:], pc1[:], pc2[:],
                                                   ALU.add), waits=[t6])
                p.v(lambda E, m=m: E.tensor_tensor(
                    Sacc[:, m + 5:m + 6], Sacc[:, m + 5:m + 6], pc1[:],
                    ALU.add), waits=[t7])
        p.sdma(o_ab[0:1, :], bufs["a_row"][:],
               waits=[p.now("ve"), p.now("sc")], ch=3)
        p.sdma(o_ab[1:2, :], bufs["b_row"][:], waits=[p.now("ve")], ch=3)
        p.sdma(o_shat[:], Sacc[:], waits=[p.now("ve")], ch=3)
        p.gdma(o_mxn[:], m1[:], ch=3, waits=[p.now("ve")])
        p.wait_on("sync", p.alldma())
        p.flush()
    es.close()
    return nc


def em_scalar_cols(p, bufs, m1_ap, invs_col, s_col, waits=()):
    oner = bufs["oner"]
    pD = bufs["pD"]
    s1 = bufs["s1"]
    t1 = p.v(lambda E: E.tensor_scalar(s1[0:1, 0:1], m1_ap, 1.0 / 32767.0,
                                       1e-12, ALU.mult, ALU.max), waits=waits)
    mm = p.t(lambda E: E.matmul(pD[:, 0:1], oner[:], s1[0:1, 0:1], start=True,
                                stop=True), waits=[t1, p.now("ve")])
    t2 = p.v(lambda E: E.tensor_copy(s_col, pD[:, 0:1]), waits=[mm])
    t3 = p.v(lambda E: E.reciprocal(invs_col, s_col), waits=[t2])
    return t3


def em_lpbq_tile(p, bufs, w_hbm, it, I, lhsT_out, ident_bf, cons=(), ch=2,
                 pre="", nb_width=32, chain_tok=()):
    wtile, wq_bf = bufs[pre + "wtile"], bufs[pre + "wq_bf"]
    scl, invscl = bufs[pre + "scl"], bufs[pre + "invscl"]
    tmpw = bufs[pre + "tmpw"]
    pT = bufs["pT"]
    NB = I // 32
    ld = p.sdma(wtile[:, 0:I], w_hbm[it * 128:(it + 1) * 128, :],
                waits=list(cons) + list(chain_tok), ch=ch)
    wv = wtile[:, 0:I].rearrange("p (b i) -> p b i", i=32)
    sv = scl[:, 0:NB]
    t1 = p.v(lambda E: E.tensor_reduce(sv, wv, AXX, ALU.max,
                                       apply_absolute_value=True),
             waits=[ld] + list(chain_tok))
    t2 = p.v(lambda E: E.tensor_scalar(sv, sv, 1.0 / 7.0, 1e-12, ALU.mult,
                                       ALU.max), waits=[t1])
    t3 = p.v(lambda E: E.reciprocal(invscl[:, 0:NB], sv), waits=[t2])
    iv = bass.AP(invscl, 0, [[nb_width, 128], [1, NB], [0, 32]])
    svb = bass.AP(scl, 0, [[nb_width, 128], [1, NB], [0, 32]])
    tv = tmpw[:, 0:I].rearrange("p (b i) -> p b i", i=32)
    t4 = p.v(lambda E: E.tensor_tensor(tv, wv, iv, ALU.mult), waits=[t3])
    t5 = p.v(lambda E: E.tensor_scalar(tmpw[:, 0:I], tmpw[:, 0:I], MAGIC,
                                       -MAGIC, ALU.add, ALU.add), waits=[t4])
    t7 = p.v(lambda E: E.tensor_tensor(
        wq_bf[:, 0:I].rearrange("p (b i) -> p b i", i=32), tv, svb, ALU.mult),
        waits=[t5])
    pT2 = bufs.get("pT2")
    banks = [pT, pT2] if pT2 is not None else [pT]
    free = bufs.setdefault("_pt_free", [()] * len(banks))
    nkc = I // 128
    ev = None
    for kc in range(nkc):
        bk = kc % len(banks)
        tt = p.t(lambda E, kc=kc, bk=bk: E.transpose(
            banks[bk][:, 0:128], wq_bf[:, kc * 128:(kc + 1) * 128],
            ident_bf[:]), waits=[t7] + list(free[bk]))
        ew = [tt] + ([ev] if kc == nkc - 1 and ev is not None else [])
        if kc % 2 == 0:
            ev = p.v(lambda E, kc=kc, bk=bk: E.tensor_copy(
                lhsT_out[:, kc * 128:(kc + 1) * 128], banks[bk][:, 0:128]),
                waits=ew)
        else:
            ev = p.s(lambda E, kc=kc, bk=bk: E.activation(
                lhsT_out[:, kc * 128:(kc + 1) * 128], banks[bk][:, 0:128],
                AF.Identity), waits=ew)
        free[bk] = (ev,)
    return ev, t7


def qdq_chain(p, dst, src, invs_col, s_col, tmp, waits=(), magic_col=None,
              split=True):
    if split and magic_col is not None:
        t1 = p.s(lambda E: E.activation(tmp, src, AF.Identity,
                                        scale=invs_col, bias=magic_col),
                 waits=waits)
    else:
        t1 = p.v(lambda E: E.tensor_scalar(tmp, src, invs_col, MAGIC,
                                           ALU.mult, ALU.add), waits=waits)
    t2 = p.v(lambda E: E.tensor_scalar(dst, tmp, -MAGIC, s_col, ALU.add,
                                       ALU.mult), waits=[t1])
    return t2


def build_ffn_stage(stage, S, n_cores):
    nc = bass.Bass("TRN2", target_bir_lowering=False)
    TC = S // 512
    NC = n_cores
    x = nc.dram_tensor("x", [D, S], F32, kind="ExternalInput")
    ab = nc.dram_tensor("ab", [2, S], F32, kind="ExternalInput")
    mxn = nc.dram_tensor("mxn", [1, NC], F32, kind="ExternalInput")
    wup = nc.dram_tensor("wup", [3072, D], F32, kind="ExternalInput")
    wgate = nc.dram_tensor("wgate", [3072, D], F32, kind="ExternalInput")
    ident = nc.dram_tensor("ident", [128, 128], BF16, kind="ExternalInput")
    if stage in ("C", "D"):
        mgate_i = nc.dram_tensor("mgate", [1, NC], F32, kind="ExternalInput")
    if stage == "D":
        mup_i = nc.dram_tensor("mup", [1, NC], F32, kind="ExternalInput")
        mg2_i = nc.dram_tensor("mg2", [1, NC], F32, kind="ExternalInput")
    nout = {"B": 2, "C": 1, "D": 1}[stage]
    o_max = nc.dram_tensor("o_max", [1, nout], F32, kind="ExternalOutput")

    es = ExitStack()
    bufs, sb, ps = common_bufs(nc, es, S, stats=False, big=False)
    bufs["pT"] = es.enter_context(nc.psum_tensor("pT", [128, 128], BF16))
    bufs["pT2"] = es.enter_context(nc.psum_tensor("pT2", [128, 128], BF16))
    xnq_bf = sb("xnq_bf", [128, 8 * S], BF16)
    bufs["s1"] = sb("s1", [1, 2])
    ident_bf = sb("ident_bf", [128, 128], BF16)
    invs_xn = sb("invs_xn", [128, 1])
    s_xn = sb("s_xn", [128, 1])
    m1a = sb("m1a", [1, 1])
    m8row = sb("m8row", [1, NC])
    mxcol = sb("mxcol", [128, 2])
    magicc = sb("magicc", [128, 1])
    bufs["magicc"] = magicc
    mcol1 = bufs["mcol1"]
    cols = {}
    for nm in ("gate", "up", "g2"):
        cols["invs_" + nm] = sb("invs_" + nm, [128, 1])
        cols["s_" + nm] = sb("s_" + nm, [128, 1])

    p = Prog(nc, es)
    p.v(lambda E: E.memset(bufs["onec"][:], 1.0))
    p.v(lambda E: E.memset(bufs["oner"][:], 1.0))
    p.v(lambda E: E.memset(mxcol[:], 0.0))
    p.v(lambda E: E.memset(magicc[:], MAGIC))
    lid = p.sdma(ident_bf[:], ident[:])
    p.wait_on("tensor", [lid])
    lab = p.sdma(bufs["a_row"][:], ab[0:1, :], ch=1)
    lab2 = p.sdma(bufs["b_row"][:], ab[1:2, :], ch=1)
    p.flush()
    lmx = em_recv_max(p, mxn, m8row, m1a[0:1, 0:1])
    em_scalar_cols(p, bufs, m1a[0:1, 0:1], invs_xn[:], s_xn[:], waits=[lmx])
    for nm, hs in (("gate", "mgate"), ("up", "mup"), ("g2", "mg2")):
        need = (stage in ("C", "D") and nm == "gate") or                (stage == "D" and nm in ("up", "g2"))
        if need:
            hten = mgate_i if hs == "mgate" else (
                mup_i if hs == "mup" else mg2_i)
            lm = em_recv_max(p, hten, m8row, m1a[0:1, 0:1],
                             waits=[p.now("ve")])
            em_scalar_cols(p, bufs, m1a[0:1, 0:1], cols["invs_" + nm][:],
                           cols["s_" + nm][:], waits=[lm])
    with ExitStack() as e1:
        sb1 = lambda n, sh, dt=F32: e1.enter_context(
            nc.sbuf_tensor(n, list(sh), dt))
        bufs["a_bc"] = sb1("a_bc", [128, S])
        bufs["b_bc"] = sb1("b_bc", [128, S])
        bufs["xrow"] = sb1("xrow", [128, S])
        bufs["xrow_b"] = sb1("xrow_b", [128, S])
        bufs["tmprow"] = sb1("tmprow", [128, S])
        bufs["tmprow_b"] = sb1("tmprow_b", [128, S])
        em_ln_bcast(p, bufs, S, waits=[lab, lab2])
        em_xn(p, bufs, x, S, xnq_bf=xnq_bf, q_invs=invs_xn[:, 0:1],
              q_s=s_xn[:, 0:1], waits=[p.now("ve")])
        p.flush()

    e2s = es.enter_context(ExitStack())
    sb2 = lambda n, sh, dt=F32: e2s.enter_context(
        nc.sbuf_tensor(n, list(sh), dt))
    for pre in ("u0_", "u1_", "g0_", "g1_"):
        bufs[pre + "wtile"] = sb2(pre + "wtile", [128, D])
        bufs[pre + "wq_bf"] = sb2(pre + "wq_bf", [128, D], BF16)
        bufs[pre + "scl"] = sb2(pre + "scl", [128, 32])
        bufs[pre + "invscl"] = sb2(pre + "invscl", [128, 32])
        bufs[pre + "tmpw"] = sb2(pre + "tmpw", [128, D])
    lhsT = [sb2("lhsT%d" % i, [128, 8 * 128], BF16) for i in range(4)]
    raw32 = sb2("raw32", [128, 512])
    up32 = sb2("up32", [128, 512])
    c1 = sb2("c1", [128, 512])
    c2 = sb2("c2", [128, 512])
    c3 = sb2("c3", [128, 512])

    raw32b = [raw32, sb2("raw32_b", [128, 512]), sb2("raw32_c", [128, 512])]
    up32b = [up32, sb2("up32_b", [128, 512]), sb2("up32_c", [128, 512])]
    c1b = [c1, sb2("c1_b", [128, 512]), sb2("c1_c", [128, 512])]
    c2b = [c2, sb2("c2_b", [128, 512]), sb2("c2_c", [128, 512])]
    c3b = [c3, sb2("c3_b", [128, 512]), sb2("c3_c", [128, 512])]
    mcol2 = [mcol1, sb2("mcol_b", [128, 1]), sb2("mcol_c", [128, 1])]
    pup3 = [bufs["pA"][:, 0:512], bufs["pA"][:, 512:1024],
            bufs["pC"][:, 0:512]]
    pgt3 = [bufs["pB"][:, 0:512], bufs["pB"][:, 512:1024],
            bufs["pD"][:, 0:512]]
    mg = magicc[:, 0:1]

    names = {"B": ("up", "gate"), "C": ("gate",), "D": ("up", "gate")}[stage]
    whbm = {"up": wup, "gate": wgate}
    ltok = {"up": {0: (), 1: ()}, "gate": {0: (), 1: ()}}
    chain_toks = {"up": {0: (), 1: ()}, "gate": {0: (), 1: ()}}
    lhsT2 = {"up": [lhsT[0], lhsT[1]], "gate": [lhsT[2], lhsT[3]]}
    pt = [dict(), dict(), dict()]
    psum_free = {"up": [(), (), ()], "gate": [(), (), ()]}
    r2_last = {0: (), 1: ()}
    for it in range(24):
        bb = it % 2
        tls = {}
        for nm in names:
            tls[nm], ct = em_lpbq_tile(
                p, bufs, whbm[nm], it, D, lhsT2[nm][bb], ident_bf,
                cons=ltok[nm][bb], ch=2 if nm == "up" else 3,
                pre=("u%d_" % bb) if nm == "up" else ("g%d_" % bb),
                chain_tok=chain_toks[nm][bb])
            chain_toks[nm][bb] = (ct,)
        mm_last = {nm: None for nm in names}
        for j in range(0, TC, 3):
            sts = []
            for tc in range(j, min(j + 3, TC)):
                bq = tc % 3
                st = {"b": bq}
                for nm in names:
                    ps_ = pup3[bq] if nm == "up" else pgt3[bq]
                    mm = None
                    for kc in range(8):
                        w = ([tls[nm]] + list(psum_free[nm][bq])) \
                            if kc == 0 else ()
                        mm = p.t(lambda E, kc=kc, tc=tc, nm=nm, bb=bb,
                                 ps_=ps_: E.matmul(
                            ps_[:, 0:512],
                            lhsT2[nm][bb][:, kc * 128:(kc + 1) * 128],
                            xnq_bf[:, kc * S + tc * 512:
                                   kc * S + (tc + 1) * 512],
                            start=(kc == 0), stop=(kc == 7)), waits=w)
                    st["mm_" + nm] = mm
                    mm_last[nm] = mm
                sts.append(st)
            if stage == "B":
                for sl, nm in enumerate(names):
                    for st in sts:
                        bq = st["b"]
                        ps_ = pup3[bq] if nm == "up" else pgt3[bq]
                        r1 = p.v(lambda E, ps_=ps_, bq=bq: E.tensor_reduce(
                            mcol2[bq][:], ps_[:, 0:512],
                            AXX, ALU.max, apply_absolute_value=True),
                            waits=[st["mm_" + nm]] + list(pt[bq].get("mc", ())))
                        psum_free[nm][bq] = (r1,)
                        r2 = p.v(lambda E, sl=sl, bq=bq: E.tensor_tensor(
                            mxcol[:, sl:sl + 1], mxcol[:, sl:sl + 1],
                            mcol2[bq][:], ALU.max),
                            waits=[r1] + list(r2_last[sl]))
                        r2_last[sl] = (r2,)
                        pt[bq]["mc"] = (r2,)
                continue
            if stage == "D":
                for st in sts:
                    bq = st["b"]
                    st["evu"] = p.s(lambda E, bq=bq: E.activation(
                        up32b[bq][:], pup3[bq][:, 0:512],
                        AF.Identity),
                        waits=[st["mm_up"]] + list(pt[bq].get("up32", ())))
                    psum_free["up"][bq] = (st["evu"],)
            for st in sts:
                bq = st["b"]
                st["evg"] = p.s(lambda E, bq=bq: E.activation(
                    raw32b[bq][:], pgt3[bq][:, 0:512],
                    AF.Identity),
                    waits=[st["mm_gate"]] + list(pt[bq].get("raw32", ())))
                psum_free["gate"][bq] = (st["evg"],)
            for st in sts:
                bq = st["b"]
                st["g1a"] = p.s(lambda E, bq=bq: E.activation(
                    c1b[bq][:], raw32b[bq][:], AF.Identity,
                    scale=cols["invs_gate"][:, 0:1], bias=mg),
                    waits=[st["evg"]] + list(pt[bq].get("c1", ())))
            for st in sts:
                bq = st["b"]
                st["g1b"] = p.v(lambda E, bq=bq: E.tensor_scalar(
                    raw32b[bq][:], c1b[bq][:], -MAGIC,
                    cols["s_gate"][:, 0:1], ALU.add, ALU.mult),
                    waits=[st["g1a"]])
            for st in sts:
                bq = st["b"]
                st["sg"] = p.s(lambda E, bq=bq: E.activation(
                    c2b[bq][:], raw32b[bq][:], AF.Sigmoid),
                    waits=[st["g1b"]] + list(pt[bq].get("c2", ())))
            for st in sts:
                bq = st["b"]
                st["f1a"] = p.s(lambda E, bq=bq: E.activation(
                    c3b[bq][:], c2b[bq][:], AF.Identity, scale=65536.0,
                    bias=mg), waits=[st["sg"]] + list(pt[bq].get("c3", ())))
            for st in sts:
                bq = st["b"]
                st["f1b"] = p.v(lambda E, bq=bq: E.tensor_scalar(
                    c2b[bq][:], c3b[bq][:], -MAGIC, 65535.0, ALU.add,
                    ALU.min), waits=[st["f1a"]])
            for st in sts:
                bq = st["b"]
                st["pm"] = p.v(lambda E, bq=bq: E.scalar_tensor_tensor(
                    raw32b[bq][:], c2b[bq][:], 1.0 / 65536.0, raw32b[bq][:],
                    ALU.mult, ALU.mult), waits=[st["f1b"]])
                pt[bq]["c2"] = (st["pm"],)
            if stage == "C":
                for st in sts:
                    bq = st["b"]
                    pt[bq]["c1"] = (st["g1b"],)
                    pt[bq]["c3"] = (st["f1b"],)
                    r1 = p.v(lambda E, bq=bq: E.tensor_reduce(
                        mcol2[bq][:], raw32b[bq][:], AXX, ALU.max,
                        apply_absolute_value=True),
                        waits=[st["pm"]] + list(pt[bq].get("mc", ())))
                    pt[bq]["raw32"] = (r1,)
                    r2 = p.v(lambda E, bq=bq: E.tensor_tensor(
                        mxcol[:, 0:1], mxcol[:, 0:1], mcol2[bq][:],
                        ALU.max), waits=[r1] + list(r2_last[0]))
                    r2_last[0] = (r2,)
                    pt[bq]["mc"] = (r2,)
                continue
            for st in sts:
                bq = st["b"]
                st["g2a"] = p.s(lambda E, bq=bq: E.activation(
                    c1b[bq][:], raw32b[bq][:], AF.Identity,
                    scale=cols["invs_g2"][:, 0:1], bias=mg),
                    waits=[st["pm"]] + list(pt[bq].get("c1", ())))
            for st in sts:
                bq = st["b"]
                st["g2b"] = p.v(lambda E, bq=bq: E.tensor_scalar(
                    raw32b[bq][:], c1b[bq][:], -MAGIC, cols["s_g2"][:, 0:1],
                    ALU.add, ALU.mult), waits=[st["g2a"]])
                pt[bq]["c1"] = (st["g2b"],)
            for st in sts:
                bq = st["b"]
                st["uqa"] = p.s(lambda E, bq=bq: E.activation(
                    c3b[bq][:], up32b[bq][:], AF.Identity,
                    scale=cols["invs_up"][:, 0:1], bias=mg),
                    waits=[st["evu"], st["f1b"]])
                pt[bq]["up32"] = (st["uqa"],)
            for st in sts:
                bq = st["b"]
                st["uqb"] = p.v(lambda E, bq=bq: E.tensor_scalar(
                    c3b[bq][:], c3b[bq][:], -MAGIC, cols["s_up"][:, 0:1],
                    ALU.add, ALU.mult), waits=[st["uqa"]])
            for st in sts:
                bq = st["b"]
                st["orw"] = p.v(lambda E, bq=bq: E.tensor_tensor(
                    raw32b[bq][:], raw32b[bq][:], c3b[bq][:], ALU.mult),
                    waits=[st["uqb"], st["g2b"]])
                pt[bq]["c3"] = (st["orw"],)
            for st in sts:
                bq = st["b"]
                r1 = p.v(lambda E, bq=bq: E.tensor_reduce(
                    mcol2[bq][:], raw32b[bq][:], AXX, ALU.max,
                    apply_absolute_value=True),
                    waits=[st["orw"]] + list(pt[bq].get("mc", ())))
                pt[bq]["raw32"] = (r1,)
                r2 = p.v(lambda E, bq=bq: E.tensor_tensor(
                    mxcol[:, 0:1], mxcol[:, 0:1], mcol2[bq][:], ALU.max),
                    waits=[r1] + list(r2_last[0]))
                r2_last[0] = (r2,)
                pt[bq]["mc"] = (r2,)
        for nm in names:
            ltok[nm][bb] = (mm_last[nm],)
    mrow = bufs["mrow"]
    res = []
    for sl in range(nout):
        d1 = p.gdma(mrow[0:1, 0:128], mxcol[:, sl:sl + 1],
                    waits=[p.now("ve")], ch=1)
        r1 = p.v(lambda E, sl=sl: E.tensor_reduce(
            bufs["s1"][0:1, sl:sl + 1], mrow[0:1, 0:128], AXX, ALU.max),
            waits=[d1])
        res.append(r1)
    st = p.gdma(o_max[:], bufs["s1"][0:1, 0:nout], waits=res, ch=2)
    p.wait_on("sync", p.alldma())
    p.flush()
    es.close()
    return nc


def fq16_ops_inline(p, dst, src, tmp, waits=(), magic_col=None):
    if magic_col is not None:
        t1 = p.s(lambda E: E.activation(tmp[:], src[:], AF.Identity,
                                        scale=65536.0, bias=magic_col),
                 waits=waits)
    else:
        t1 = p.v(lambda E: E.tensor_scalar(tmp[:], src[:], 65536.0, MAGIC,
                                           ALU.mult, ALU.add), waits=waits)
    t2 = p.v(lambda E: E.tensor_scalar(dst[:], tmp[:], -MAGIC, 65535.0,
                                       ALU.add, ALU.min), waits=[t1])
    return t2




def build_E(S, n_cores):
    nc = bass.Bass("TRN2", target_bir_lowering=False)
    TC = S // 512
    TT = S // 128
    x = nc.dram_tensor("x", [D, S], F32, kind="ExternalInput")
    ab = nc.dram_tensor("ab", [2, S], F32, kind="ExternalInput")
    shat = nc.dram_tensor("shat", [128, 10], F32, kind="ExternalInput")
    NC = n_cores
    m_in = {}
    for nm in ("mxn", "mgate", "mup", "mg2", "mo"):
        m_in[nm] = nc.dram_tensor(nm, [1, NC], F32, kind="ExternalInput")
    wqT = nc.dram_tensor("wqT", [D, D], F32, kind="ExternalInput")
    bqc = nc.dram_tensor("bqc", [128, 8], F32, kind="ExternalInput")
    fpad_h = nc.dram_tensor("fpad", [D, 2 * NF], BF16, kind="ExternalInput")
    cmat_h = nc.dram_tensor("cmat", [NF, 2 * D], BF16, kind="ExternalInput")
    ident = nc.dram_tensor("ident", [128, 128], BF16, kind="ExternalInput")
    wup = nc.dram_tensor("wup", [3072, D], F32, kind="ExternalInput")
    wgate = nc.dram_tensor("wgate", [3072, D], F32, kind="ExternalInput")
    wdown = nc.dram_tensor("wdown", [D, 3072], F32, kind="ExternalInput")
    awq = nc.dram_tensor("awq", [D, 256], F32, kind="ExternalInput")
    awk = nc.dram_tensor("awk", [D, 256], F32, kind="ExternalInput")
    wvdT = nc.dram_tensor("wvdT", [256, D], F32, kind="ExternalInput")
    wvup = nc.dram_tensor("wvup", [256, D], F32, kind="ExternalInput")
    rwqT = nc.dram_tensor("rwqT", [D, D], F32, kind="ExternalInput")
    rwkT = nc.dram_tensor("rwkT", [D, D], F32, kind="ExternalInput")
    rbqc = nc.dram_tensor("rbqc", [128, 8], F32, kind="ExternalInput")
    rbkc = nc.dram_tensor("rbkc", [128, 8], F32, kind="ExternalInput")
    ewqT = nc.dram_tensor("ewqT", [128, 8 * 128], F32, kind="ExternalInput")
    ewkT = nc.dram_tensor("ewkT", [128, 8 * 128], F32, kind="ExternalInput")
    ebqc = nc.dram_tensor("ebqc", [128, 8], F32, kind="ExternalInput")
    ebkc = nc.dram_tensor("ebkc", [128, 8], F32, kind="ExternalInput")
    wexp = nc.dram_tensor("wexp", [8, D, D], F32, kind="ExternalInput")
    bexp = nc.dram_tensor("bexp", [8, D], F32, kind="ExternalInput")
    o_x2 = nc.dram_tensor("o_x2", [D, S], F32, kind="ExternalOutput")
    xn_d = nc.dram_tensor("xn_d", [128, 8 * S], BF16, kind="Internal")
    xnq_d = nc.dram_tensor("xnq_d", [128, 8 * S], BF16, kind="Internal")
    sfm_d = nc.dram_tensor("sfm_d", [D, S], F32, kind="Internal")
    wqlu_d = nc.dram_tensor("wqlu_d", [128, 24 * 1024], BF16, kind="Internal")
    wqlg_d = nc.dram_tensor("wqlg_d", [128, 24 * 1024], BF16, kind="Internal")
    wqld_d = nc.dram_tensor("wqld_d", [128, 8 * 24 * 128], BF16, kind="Internal")
    x1_d = nc.dram_tensor("x1_d", [D, S], F32, kind="Internal")
    xn1_d = nc.dram_tensor("xn1_d", [128, 8 * S], BF16, kind="Internal")
    s1_d = nc.dram_tensor("s1_d", [D, S], F32, kind="Internal")
    m_d = nc.dram_tensor("m_d", [8, S], F32, kind="Internal")

    es = ExitStack()
    sb = lambda n, sh, dt=F32: es.enter_context(nc.sbuf_tensor(n, list(sh), dt))
    ps_ = lambda n, sh, dt=F32: es.enter_context(nc.psum_tensor(n, list(sh), dt))
    bufs = {}
    bufs["pA"] = ps_("pA", [128, 1024])
    bufs["pB"] = ps_("pB", [128, 1024])
    bufs["pCD"] = ps_("pCD", [128, 1024])
    bufs["pC"] = bufs["pCD"][:, 0:512]
    bufs["pD"] = bufs["pCD"][:, 512:1024]
    pre_ps = ExitStack()
    bufs["pT"] = pre_ps.enter_context(nc.psum_tensor("pT", [128, 128], BF16))
    bufs["pT2"] = pre_ps.enter_context(nc.psum_tensor("pT2", [128, 128],
                                                      BF16))
    for nm in ("onec", "mcol1"):
        bufs[nm] = sb(nm, [128, 1])
    bufs["oner"] = sb("oner", [1, 128])
    bufs["eps1"] = sb("eps1", [1, 1])
    bufs["mrow"] = sb("mrow", [1, 128])
    bufs["s1"] = sb("s1", [1, 2])
    bufs["xt2"] = sb("xt2", [128, 1024])
    bufs["sq"] = sb("sq", [128, 512])
    m1a = sb("m1a", [1, 1])
    magicc = sb("magicc", [128, 1])
    bufs["magicc"] = magicc
    ident_bf = sb("ident_bf", [128, 128], BF16)
    cols = {}
    for nm in ("xn", "gate", "up", "g2", "o"):
        cols["invs_" + nm] = sb("invs_" + nm, [128, 1])
        cols["s_" + nm] = sb("s_" + nm, [128, 1])

    p = Prog(nc, es)
    p.v(lambda E: E.memset(bufs["onec"][:], 1.0))
    p.v(lambda E: E.memset(bufs["oner"][:], 1.0))
    p.v(lambda E: E.memset(magicc[:], MAGIC))
    lid = p.sdma(ident_bf[:], ident[:])
    p.wait_on("tensor", [lid])
    m8row = sb("m8row", [1, NC])
    for nm in ("xn", "gate", "up", "g2", "o"):
        lm = em_recv_max(p, m_in[{"xn": "mxn", "gate": "mgate", "up": "mup",
                                  "g2": "mg2", "o": "mo"}[nm]], m8row,
                         m1a[0:1, 0:1], waits=[p.now("ve")])
        em_scalar_cols(p, bufs, m1a[0:1, 0:1], cols["invs_" + nm][:],
                       cols["s_" + nm][:], waits=[lm])
    p.flush()

    with ExitStack() as e1:
        sb1 = lambda n, sh, dt=F32: e1.enter_context(
            nc.sbuf_tensor(n, list(sh), dt))
        bufs["a_row"] = sb1("a_row", [1, S])
        bufs["b_row"] = sb1("b_row", [1, S])
        bufs["a_bc"] = sb1("a_bc", [128, S])
        bufs["b_bc"] = sb1("b_bc", [128, S])
        xr2 = [sb1("xrow%d" % i, [128, S]) for i in range(2)]
        tp2 = [sb1("tmpr%d" % i, [128, S]) for i in range(2)]
        ob2 = [sb1("obf%d" % i, [128, S], BF16) for i in range(2)]
        oq2 = [sb1("obfq%d" % i, [128, S], BF16) for i in range(2)]
        lab = p.sdma(bufs["a_row"][:], ab[0:1, :], ch=1)
        lab2 = p.sdma(bufs["b_row"][:], ab[1:2, :], ch=1)
        em_ln_bcast(p, bufs, S, waits=[lab, lab2])
        bw = (p.now("ve"),)
        rd1 = [dict(), dict()]
        st1 = st2 = None
        for dc in range(8):
            bq = dc % 2
            xr = xr2[bq]
            ld = p.sdma(xr[:], x[dc * 128:(dc + 1) * 128, :],
                        waits=list(rd1[bq].get("xr", ())), ch=1)
            t1 = p.v(lambda E, xr=xr: E.tensor_tensor(
                xr[:], xr[:], bufs["a_bc"][:], ALU.mult),
                waits=[ld] + list(bw))
            t2 = p.v(lambda E, xr=xr: E.tensor_tensor(
                xr[:], xr[:], bufs["b_bc"][:], ALU.add), waits=[t1])
            c1t = p.v(lambda E, xr=xr, bq=bq: E.tensor_copy(
                ob2[bq][:], xr[:]),
                waits=[t2] + list(rd1[bq].get("obf", ())))
            st1 = p.sdma(xn_d[:, dc * S:(dc + 1) * S], ob2[bq][:],
                         waits=[c1t], ch=2)
            rd1[bq]["obf"] = (st1,)
            q3 = qdq_chain(p, oq2[bq][:], xr[:], cols["invs_xn"][:, 0:1],
                           cols["s_xn"][:, 0:1], tp2[bq][:],
                           waits=[t2] + list(rd1[bq].get("oq", ())))
            st2 = p.sdma(xnq_d[:, dc * S:(dc + 1) * S], oq2[bq][:],
                         waits=[q3], ch=3)
            rd1[bq]["oq"] = (st2,)
            rd1[bq]["xr"] = (c1t, q3)
        p.wait_on("vector", [st1, st2])
        p.flush()

    with ExitStack() as e2:
        sb2 = lambda n, sh, dt=F32: e2.enter_context(
            nc.sbuf_tensor(n, list(sh), dt))
        bufs["Fpad"] = sb2("Fpad", [128, 8 * 2 * NF], BF16)
        WqF = sb2("WqF", [128, 8 * 2 * NF], BF16)
        wT_bf = sb2("wT_bf", [128, 8 * D], BF16)
        bc_bf = sb2("bc_bf", [128, 8], BF16)
        bqF = sb2("bqF", [128, 10])
        Cmat = sb2("Cmat", [128, 5 * 2048], BF16)
        Scol = sb2("Scol", [128, 10])
        xnsl = sb2("xnsl", [128, 8 * 512], BF16)
        Q32 = sb2("Q32", [128, 10 * 512])
        Pbuf = sb2("Pbuf", [128, 10 * 512], BF16)
        ctmp = sb2("ctmp", [128, 512])
        sfmt = sb2("sfmt", [128, 512])
        lf = p.sdma(bufs["Fpad"][:].rearrange("p (c f) -> p c f", c=8),
                    fpad_h[:].rearrange("(c p) f -> p c f", p=128))
        lw = p.gdma(wT_bf[:].rearrange("p (c f) -> p c f", c=8),
                    wqT[:].rearrange("(c p) f -> p c f", p=128))
        lb = p.gdma(bc_bf[:], bqc[:])
        lc = p.sdma(Cmat[:].rearrange("p (c f) -> p c f", c=5),
                    cmat_h[:].rearrange("(c p) f -> p c f", p=128), ch=1)
        lsh = p.sdma(Scol[:], shat[:], ch=1)
        em_fold(p, bufs, wT_bf, WqF, bc_bf, bqF, waits=[lf, lw, lb])
        ldx = None
        for tc in range(TC):
            ldx = p.sdma(
                xnsl[:].rearrange("p (c s) -> p c s", c=8),
                xn_d[:].rearrange("p (c s) -> p c s", c=8)[
                    :, :, tc * 512:(tc + 1) * 512],
                waits=[p.now("te")], ch=2)
            em_rfft(p, bufs, WqF, bqF, xnsl, 0, 512, Q32,
                    waits=[ldx, p.now("ve")])
            tq = p.now("sc")
            last = []
            for m in range(5):
                Qr = Q32[:, m * 512:(m + 1) * 512]
                Qi = Q32[:, (m + 5) * 512:(m + 6) * 512]
                Pr = Pbuf[:, m * 512:(m + 1) * 512]
                Pi = Pbuf[:, (m + 5) * 512:(m + 6) * 512]
                u1 = p.v(lambda E, Qr=Qr, m=m: E.tensor_scalar(
                    ctmp[:], Qr, Scol[:, m:m + 1], None, ALU.mult),
                    waits=[tq, lsh, p.now("ve"), p.now("te")])
                u2 = p.v(lambda E, Qi=Qi, Pr=Pr, m=m: E.scalar_tensor_tensor(
                    Pr, Qi, Scol[:, m + 5:m + 6], ctmp[:], ALU.mult, ALU.add),
                    waits=[u1])
                u3 = p.v(lambda E, Qi=Qi, m=m: E.tensor_scalar(
                    ctmp[:], Qi, Scol[:, m:m + 1], -1.0, ALU.mult, ALU.mult),
                    waits=[u2])
                u4 = p.v(lambda E, Qr=Qr, Pi=Pi, m=m: E.scalar_tensor_tensor(
                    Pi, Qr, Scol[:, m + 5:m + 6], ctmp[:], ALU.mult, ALU.add),
                    waits=[u3])
                last.append(u4)
            for dm in range(8):
                mm = None
                for fc in range(5):
                    w = [last[-1]] if fc == 0 else ()
                    mm = p.t(lambda E, dm=dm, fc=fc: E.matmul(
                        bufs["pC"][:, 0:512],
                        Cmat[:, fc * 2048 + dm * 128:fc * 2048 + (dm + 1) * 128],
                        Pbuf[:, fc * 512:(fc + 1) * 512],
                        start=(fc == 0), stop=False), waits=w)
                for fc in range(5):
                    mm = p.t(lambda E, dm=dm, fc=fc: E.matmul(
                        bufs["pC"][:, 0:512],
                        Cmat[:, fc * 2048 + 1024 + dm * 128:
                             fc * 2048 + 1024 + (dm + 1) * 128],
                        Pbuf[:, (fc + 5) * 512:(fc + 6) * 512],
                        start=False, stop=(fc == 4)))
                ev = p.v(lambda E: E.tensor_copy(sfmt[:], bufs["pC"][:, 0:512]),
                         waits=[mm, p.now("sd3")])
                p.wait_on("tensor", [ev])
                p.sdma(sfm_d[dm * 128:(dm + 1) * 128, tc * 512:(tc + 1) * 512],
                       sfmt[:], waits=[ev], ch=3)
        p.flush()

    with ExitStack() as e3:
        sb3 = lambda n, sh, dt=F32: e3.enter_context(
            nc.sbuf_tensor(n, list(sh), dt))
        for pre, I in (("u0_", D), ("u1_", D), ("d0_", 3072), ("d1_", 3072)):
            bufs[pre + "wtile"] = sb3(pre + "wtile", [128, I])
            bufs[pre + "wq_bf"] = sb3(pre + "wq_bf", [128, I], BF16)
            bufs[pre + "scl"] = sb3(pre + "scl", [128, I // 32])
            bufs[pre + "invscl"] = sb3(pre + "invscl", [128, I // 32])
            bufs[pre + "tmpw"] = sb3(pre + "tmpw", [128, I])
        lhsTa = [sb3("lhsTa0", [128, 1024], BF16),
                 sb3("lhsTa1", [128, 1024], BF16)]
        lhsTd = [sb3("lhsTd0", [128, 3072], BF16),
                 sb3("lhsTd1", [128, 3072], BF16)]
        stok = {0: (), 1: ()}
        ctok = {0: (), 1: ()}
        k = 0
        for nm, wh, dst in (("up", wup, wqlu_d), ("gate", wgate, wqlg_d)):
            for it in range(24):
                bb = k % 2
                k += 1
                tl, ct = em_lpbq_tile(p, bufs, wh, it, D, lhsTa[bb], ident_bf,
                                      cons=stok[bb], ch=2 + bb,
                                      pre="u%d_" % bb, chain_tok=ctok[bb])
                ctok[bb] = (ct,)
                st = p.sdma(dst[:, it * 1024:(it + 1) * 1024], lhsTa[bb][:],
                            waits=[tl], ch=bb)
                stok[bb] = (st,)
        stok = {0: (), 1: ()}
        ctok = {0: (), 1: ()}
        for ot in range(8):
            bb = ot % 2
            tl, ct = em_lpbq_tile(p, bufs, wdown, ot, 3072, lhsTd[bb],
                                  ident_bf, cons=stok[bb], ch=2 + bb,
                                  pre="d%d_" % bb, nb_width=96,
                                  chain_tok=ctok[bb])
            ctok[bb] = (ct,)
            st = p.sdma(wqld_d[:, ot * 3072:(ot + 1) * 3072], lhsTd[bb][:],
                        waits=[tl], ch=bb)
            stok[bb] = (st,)
        p.wait_on("vector", p.alldma())
        p.flush()

    NQ = max(1, TC // 2)
    with ExitStack() as e4:
        sb4 = lambda n, sh, dt=F32: e4.enter_context(
            nc.sbuf_tensor(n, list(sh), dt))
        xnqq = sb4("xnqq", [128, 8 * 1024], BF16)
        oqq = sb4("oqq", [128, 24 * 1024], BF16)
        lu = [sb4("lu%d" % i, [128, 1024], BF16) for i in range(2)]
        lg = [sb4("lg%d" % i, [128, 1024], BF16) for i in range(2)]
        ldn = [sb4("ldn%d" % i, [128, 3072], BF16) for i in range(2)]
        up32b = [sb4("up32_%d" % i, [128, 512]) for i in range(2)]
        g32b = [sb4("g32_%d" % i, [128, 512]) for i in range(2)]
        c1b = [sb4("cc1_%d" % i, [128, 512]) for i in range(2)]
        c2b = [sb4("cc2_%d" % i, [128, 512]) for i in range(2)]
        c3b = [sb4("cc3_%d" % i, [128, 512]) for i in range(2)]
        sftb = [sb4("sft_%d" % i, [128, 512]) for i in range(2)]
        xtib = [sb4("xti_%d" % i, [128, 512]) for i in range(2)]
        x1tb = [sb4("x1t_%d" % i, [128, 512]) for i in range(2)]
        mg = magicc[:, 0:1]
        pt = [dict(), dict()]
        psum_free = {"up": [(), ()], "gate": [(), ()]}
        pt2 = [dict(), dict()]
        lx_prev = ()
        dn_prev = ()
        for q in range(NQ):
            nch = min(2, TC - q * 2)
            lx = p.sdma(
                xnqq[:].rearrange("p (c s) -> p c s", c=8)[:, :, 0:nch * 512],
                xnq_d[:].rearrange("p (c s) -> p c s", c=8)[
                    :, :, q * 1024:q * 1024 + nch * 512],
                waits=list(lx_prev), ch=1)
            ltoks = {0: (), 1: ()}
            mm_all = []
            for it in range(24):
                bb = it % 2
                l1 = p.sdma(lu[bb][:], wqlu_d[:, it * 1024:(it + 1) * 1024],
                            waits=ltoks[bb], ch=2)
                l2 = p.sdma(lg[bb][:], wqlg_d[:, it * 1024:(it + 1) * 1024],
                            waits=(), ch=2)
                sts = []
                for tcq in range(nch):
                    bq = tcq % 2
                    st = {"b": bq}
                    for nm, ps_, lt, lhs in (("up", bufs["pA"], l1, lu),
                                             ("gate", bufs["pB"], l2, lg)):
                        mm = None
                        for kc in range(8):
                            w = ([lt, lx] + list(psum_free[nm][bq])) \
                                if kc == 0 else ()
                            mm = p.t(lambda E, kc=kc, tcq=tcq, bb=bb, bq=bq,
                                     ps_=ps_, lhs=lhs: E.matmul(
                                ps_[:, bq * 512:bq * 512 + 512],
                                lhs[bb][:, kc * 128:(kc + 1) * 128],
                                xnqq[:, kc * 1024 + tcq * 512:
                                     kc * 1024 + (tcq + 1) * 512],
                                start=(kc == 0), stop=(kc == 7)), waits=w)
                        st["mm_" + nm] = mm
                        mm_all.append(mm)
                    sts.append(st)
                ltoks[bb] = (sts[-1]["mm_up"], sts[-1]["mm_gate"])
                for st in sts:
                    bq = st["b"]
                    st["evu"] = p.s(lambda E, bq=bq: E.activation(
                        up32b[bq][:], bufs["pA"][:, bq * 512:bq * 512 + 512],
                        AF.Identity),
                        waits=[st["mm_up"]] + list(pt[bq].get("up32", ())))
                    psum_free["up"][bq] = (st["evu"],)
                for st in sts:
                    bq = st["b"]
                    st["evg"] = p.s(lambda E, bq=bq: E.activation(
                        g32b[bq][:], bufs["pB"][:, bq * 512:bq * 512 + 512],
                        AF.Identity),
                        waits=[st["mm_gate"]] + list(pt[bq].get("g32", ())))
                    psum_free["gate"][bq] = (st["evg"],)
                for st in sts:
                    bq = st["b"]
                    st["g1a"] = p.s(lambda E, bq=bq: E.activation(
                        c1b[bq][:], g32b[bq][:], AF.Identity,
                        scale=cols["invs_gate"][:, 0:1], bias=mg),
                        waits=[st["evg"]] + list(pt[bq].get("c1", ())))
                for st in sts:
                    bq = st["b"]
                    st["g1b"] = p.v(lambda E, bq=bq: E.tensor_scalar(
                        g32b[bq][:], c1b[bq][:], -MAGIC,
                        cols["s_gate"][:, 0:1], ALU.add, ALU.mult),
                        waits=[st["g1a"]])
                for st in sts:
                    bq = st["b"]
                    st["sg"] = p.s(lambda E, bq=bq: E.activation(
                        c2b[bq][:], g32b[bq][:], AF.Sigmoid),
                        waits=[st["g1b"]] + list(pt[bq].get("c2", ())))
                for st in sts:
                    bq = st["b"]
                    st["f1a"] = p.s(lambda E, bq=bq: E.activation(
                        c3b[bq][:], c2b[bq][:], AF.Identity, scale=65536.0,
                        bias=mg),
                        waits=[st["sg"]] + list(pt[bq].get("c3", ())))
                for st in sts:
                    bq = st["b"]
                    st["f1b"] = p.v(lambda E, bq=bq: E.tensor_scalar(
                        c2b[bq][:], c3b[bq][:], -MAGIC, 65535.0, ALU.add,
                        ALU.min), waits=[st["f1a"]])
                for st in sts:
                    bq = st["b"]
                    st["pm"] = p.v(lambda E, bq=bq: E.scalar_tensor_tensor(
                        g32b[bq][:], c2b[bq][:], 1.0 / 65536.0, g32b[bq][:],
                        ALU.mult, ALU.mult), waits=[st["f1b"]])
                    pt[bq]["c2"] = (st["pm"],)
                for st in sts:
                    bq = st["b"]
                    st["g2a"] = p.s(lambda E, bq=bq: E.activation(
                        c1b[bq][:], g32b[bq][:], AF.Identity,
                        scale=cols["invs_g2"][:, 0:1], bias=mg),
                        waits=[st["pm"]])
                for st in sts:
                    bq = st["b"]
                    st["g2b"] = p.v(lambda E, bq=bq: E.tensor_scalar(
                        g32b[bq][:], c1b[bq][:], -MAGIC, cols["s_g2"][:, 0:1],
                        ALU.add, ALU.mult), waits=[st["g2a"]])
                for st in sts:
                    bq = st["b"]
                    st["uqa"] = p.s(lambda E, bq=bq: E.activation(
                        c3b[bq][:], up32b[bq][:], AF.Identity,
                        scale=cols["invs_up"][:, 0:1], bias=mg),
                        waits=[st["evu"], st["f1b"]])
                    pt[bq]["up32"] = (st["uqa"],)
                for st in sts:
                    bq = st["b"]
                    st["uqb"] = p.v(lambda E, bq=bq: E.tensor_scalar(
                        c3b[bq][:], c3b[bq][:], -MAGIC, cols["s_up"][:, 0:1],
                        ALU.add, ALU.mult), waits=[st["uqa"]])
                for st in sts:
                    bq = st["b"]
                    st["orw"] = p.v(lambda E, bq=bq: E.tensor_tensor(
                        g32b[bq][:], g32b[bq][:], c3b[bq][:], ALU.mult),
                        waits=[st["uqb"], st["g2b"]])
                    pt[bq]["c3"] = (st["orw"],)
                for st in sts:
                    bq = st["b"]
                    st["oqa"] = p.s(lambda E, bq=bq: E.activation(
                        c1b[bq][:], g32b[bq][:], AF.Identity,
                        scale=cols["invs_o"][:, 0:1], bias=mg),
                        waits=[st["orw"]])
                    pt[bq]["g32"] = (st["oqa"],)
                for st in sts:
                    bq = st["b"]
                    tcq = sts.index(st)
                    st["oqb"] = p.v(lambda E, bq=bq, it=it, tcq=tcq:
                                    E.tensor_scalar(
                        oqq[:, it * 1024 + tcq * 512:
                            it * 1024 + (tcq + 1) * 512],
                        c1b[bq][:], -MAGIC, cols["s_o"][:, 0:1], ALU.add,
                        ALU.mult), waits=[st["oqa"]] +
                        (list(dn_prev) if it == 0 else []))
                    pt[bq]["c1"] = (st["oqb"],)
            lx_prev = tuple(mm_all[-4:])
            stok2 = {0: (), 1: ()}
            oq_done = [st["oqb"] for st in sts]
            for ot in range(8):
                bb = ot % 2
                l3 = p.sdma(ldn[bb][:], wqld_d[:, ot * 3072:(ot + 1) * 3072],
                            waits=stok2[bb], ch=2)
                sts2 = []
                for tcq in range(nch):
                    bq = tcq % 2
                    s2 = {"b": bq}
                    mm = None
                    for ic in range(24):
                        w = ([l3] + oq_done + list(pt2[bq].get("ps", ()))) \
                            if ic == 0 else ()
                        mm = p.t(lambda E, ic=ic, tcq=tcq, bb=bb, bq=bq:
                                 E.matmul(
                            bufs["pA"][:, bq * 512:bq * 512 + 512],
                            ldn[bb][:, ic * 128:(ic + 1) * 128],
                            oqq[:, ic * 1024 + tcq * 512:
                                ic * 1024 + (tcq + 1) * 512],
                            start=(ic == 0), stop=(ic == 23)), waits=w)
                    s2["mm"] = mm
                    tcg = q * 2 + tcq
                    s2["ls"] = p.sdma(
                        sftb[bq][:], sfm_d[ot * 128:(ot + 1) * 128,
                                           tcg * 512:(tcg + 1) * 512],
                        waits=list(pt2[bq].get("sft", ())), ch=1)
                    s2["lxi"] = p.sdma(
                        xtib[bq][:], x[ot * 128:(ot + 1) * 128,
                                       tcg * 512:(tcg + 1) * 512],
                        waits=list(pt2[bq].get("xti", ())), ch=3)
                    s2["tcg"] = tcg
                    sts2.append(s2)
                stok2[bb] = (sts2[-1]["mm"],)
                for s2 in sts2:
                    bq = s2["b"]
                    s2["gv"] = p.s(lambda E, bq=bq: E.activation(
                        g32b[bq][:], bufs["pA"][:, bq * 512:bq * 512 + 512],
                        AF.Silu),
                        waits=[s2["mm"]] + list(pt2[bq].get("g32", ())))
                    pt2[bq]["ps"] = (s2["gv"],)
                for s2 in sts2:
                    bq = s2["b"]
                    s2["v1"] = p.v(lambda E, bq=bq: E.tensor_tensor(
                        x1tb[bq][:], sftb[bq][:], g32b[bq][:], ALU.mult),
                        waits=[s2["gv"], s2["ls"], s2["lxi"]] +
                        list(pt2[bq].get("x1t", ())))
                    pt2[bq]["g32"] = (s2["v1"],)
                    pt2[bq]["sft"] = (s2["v1"],)
                for s2 in sts2:
                    bq = s2["b"]
                    s2["v2"] = p.v(lambda E, bq=bq: E.tensor_tensor(
                        x1tb[bq][:], x1tb[bq][:], xtib[bq][:], ALU.add),
                        waits=[s2["v1"]])
                    pt2[bq]["xti"] = (s2["v2"],)
                for s2 in sts2:
                    bq = s2["b"]
                    so = p.sdma(x1_d[ot * 128:(ot + 1) * 128,
                                     s2["tcg"] * 512:(s2["tcg"] + 1) * 512],
                                x1tb[bq][:], waits=[s2["v2"]], ch=3)
                    pt2[bq]["x1t"] = (so,)
            dn_prev = (sts2[-1]["mm"],)
            psum_free["up"] = [tuple(pt2[bq2].get("ps", ()))
                               for bq2 in range(2)]
        p.wait_on("vector", p.alldma())
        p.flush()

    with ExitStack() as e5:
        sb5 = lambda n, sh, dt=F32: e5.enter_context(
            nc.sbuf_tensor(n, list(sh), dt))
        bufs["mu_row"] = sb5("mu_row", [1, S])
        bufs["sq_row"] = sb5("sq_row", [1, S])
        bufs["a_row"] = sb5("a_row5", [1, S])
        bufs["b_row"] = bufs["mu_row"]
        bufs["a_bc"] = sb5("a_bc2", [128, S])
        bufs["b_bc"] = sb5("b_bc2", [128, S])
        xr2 = [sb5("xrow5%d" % i, [128, S]) for i in range(2)]
        ob2 = [sb5("obf5%d" % i, [128, S], BF16) for i in range(2)]
        em_ln_stats(p, bufs, x1_d, S)
        em_ln_bcast(p, bufs, S, waits=[p.now("ve"), p.now("sc")])
        bw = (p.now("ve"),)
        rd5 = [dict(), dict()]
        st1 = None
        for dc in range(8):
            bq = dc % 2
            xr = xr2[bq]
            ld = p.sdma(xr[:], x1_d[dc * 128:(dc + 1) * 128, :],
                        waits=list(rd5[bq].get("xr", ())), ch=1)
            t1 = p.v(lambda E, xr=xr: E.tensor_tensor(
                xr[:], xr[:], bufs["a_bc"][:], ALU.mult),
                waits=[ld] + list(bw))
            t2 = p.v(lambda E, xr=xr: E.tensor_tensor(
                xr[:], xr[:], bufs["b_bc"][:], ALU.add), waits=[t1])
            c1t = p.v(lambda E, xr=xr, bq=bq: E.tensor_copy(
                ob2[bq][:], xr[:]),
                waits=[t2] + list(rd5[bq].get("obf", ())))
            st1 = p.sdma(xn1_d[:, dc * S:(dc + 1) * S], ob2[bq][:],
                         waits=[c1t], ch=2)
            rd5[bq]["obf"] = (st1,)
            rd5[bq]["xr"] = (c1t,)
        p.wait_on("vector", [st1])
        p.flush()

    with ExitStack() as e8:
        sb8 = lambda n, sh, dt=F32: e8.enter_context(
            nc.sbuf_tensor(n, list(sh), dt))
        Vtok = sb8("Vtok", [128, TT * 1024], BF16)
        Qf = sb8("Qf", [128, 2 * S], BF16)
        Kf = sb8("Kf", [128, 2 * S], BF16)
        with ExitStack() as e8v:
            sb8v = lambda n, sh, dt=F32: e8v.enter_context(
                nc.sbuf_tensor(n, list(sh), dt))
            veff = sb8v("veff", [128, 8 * 1024], BF16)
            with ExitStack() as e8w:
                sb8w = lambda n, sh, dt=F32: e8w.enter_context(
                    nc.sbuf_tensor(n, list(sh), dt))
                wvdT_bf = sb8w("wvdT_bf", [128, 2 * D], BF16)
                wvup_bf = sb8w("wvup_bf", [128, 2 * D], BF16)
                l3_ = p.gdma(wvdT_bf[:].rearrange("p (c f) -> p c f", c=2),
                             wvdT[:].rearrange("(c p) f -> p c f", p=128),
                             ch=1)
                l4_ = p.gdma(wvup_bf[:].rearrange("p (c f) -> p c f", c=2),
                             wvup[:].rearrange("(c p) f -> p c f", p=128),
                             ch=1)
                for dm in range(8):
                    for ocn in range(2):
                        mm = None
                        for rc in range(2):
                            w = [l3_, l4_, p.now("ve")] if rc == 0 else ()
                            mm = p.t(lambda E, dm=dm, ocn=ocn, rc=rc: E.matmul(
                                bufs["pD"][:, 0:512],
                                wvdT_bf[:, rc * D + dm * 128:
                                        rc * D + (dm + 1) * 128],
                                wvup_bf[:, rc * D + ocn * 512:
                                        rc * D + (ocn + 1) * 512],
                                start=(rc == 0), stop=(rc == 1)), waits=w)
                        p.v(lambda E, dm=dm, ocn=ocn: E.tensor_copy(
                            veff[:, dm * 1024 + ocn * 512:
                                 dm * 1024 + (ocn + 1) * 512],
                            bufs["pD"][:, 0:512]), waits=[mm])
                p.flush()
            with ExitStack() as e8x:
                sb8x = lambda n, sh, dt=F32: e8x.enter_context(
                    nc.sbuf_tensor(n, list(sh), dt))
                xn1 = sb8x("xn1", [128, 8 * S], BF16)
                awq_bf = sb8x("awq_bf", [128, 8 * 256], BF16)
                awk_bf = sb8x("awk_bf", [128, 8 * 256], BF16)
                lx1 = p.sdma(xn1[:], xn1_d[:], ch=1)
                l1_ = p.gdma(awq_bf[:].rearrange("p (c f) -> p c f", c=8),
                             awq[:].rearrange("(c p) f -> p c f", p=128))
                l2_ = p.gdma(awk_bf[:].rearrange("p (c f) -> p c f", c=8),
                             awk[:].rearrange("(c p) f -> p c f", p=128))
                for dst, wbf, ltok in ((Qf, awq_bf, l1_), (Kf, awk_bf, l2_)):
                    for ot in range(2):
                        for tc in range(S // 512):
                            mm = None
                            for kc in range(8):
                                w = [ltok, lx1, p.now("ve")] if kc == 0 else ()
                                mm = p.t(lambda E, kc=kc, ot=ot, tc=tc,
                                         wbf=wbf: E.matmul(
                                    bufs["pC"][:, 0:512],
                                    wbf[:, kc * 256 + ot * 128:
                                        kc * 256 + (ot + 1) * 128],
                                    xn1[:, kc * S + tc * 512:
                                        kc * S + (tc + 1) * 512],
                                    start=(kc == 0), stop=(kc == 7)), waits=w)
                            p.v(lambda E, dst=dst, ot=ot, tc=tc: E.tensor_copy(
                                dst[:, ot * S + tc * 512:
                                    ot * S + (tc + 1) * 512],
                                bufs["pC"][:, 0:512]), waits=[mm])
                for tm in range(TT):
                    for ocn in range(2):
                        mm = None
                        for kc in range(8):
                            w = [p.now("ve")] if kc == 0 else ()
                            mm = p.t(lambda E, tm=tm, ocn=ocn, kc=kc: E.matmul(
                                bufs["pC"][:, 0:512],
                                xn1[:, kc * S + tm * 128:kc * S + (tm + 1) * 128],
                                veff[:, kc * 1024 + ocn * 512:
                                     kc * 1024 + (ocn + 1) * 512],
                                start=(kc == 0), stop=(kc == 7)), waits=w)
                        p.v(lambda E, tm=tm, ocn=ocn: E.tensor_copy(
                            Vtok[:, tm * 1024 + ocn * 512:
                                 tm * 1024 + (ocn + 1) * 512],
                            bufs["pC"][:, 0:512]), waits=[mm])
                p.flush()
        logits2 = [sb8("logits_%d" % i, [128, S]) for i in range(2)]
        attnP2 = [sb8("attnP_%d" % i, [128, S], BF16) for i in range(2)]
        attnT2 = [sb8("attnT_%d" % i, [128, S], BF16) for i in range(2)]
        mtmp2 = [sb8("mtmp_%d" % i, [128, 512]) for i in range(2)]
        zb12 = [sb8("zb1_%d" % i, [128, S]) for i in range(2)]
        negmax2 = [sb8("negmax_%d" % i, [128, 1]) for i in range(2)]
        Zc2 = [sb8("Zc_%d" % i, [128, 1]) for i in range(2)]
        rZ2 = [sb8("rZ_%d" % i, [128, 1]) for i in range(2)]
        s1t2 = [sb8("s1t_%d" % i, [128, 128]) for i in range(2)]
        pT2 = [bufs["pT"], bufs["pT2"]]
        avps = [(bufs["pA"], 0), (bufs["pA"], 512), (bufs["pB"], 0),
                (bufs["pB"], 512)]
        pt8 = [dict(), dict()]
        qk_free = {0: (), 1: ()}
        av_free = {0: (), 1: (), 2: (), 3: ()}
        st_free = {0: (), 1: ()}
        ptr_free = {0: (), 1: ()}

        def emit_av(st, bi):
            ntt = st + 1
            ttoks = []
            avlast = list(pt8[bi].get("avlast", ()))
            for tt in range(ntt):
                sp = tt % 2
                w = [pt8[bi]["sc1"]] + list(ptr_free[sp])
                ttk = p.t(lambda E, tt=tt, sp=sp, bi=bi: E.transpose(
                    pT2[sp][:, 0:128], attnP2[bi][:, tt * 128:(tt + 1) * 128],
                    ident_bf[:]), waits=w)
                ev = p.v(lambda E, tt=tt, sp=sp, bi=bi: E.tensor_copy(
                    attnT2[bi][:, tt * 128:(tt + 1) * 128], pT2[sp][:, 0:128]),
                    waits=[ttk] + (avlast if tt == 0 else []))
                ptr_free[sp] = (ev,)
                ttoks.append(ev)
            pt8[bi]["attnP"] = (ttk,)
            mm = None
            for dm in range(8):
                ap = dm % 4
                ps2, off = avps[ap]
                mm = None
                for tt in range(ntt):
                    w = []
                    if tt == 0:
                        w += list(av_free[ap % 2 if False else ap])
                    if dm == 0:
                        w.append(ttoks[tt])
                    mm = p.t(lambda E, dm=dm, tt=tt, ps2=ps2, off=off,
                             ntt=ntt, bi=bi, st=st: E.matmul(
                        ps2[:, off:off + 128],
                        Vtok[:, tt * 1024 + dm * 128:tt * 1024 + (dm + 1) * 128],
                        attnT2[bi][:, tt * 128:(tt + 1) * 128],
                        start=(tt == 0), stop=(tt == ntt - 1)), waits=w)
                sp2 = dm % 2
                ev2 = p.v(lambda E, ps2=ps2, off=off, sp2=sp2: E.tensor_copy(
                    s1t2[sp2][:], ps2[:, off:off + 128]),
                    waits=[mm] + list(st_free[sp2]))
                av_free[ap] = (ev2,)
                so = p.sdma(s1_d[dm * 128:(dm + 1) * 128,
                                 st * 128:(st + 1) * 128],
                            s1t2[sp2][:], waits=[ev2], ch=3)
                st_free[sp2] = (so,)
            pt8[bi]["avlast"] = (mm,)

        pending = None
        for st in range(TT):
            bi = st % 2
            L = st * 128 + 128
            ncc = (L + 511) // 512
            evl = None
            for c in range(ncc):
                cp = c % 2
                ps2 = bufs["pC"] if cp == 0 else bufs["pD"]
                mm = None
                for ot in range(2):
                    w = list(qk_free[cp]) if ot == 0 else ()
                    mm = p.t(lambda E, st=st, c=c, ot=ot, ps2=ps2: E.matmul(
                        ps2[:, 0:512],
                        Qf[:, ot * S + st * 128:ot * S + (st + 1) * 128],
                        Kf[:, ot * S + c * 512:ot * S + (c + 1) * 512],
                        start=(ot == 0), stop=(ot == 1)), waits=w)
                evl = p.s(lambda E, c=c, ps2=ps2, bi=bi: E.activation(
                    logits2[bi][:, c * 512:(c + 1) * 512], ps2[:, 0:512],
                    AF.Identity, scale=0.0625),
                    waits=[mm] + (list(pt8[bi].get("log", ()))
                                  if c == 0 else []))
                qk_free[cp] = (evl,)
            cd = ncc - 1
            a1 = p.g(lambda E, st=st, cd=cd, bi=bi: E.affine_select(
                mtmp2[bi][:], logits2[bi][:, cd * 512:(cd + 1) * 512],
                [[-1, 512]], ALU.is_ge, NEG_FILL, base=st * 128 - cd * 512,
                channel_multiplier=1),
                waits=[evl] + list(pt8[bi].get("mtmp", ())))
            a2 = p.v(lambda E, cd=cd, bi=bi: E.tensor_copy(
                logits2[bi][:, cd * 512:(cd + 1) * 512], mtmp2[bi][:]),
                waits=[a1])
            pt8[bi]["mtmp"] = (a2,)
            Lr = cd * 512 + 512
            z1 = p.v(lambda E, Lr=Lr, bi=bi: E.tensor_scalar(
                zb12[bi][:, 0:Lr], logits2[bi][:, 0:Lr], 0.0, None,
                ALU.is_equal),
                waits=[a2, evl] + list(pt8[bi].get("zb1", ())))
            z2 = p.v(lambda E, Lr=Lr, bi=bi: E.scalar_tensor_tensor(
                logits2[bi][:, 0:Lr], zb12[bi][:, 0:Lr], NEG_FILL,
                logits2[bi][:, 0:Lr], ALU.mult, ALU.add), waits=[z1])
            pt8[bi]["zb1"] = (z2,)
            r1 = p.v(lambda E, Lr=Lr, bi=bi: E.tensor_reduce(
                negmax2[bi][:], logits2[bi][:, 0:Lr], AXX, ALU.max,
                negate=True), waits=[z2])
            ex = p.s(lambda E, Lr=Lr, bi=bi: E.activation(
                attnP2[bi][:, 0:Lr], logits2[bi][:, 0:Lr], AF.Exp,
                bias=negmax2[bi][:, 0:1], accum_out=Zc2[bi][:]),
                waits=[r1] + list(pt8[bi].get("attnP", ())))
            pt8[bi]["log"] = (ex,)
            rc1 = p.v(lambda E, bi=bi: E.reciprocal(rZ2[bi][:], Zc2[bi][:]),
                      waits=[ex])
            sc1 = p.v(lambda E, Lr=Lr, bi=bi: E.tensor_scalar(
                attnP2[bi][:, 0:Lr], attnP2[bi][:, 0:Lr], rZ2[bi][:, 0:1],
                None, ALU.mult), waits=[rc1])
            pt8[bi]["sc1"] = sc1
            if pending is not None:
                emit_av(*pending)
            pending = (st, bi)
        emit_av(*pending)
        p.wait_on("vector", p.alldma())
        p.flush()

    pre_ps.close()
    SCL = float(1.0 / np.sqrt(128.0))
    with ExitStack() as e6:
        sb6 = lambda n, sh, dt=F32: e6.enter_context(
            nc.sbuf_tensor(n, list(sh), dt))
        bufs["pSc"] = e6.enter_context(nc.psum_tensor("pSc", [8, 512], F32))
        pE = e6.enter_context(nc.psum_tensor("pE", [128, 512], F32))
        xn1 = sb6("xn1r", [128, 8 * S], BF16)
        rwq_bf = sb6("rwq_bf", [128, 8 * D], BF16)
        rwk_bf = sb6("rwk_bf", [128, 8 * D], BF16)
        rbq_sb = sb6("rbq_sb", [128, 8])
        rbk_sb = sb6("rbk_sb", [128, 8])
        ewq_bf = sb6("ewq_bf", [128, 8 * 128], BF16)
        ewk_bf = sb6("ewk_bf", [128, 8 * 128], BF16)
        ebqs = sb6("ebqs", [128, 8])
        ebk_sb = sb6("ebk_sb", [128, 8])
        qh = sb6("qh", [128, S], BF16)
        kh = sb6("kh", [128, S], BF16)
        qe = sb6("qe", [128, S], BF16)
        ke = sb6("ke", [128, S], BF16)
        expL = sb6("expL", [128, S], BF16)
        Zcol = sb6("Zcol", [128, 1])
        wcol = sb6("wcol", [128, 1])
        ones8 = sb6("ones8", [128, 8])
        scores_sb = sb6("scores_sb", [8, 8 * 512])
        m8 = scores_sb
        mx8 = sb6("mx8", [8, 512])
        tak8 = sb6("tak8", [8, 512])
        eq8 = sb6("eq8", [8, 512])
        ngr = max(1, TC // 2)
        gw = min(1024, S)
        lx1 = p.sdma(xn1[:], xn1_d[:], ch=1)
        lq = p.gdma(rwq_bf[:].rearrange("p (c f) -> p c f", c=8),
                    rwqT[:].rearrange("(c p) f -> p c f", p=128))
        lk = p.gdma(rwk_bf[:].rearrange("p (c f) -> p c f", c=8),
                    rwkT[:].rearrange("(c p) f -> p c f", p=128))
        leq = p.gdma(ewq_bf[:], ewqT[:], ch=1)
        lek = p.gdma(ewk_bf[:], ewkT[:], ch=1)
        lb1 = p.sdma(rbq_sb[:], rbqc[:], ch=2)
        lb2 = p.sdma(rbk_sb[:], rbkc[:], ch=2)
        lb3 = p.sdma(ebqs[:], ebqc[:], ch=2)
        lb4 = p.sdma(ebk_sb[:], ebkc[:], ch=2)
        w0 = p.v(lambda E: E.memset(ones8[:], 1.0))

        w2 = p.v(lambda E: E.memset(scores_sb[:], 0.0))

        expL2 = [expL, sb6("expL_b", [128, S], BF16),
                 sb6("expL_c", [128, S], BF16)]
        zpart2 = [sb6("zpart_%d" % i, [128, 4]) for i in range(2)]
        Zcol2 = [Zcol, sb6("Zcol_b", [128, 1])]
        wcol2 = [wcol, sb6("wcol_b", [128, 1])]
        wledAB = sb6("wledAB", [128, 128], BF16)
        wled2 = [wledAB[:, 0:64], wledAB[:, 64:128]]
        w1b = p.v(lambda E: E.memset(wledAB[:], 0.0))
        pairs3 = [bufs["pA"], bufs["pB"], bufs["pCD"]]
        bank_tok = {0: (), 1: (), 2: ()}
        bcnt = 0
        zp_tok = {0: (), 1: ()}
        mv_tok3 = {0: (), 1: (), 2: ()}
        mv_tokw = {0: (), 1: ()}
        prev_ea = ()
        pendq = []
        gcnt = 0

        def emit_matvecs(pend_info, prev_ea):
            ei, bi, wl_tok, e_, first, laststl, h_ = pend_info
            tok = None
            for c in range(TC):
                w = ([wl_tok] + list(prev_ea)) if c == 0 else ()
                tok = p.t(lambda E, c=c, ei=ei, bi=bi, first=first,
                          laststl=laststl: E.matmul(
                    bufs["pSc"][0:8, 0:512],
                    wled2[bi][:, c * 8:(c + 1) * 8],
                    expL2[ei][:, c * 512:(c + 1) * 512],
                    start=(first and c == 0),
                    stop=(laststl and c == TC - 1)), waits=w)
            mv_tok3[ei] = (tok,)
            mv_tokw[bi] = (tok,)
            if laststl:
                if h_ == 0:
                    ea = p.v(lambda E, e_=e_: E.tensor_copy(
                        scores_sb[0:8, e_ * 512:(e_ + 1) * 512],
                        bufs["pSc"][0:8, 0:512]), waits=[tok])
                else:
                    ea = p.v(lambda E, e_=e_: E.tensor_tensor(
                        scores_sb[0:8, e_ * 512:(e_ + 1) * 512],
                        scores_sb[0:8, e_ * 512:(e_ + 1) * 512],
                        bufs["pSc"][0:8, 0:512], ALU.add), waits=[tok])
                return (ea,)
            return prev_ea

        pc_tok = ()
        pd_tok = ()
        qh2 = [qh, qh]
        kh2 = [kh, kh]
        qe2 = [qe, sb6("qe_b", [128, S], BF16)]
        ke2 = [ke, sb6("ke_b", [128, S], BF16)]
        state = {"pe": ()}
        qh_done = {}
        qe_ready = {}
        qe_lastread = [(), ()]
        qh_lastread = [(), ()]

        def qhkh_steps(h):
            hb = h % 2
            out = []
            for tc in range(TC):
                for key in ("q", "k"):
                    def step(key=key, tc=tc, h=h, hb=hb):
                        dst = qh2[hb] if key == "q" else kh2[hb]
                        wbf = rwq_bf if key == "q" else rwk_bf
                        bias_sb = rbq_sb if key == "q" else rbk_sb
                        ltok = lq if key == "q" else lk
                        mm = None
                        for kc in range(8):
                            w = ([ltok, lx1, lb1, lb2] + list(state["pe"]) +
                                 list(qh_lastread[hb])) if kc == 0 else ()
                            mm = p.t(lambda E, kc=kc, tc=tc, h=h, wbf=wbf:
                                     E.matmul(
                                pE[:, 0:512],
                                wbf[:, kc * D + h * 128:
                                    kc * D + (h + 1) * 128],
                                xn1[:, kc * S + tc * 512:
                                    kc * S + (tc + 1) * 512],
                                start=(kc == 0), stop=(kc == 7)), waits=w)
                        ev = p.v(lambda E, dst=dst, tc=tc, h=h,
                                 bias_sb=bias_sb: E.tensor_scalar(
                            dst[:, tc * 512:(tc + 1) * 512], pE[:, 0:512],
                            bias_sb[:, h:h + 1], None, ALU.add), waits=[mm])
                        state["pe"] = (ev,)
                        cur = qh_done.get(h, ())
                        qh_done[h] = (cur + (ev,))[-2:]
                    out.append(step)
            return out

        def qeke_steps(h, e):
            hb = h % 2
            be = e % 2
            out = []
            for tc in range(TC):
                def stepQ(tc=tc, h=h, e=e, hb=hb, be=be):
                    qw = list(qh_done.get(h, ()))
                    mq = p.t(lambda E, e=e, tc=tc, hb=hb: E.matmul(
                        pE[:, 0:512],
                        ewq_bf[:, e * 128:(e + 1) * 128],
                        qh2[hb][:, tc * 512:(tc + 1) * 512],
                        start=True, stop=True),
                        waits=[leq] + qw + list(state["pe"]) +
                        list(qe_lastread[be]))
                    evq = p.v(lambda E, e=e, tc=tc, be=be: E.tensor_scalar(
                        qe2[be][:, tc * 512:(tc + 1) * 512],
                        pE[:, 0:512],
                        ebqs[:, e:e + 1], SCL, ALU.add, ALU.mult),
                        waits=[mq, lb3])
                    state["pe"] = (evq,)
                    qh_lastread[hb] = qh_lastread[hb] + (mq,)
                    qe_ready[(h, e)] = (evq,)

                def stepK(tc=tc, h=h, e=e, hb=hb, be=be):
                    qw = list(qh_done.get(h, ()))
                    mk = p.t(lambda E, e=e, tc=tc, hb=hb: E.matmul(
                        pE[:, 0:512],
                        ewk_bf[:, e * 128:(e + 1) * 128],
                        kh2[hb][:, tc * 512:(tc + 1) * 512],
                        start=True, stop=True),
                        waits=[lek] + qw + list(state["pe"]) +
                        list(qe_lastread[be]))
                    evk = p.v(lambda E, e=e, tc=tc, be=be: E.tensor_scalar(
                        ke2[be][:, tc * 512:(tc + 1) * 512],
                        pE[:, 0:512],
                        ebk_sb[:, e:e + 1], None, ALU.add), waits=[mk, lb4])
                    state["pe"] = (evk,)
                    qh_lastread[hb] = (mk,)
                    qe_ready[(h, e)] = qe_ready.get((h, e), ()) + (evk,)
                out.append(stepQ)
                out.append(stepK)
            return out

        bg = []
        for f in qhkh_steps(0):
            f()
        for f in qeke_steps(0, 0):
            f()
        for h in range(8):
            for e in range(8):
                be = e % 2
                if e < 7:
                    bg += qeke_steps(h, e + 1)
                if e == 6 and h < 7:
                    bg += qhkh_steps(h + 1)
                if e == 7 and h < 7:
                    bg += qeke_steps(h + 1, 0)
                lg_last = None
                for stl in range(TT):
                    bi = gcnt % 2
                    ei = gcnt % 3
                    gcnt += 1
                    exp_toks = []
                    for g in range(ngr):
                        gb = bcnt % 3
                        bcnt += 1
                        psg = pairs3[gb]
                        nsub = min(2, TC - g * 2)
                        mm = None
                        for sub in range(nsub):
                            w = list(bank_tok[gb]) if sub == 0 else []
                            if g == 0 and sub == 0 and stl == 0:
                                w = w + list(qe_ready[(h, e)])
                            mm = p.t(lambda E, stl=stl, g=g, sub=sub, psg=psg,
                                     be=be: E.matmul(
                                psg[:, sub * 512:(sub + 1) * 512],
                                qe2[be][:, stl * 128:(stl + 1) * 128],
                                ke2[be][:, (g * 2 + sub) * 512:
                                   (g * 2 + sub + 1) * 512],
                                start=True, stop=True), waits=w)
                        lg_last = mm
                        ew = [mm]
                        if g == 0:
                            ew += list(zp_tok[bi]) + list(mv_tok3[ei])
                        if g == 0:
                            ew += list(zp_tok[bi])
                        et = p.s(lambda E, g=g, psg=psg, nsub=nsub, ei=ei,
                                 bi=bi: E.activation(
                            expL2[ei][:, g * 1024:g * 1024 + nsub * 512],
                            psg[:, 0:nsub * 512], AF.Exp,
                            accum_out=zpart2[bi][:, g:g + 1]), waits=ew)
                        bank_tok[gb] = (et,)
                        exp_toks.append(et)
                        if g == min(1, ngr - 1) and len(pendq) >= 2:
                            prev_ea = emit_matvecs(pendq.pop(0), prev_ea)
                    zz = p.v(lambda E, bi=bi: E.tensor_reduce(
                        Zcol2[bi][:], zpart2[bi][:, 0:ngr], AXX, ALU.add),
                        waits=exp_toks)
                    zp_tok[bi] = (zz,)
                    zr = p.v(lambda E, bi=bi: E.reciprocal(
                        wcol2[bi][:], Zcol2[bi][:]), waits=[zz])
                    wl = p.v(lambda E, bi=bi: E.tensor_scalar(
                        bass.AP(wledAB, bi * 64, [[128, 128], [9, TC]]),
                        ones8[:, 0:TC], wcol2[bi][:, 0:1], None, ALU.mult),
                        waits=[zr, w0, w1b] + list(mv_tokw[bi]))
                    pendq.append((ei, bi, wl, e, stl == 0, stl == TT - 1, h))
                    if bg:
                        bg.pop(0)()
                qe_lastread[be] = (lg_last,)
        while pendq:
            prev_ea = emit_matvecs(pendq.pop(0), prev_ea)
        tvv = p.now("ve")
        sl = lambda t, e: t[0:TC, e * 512:(e + 1) * 512]
        mx1 = p.v(lambda E: E.tensor_copy(mx8[0:TC, :], sl(scores_sb, 0)),
                  waits=[tvv])
        for e in range(1, 8):
            mx1 = p.v(lambda E, e=e: E.tensor_tensor(
                mx8[0:TC, :], mx8[0:TC, :], sl(scores_sb, e), ALU.max),
                waits=[mx1])
        prev = p.v(lambda E: E.memset(tak8[:], 0.0), waits=[mx1])
        for e in range(8):
            q1 = p.v(lambda E, e=e: E.tensor_tensor(
                eq8[0:TC, :], sl(scores_sb, e), mx8[0:TC, :], ALU.is_equal),
                waits=[prev])
            q4 = p.v(lambda E, e=e: E.tensor_scalar(
                sl(m8, e), tak8[0:TC, :], -1.0, 1.0, ALU.mult, ALU.add),
                waits=[q1])
            q5 = p.v(lambda E, e=e: E.tensor_tensor(
                sl(m8, e), sl(m8, e), eq8[0:TC, :], ALU.mult), waits=[q4])
            prev = p.v(lambda E, e=e: E.tensor_tensor(
                tak8[0:TC, :], tak8[0:TC, :], sl(m8, e), ALU.add), waits=[q5])
        de = []
        for e in range(8):
            de.append(p.gdma(
                m_d[e:e + 1, :].rearrange("o (c s) -> (o c) s", c=TC),
                m8[0:TC, e * 512:(e + 1) * 512], waits=[prev], ch=2))
        p.wait_on("vector", [de[-1]])
        p.flush()

    with ExitStack() as e9:
        sb9 = lambda n, sh, dt=F32: e9.enter_context(
            nc.sbuf_tensor(n, list(sh), dt))
        m_fm = sb9("m_fm", [8, S], BF16)
        lmf = p.gdma(m_fm[:], m_d[:], ch=3)
        wexp_bf = [sb9("wexp%d" % e, [128, 8 * 512], BF16) for e in range(8)]
        bexp_bf = sb9("bexp_bf", [8, D], BF16)
        xn1sl = [sb9("xn1sl%d" % i, [128, 8 * 512], BF16) for i in range(2)]
        mrow_t = [sb9("mrow_t%d" % i, [1, 8 * 512]) for i in range(2)]
        maskb2 = [sb9("maskb%d" % i, [128, 512], BF16) for i in range(2)]
        xnm2 = [sb9("xnm%d" % i, [128, 8 * 512], BF16) for i in range(2)]
        g32b2 = [sb9("g32b%d" % i, [128, 512]) for i in range(2)]
        s1tt2 = [sb9("s1tt%d" % i, [128, 512]) for i in range(2)]
        x1tt2 = [sb9("x1tt%d" % i, [128, 512]) for i in range(2)]
        x2t2 = [sb9("x2t%d" % i, [128, 512]) for i in range(2)]
        lbx = p.gdma(bexp_bf[:], bexp[:], ch=2)
        pd_free = ()
        mask_rd = [(), ()]
        xnm_rd = [(), ()]
        xn_rd = [(), ()]
        mr_rd = [(), ()]
        ps_free = [()] * 4
        cmb = [dict(), dict()]
        lwe_rd = ()
        for grp in range(2):
            lwe = []
            for e in range(8):
                lwe.append(p.gdma(
                    wexp_bf[e][:].rearrange("p (c f) -> p c f", c=8),
                    wexp[e][:, grp * 512:(grp + 1) * 512].rearrange(
                        "(c p) f -> p c f", p=128),
                    waits=list(lwe_rd) if e == 0 else ()))
            for tc in range(TC):
                bb = tc % 2
                lxn = p.sdma(
                    xn1sl[bb][:].rearrange("p (c s) -> p c s", c=8),
                    xn1_d[:].rearrange("p (c s) -> p c s", c=8)[
                        :, :, tc * 512:(tc + 1) * 512],
                    waits=list(xn_rd[bb]), ch=1)
                lmr = p.gdma(
                    mrow_t[bb][0:1, :].rearrange("o (e s) -> o e s", e=8),
                    bass.AP(m_d, tc * 512,
                            [[S, 1], [S, 8], [1, 512]]),
                    waits=list(mr_rd[bb]), ch=2)
                mm_last = None
                for e in range(8):
                    eb = e % 2
                    bc = p.t(lambda E, e=e, tc=tc, bb=bb: E.matmul(
                        bufs["pD"][:, 0:512], bufs["oner"][:],
                        mrow_t[bb][0:1, e * 512:(e + 1) * 512],
                        start=True, stop=True),
                        waits=[lmr] + list(pd_free))
                    bv = p.v(lambda E, eb=eb: E.tensor_copy(
                        maskb2[eb][:], bufs["pD"][:, 0:512]),
                        waits=[bc] + list(mask_rd[eb]))
                    pd_free = (bv,)
                    xms = []
                    for kc in range(8):
                        xms.append(p.v(lambda E, kc=kc, bb=bb, eb=eb:
                                       E.tensor_tensor(
                            xnm2[eb][:, kc * 512:(kc + 1) * 512],
                            xn1sl[bb][:, kc * 512:(kc + 1) * 512],
                            maskb2[eb][:], ALU.mult),
                            waits=[bv, lxn] +
                            (list(xnm_rd[eb]) if kc == 0 else [])))
                    mask_rd[eb] = (xms[-1],)
                    for ot in range(4):
                        psg = bufs["pA"] if ot < 2 else bufs["pB"]
                        off = (ot % 2) * 512
                        for kc in range(8):
                            w = [xms[-1], lwe[e]] +                                 (list(ps_free[ot]) if e == 0 else [])                                 if kc == 0 else ()
                            mm_last = p.t(lambda E, e=e, ot=ot, kc=kc,
                                          psg=psg, off=off, eb=eb: E.matmul(
                                psg[:, off:off + 512],
                                wexp_bf[e][:, kc * 512 + ot * 128:
                                            kc * 512 + (ot + 1) * 128],
                                xnm2[eb][:, kc * 512:(kc + 1) * 512],
                                start=(e == 0 and kc == 0), stop=False),
                                waits=w)
                    xnm_rd[eb] = (mm_last,)
                xn_rd[bb] = (mm_last,)
                mr_rd[bb] = (mm_last,)
                for ot in range(4):
                    ob = ot % 2
                    psg = bufs["pA"] if ot < 2 else bufs["pB"]
                    off = (ot % 2) * 512
                    mb = p.t(lambda E, ot=ot, psg=psg, off=off, grp=grp,
                             tc=tc: E.matmul(
                        psg[:, off:off + 512],
                        bexp_bf[0:8, grp * 512 + ot * 128:
                                grp * 512 + (ot + 1) * 128],
                        m_fm[0:8, tc * 512:(tc + 1) * 512],
                        start=False, stop=True), waits=[lbx, lmf])
                    gv = p.s(lambda E, psg=psg, off=off, ob=ob: E.activation(
                        g32b2[ob][:], psg[:, off:off + 512], AF.Silu),
                        waits=[mb] + list(cmb[ob].get("g", ())))
                    ps_free[ot] = (gv,)
                    rr = grp * 512 + ot * 128
                    ls = p.sdma(s1tt2[ob][:],
                                s1_d[rr:rr + 128, tc * 512:(tc + 1) * 512],
                                waits=list(cmb[ob].get("s", ())), ch=2)
                    lx = p.sdma(x1tt2[ob][:],
                                x1_d[rr:rr + 128, tc * 512:(tc + 1) * 512],
                                waits=list(cmb[ob].get("x", ())), ch=3)
                    v1 = p.v(lambda E, ob=ob: E.tensor_tensor(
                        x2t2[ob][:], s1tt2[ob][:], g32b2[ob][:], ALU.mult),
                        waits=[gv, ls, lx] + list(cmb[ob].get("o", ())))
                    cmb[ob]["g"] = (v1,)
                    cmb[ob]["s"] = (v1,)
                    v2 = p.v(lambda E, ob=ob: E.tensor_tensor(
                        x2t2[ob][:], x2t2[ob][:], x1tt2[ob][:], ALU.add),
                        waits=[v1])
                    cmb[ob]["x"] = (v2,)
                    so = p.sdma(o_x2[rr:rr + 128, tc * 512:(tc + 1) * 512],
                                x2t2[ob][:], waits=[v2], ch=0)
                    cmb[ob]["o"] = (so,)
            lwe_rd = (mm_last,)
        p.wait_on("sync", p.alldma())
        p.flush()
    es.close()
    return nc


def em_scalar_cols(p, bufs, m1_ap, invs_col, s_col, waits=()):
    oner = bufs["oner"]
    pD = bufs["pD"]
    s1 = bufs["s1"]
    t1 = p.v(lambda E: E.tensor_scalar(s1[0:1, 0:1], m1_ap, 1.0 / 32767.0,
                                       1e-12, ALU.mult, ALU.max), waits=waits)
    mm = p.t(lambda E: E.matmul(pD[:, 0:1], oner[:], s1[0:1, 0:1], start=True,
                                stop=True), waits=[t1, p.now("ve")])
    t2 = p.v(lambda E: E.tensor_copy(s_col, pD[:, 0:1]), waits=[mm])
    t3 = p.v(lambda E: E.reciprocal(invs_col, s_col), waits=[t2])
    return t3


def em_lpbq_tile(p, bufs, w_hbm, it, I, lhsT_out, ident_bf, cons=(), ch=2,
                 pre="", nb_width=32, chain_tok=()):
    wtile, wq_bf = bufs[pre + "wtile"], bufs[pre + "wq_bf"]
    scl, invscl = bufs[pre + "scl"], bufs[pre + "invscl"]
    tmpw = bufs[pre + "tmpw"]
    pT = bufs["pT"]
    NB = I // 32
    ld = p.sdma(wtile[:, 0:I], w_hbm[it * 128:(it + 1) * 128, :],
                waits=list(cons) + list(chain_tok), ch=ch)
    wv = wtile[:, 0:I].rearrange("p (b i) -> p b i", i=32)
    sv = scl[:, 0:NB]
    t1 = p.v(lambda E: E.tensor_reduce(sv, wv, AXX, ALU.max,
                                       apply_absolute_value=True),
             waits=[ld] + list(chain_tok))
    t2 = p.v(lambda E: E.tensor_scalar(sv, sv, 1.0 / 7.0, 1e-12, ALU.mult,
                                       ALU.max), waits=[t1])
    t3 = p.v(lambda E: E.reciprocal(invscl[:, 0:NB], sv), waits=[t2])
    iv = bass.AP(invscl, 0, [[nb_width, 128], [1, NB], [0, 32]])
    svb = bass.AP(scl, 0, [[nb_width, 128], [1, NB], [0, 32]])
    tv = tmpw[:, 0:I].rearrange("p (b i) -> p b i", i=32)
    t4 = p.v(lambda E: E.tensor_tensor(tv, wv, iv, ALU.mult), waits=[t3])
    t5 = p.v(lambda E: E.tensor_scalar(tmpw[:, 0:I], tmpw[:, 0:I], MAGIC,
                                       -MAGIC, ALU.add, ALU.add), waits=[t4])
    t7 = p.v(lambda E: E.tensor_tensor(
        wq_bf[:, 0:I].rearrange("p (b i) -> p b i", i=32), tv, svb, ALU.mult),
        waits=[t5])
    pT2 = bufs.get("pT2")
    banks = [pT, pT2] if pT2 is not None else [pT]
    free = bufs.setdefault("_pt_free", [()] * len(banks))
    nkc = I // 128
    ev = None
    for kc in range(nkc):
        bk = kc % len(banks)
        tt = p.t(lambda E, kc=kc, bk=bk: E.transpose(
            banks[bk][:, 0:128], wq_bf[:, kc * 128:(kc + 1) * 128],
            ident_bf[:]), waits=[t7] + list(free[bk]))
        ew = [tt] + ([ev] if kc == nkc - 1 and ev is not None else [])
        if kc % 2 == 0:
            ev = p.v(lambda E, kc=kc, bk=bk: E.tensor_copy(
                lhsT_out[:, kc * 128:(kc + 1) * 128], banks[bk][:, 0:128]),
                waits=ew)
        else:
            ev = p.s(lambda E, kc=kc, bk=bk: E.activation(
                lhsT_out[:, kc * 128:(kc + 1) * 128], banks[bk][:, 0:128],
                AF.Identity), waits=ew)
        free[bk] = (ev,)
    return ev, t7


def qdq_chain(p, dst, src, invs_col, s_col, tmp, waits=(), magic_col=None,
              split=True):
    if split and magic_col is not None:
        t1 = p.s(lambda E: E.activation(tmp, src, AF.Identity,
                                        scale=invs_col, bias=magic_col),
                 waits=waits)
    else:
        t1 = p.v(lambda E: E.tensor_scalar(tmp, src, invs_col, MAGIC,
                                           ALU.mult, ALU.add), waits=waits)
    t2 = p.v(lambda E: E.tensor_scalar(dst, tmp, -MAGIC, s_col, ALU.add,
                                       ALU.mult), waits=[t1])
    return t2


def build_ffn_stage(stage, S, n_cores):
    nc = bass.Bass("TRN2", target_bir_lowering=False)
    TC = S // 512
    NC = n_cores
    x = nc.dram_tensor("x", [D, S], F32, kind="ExternalInput")
    ab = nc.dram_tensor("ab", [2, S], F32, kind="ExternalInput")
    mxn = nc.dram_tensor("mxn", [1, NC], F32, kind="ExternalInput")
    wup = nc.dram_tensor("wup", [3072, D], F32, kind="ExternalInput")
    wgate = nc.dram_tensor("wgate", [3072, D], F32, kind="ExternalInput")
    ident = nc.dram_tensor("ident", [128, 128], BF16, kind="ExternalInput")
    if stage in ("C", "D"):
        mgate_i = nc.dram_tensor("mgate", [1, NC], F32, kind="ExternalInput")
    if stage == "D":
        mup_i = nc.dram_tensor("mup", [1, NC], F32, kind="ExternalInput")
        mg2_i = nc.dram_tensor("mg2", [1, NC], F32, kind="ExternalInput")
    nout = {"B": 2, "C": 1, "D": 1}[stage]
    o_max = nc.dram_tensor("o_max", [1, nout], F32, kind="ExternalOutput")

    es = ExitStack()
    bufs, sb, ps = common_bufs(nc, es, S, stats=False, big=False)
    bufs["pT"] = es.enter_context(nc.psum_tensor("pT", [128, 128], BF16))
    bufs["pT2"] = es.enter_context(nc.psum_tensor("pT2", [128, 128], BF16))
    xnq_bf = sb("xnq_bf", [128, 8 * S], BF16)
    bufs["s1"] = sb("s1", [1, 2])
    ident_bf = sb("ident_bf", [128, 128], BF16)
    invs_xn = sb("invs_xn", [128, 1])
    s_xn = sb("s_xn", [128, 1])
    m1a = sb("m1a", [1, 1])
    m8row = sb("m8row", [1, NC])
    mxcol = sb("mxcol", [128, 2])
    magicc = sb("magicc", [128, 1])
    bufs["magicc"] = magicc
    mcol1 = bufs["mcol1"]
    cols = {}
    for nm in ("gate", "up", "g2"):
        cols["invs_" + nm] = sb("invs_" + nm, [128, 1])
        cols["s_" + nm] = sb("s_" + nm, [128, 1])

    p = Prog(nc, es)
    p.v(lambda E: E.memset(bufs["onec"][:], 1.0))
    p.v(lambda E: E.memset(bufs["oner"][:], 1.0))
    p.v(lambda E: E.memset(mxcol[:], 0.0))
    p.v(lambda E: E.memset(magicc[:], MAGIC))
    lid = p.sdma(ident_bf[:], ident[:])
    p.wait_on("tensor", [lid])
    lab = p.sdma(bufs["a_row"][:], ab[0:1, :], ch=1)
    lab2 = p.sdma(bufs["b_row"][:], ab[1:2, :], ch=1)
    p.flush()
    lmx = em_recv_max(p, mxn, m8row, m1a[0:1, 0:1])
    em_scalar_cols(p, bufs, m1a[0:1, 0:1], invs_xn[:], s_xn[:], waits=[lmx])
    for nm, hs in (("gate", "mgate"), ("up", "mup"), ("g2", "mg2")):
        need = (stage in ("C", "D") and nm == "gate") or                (stage == "D" and nm in ("up", "g2"))
        if need:
            hten = mgate_i if hs == "mgate" else (
                mup_i if hs == "mup" else mg2_i)
            lm = em_recv_max(p, hten, m8row, m1a[0:1, 0:1],
                             waits=[p.now("ve")])
            em_scalar_cols(p, bufs, m1a[0:1, 0:1], cols["invs_" + nm][:],
                           cols["s_" + nm][:], waits=[lm])
    with ExitStack() as e1:
        sb1 = lambda n, sh, dt=F32: e1.enter_context(
            nc.sbuf_tensor(n, list(sh), dt))
        bufs["a_bc"] = sb1("a_bc", [128, S])
        bufs["b_bc"] = sb1("b_bc", [128, S])
        bufs["xrow"] = sb1("xrow", [128, S])
        bufs["xrow_b"] = sb1("xrow_b", [128, S])
        bufs["tmprow"] = sb1("tmprow", [128, S])
        bufs["tmprow_b"] = sb1("tmprow_b", [128, S])
        em_ln_bcast(p, bufs, S, waits=[lab, lab2])
        em_xn(p, bufs, x, S, xnq_bf=xnq_bf, q_invs=invs_xn[:, 0:1],
              q_s=s_xn[:, 0:1], waits=[p.now("ve")])
        p.flush()

    e2s = es.enter_context(ExitStack())
    sb2 = lambda n, sh, dt=F32: e2s.enter_context(
        nc.sbuf_tensor(n, list(sh), dt))
    for pre in ("u0_", "u1_", "g0_", "g1_"):
        bufs[pre + "wtile"] = sb2(pre + "wtile", [128, D])
        bufs[pre + "wq_bf"] = sb2(pre + "wq_bf", [128, D], BF16)
        bufs[pre + "scl"] = sb2(pre + "scl", [128, 32])
        bufs[pre + "invscl"] = sb2(pre + "invscl", [128, 32])
        bufs[pre + "tmpw"] = sb2(pre + "tmpw", [128, D])
    lhsT = [sb2("lhsT%d" % i, [128, 8 * 128], BF16) for i in range(4)]
    raw32 = sb2("raw32", [128, 512])
    up32 = sb2("up32", [128, 512])
    c1 = sb2("c1", [128, 512])
    c2 = sb2("c2", [128, 512])
    c3 = sb2("c3", [128, 512])

    raw32b = [raw32, sb2("raw32_b", [128, 512]), sb2("raw32_c", [128, 512])]
    up32b = [up32, sb2("up32_b", [128, 512]), sb2("up32_c", [128, 512])]
    c1b = [c1, sb2("c1_b", [128, 512]), sb2("c1_c", [128, 512])]
    c2b = [c2, sb2("c2_b", [128, 512]), sb2("c2_c", [128, 512])]
    c3b = [c3, sb2("c3_b", [128, 512]), sb2("c3_c", [128, 512])]
    mcol2 = [mcol1, sb2("mcol_b", [128, 1]), sb2("mcol_c", [128, 1])]
    pup3 = [bufs["pA"][:, 0:512], bufs["pA"][:, 512:1024],
            bufs["pC"][:, 0:512]]
    pgt3 = [bufs["pB"][:, 0:512], bufs["pB"][:, 512:1024],
            bufs["pD"][:, 0:512]]
    mg = magicc[:, 0:1]

    names = {"B": ("up", "gate"), "C": ("gate",), "D": ("up", "gate")}[stage]
    whbm = {"up": wup, "gate": wgate}
    ltok = {"up": {0: (), 1: ()}, "gate": {0: (), 1: ()}}
    chain_toks = {"up": {0: (), 1: ()}, "gate": {0: (), 1: ()}}
    lhsT2 = {"up": [lhsT[0], lhsT[1]], "gate": [lhsT[2], lhsT[3]]}
    pt = [dict(), dict(), dict()]
    psum_free = {"up": [(), (), ()], "gate": [(), (), ()]}
    r2_last = {0: (), 1: ()}
    for it in range(24):
        bb = it % 2
        tls = {}
        for nm in names:
            tls[nm], ct = em_lpbq_tile(
                p, bufs, whbm[nm], it, D, lhsT2[nm][bb], ident_bf,
                cons=ltok[nm][bb], ch=2 if nm == "up" else 3,
                pre=("u%d_" % bb) if nm == "up" else ("g%d_" % bb),
                chain_tok=chain_toks[nm][bb])
            chain_toks[nm][bb] = (ct,)
        mm_last = {nm: None for nm in names}
        for j in range(0, TC, 3):
            sts = []
            for tc in range(j, min(j + 3, TC)):
                bq = tc % 3
                st = {"b": bq}
                for nm in names:
                    ps_ = pup3[bq] if nm == "up" else pgt3[bq]
                    mm = None
                    for kc in range(8):
                        w = ([tls[nm]] + list(psum_free[nm][bq])) \
                            if kc == 0 else ()
                        mm = p.t(lambda E, kc=kc, tc=tc, nm=nm, bb=bb,
                                 ps_=ps_: E.matmul(
                            ps_[:, 0:512],
                            lhsT2[nm][bb][:, kc * 128:(kc + 1) * 128],
                            xnq_bf[:, kc * S + tc * 512:
                                   kc * S + (tc + 1) * 512],
                            start=(kc == 0), stop=(kc == 7)), waits=w)
                    st["mm_" + nm] = mm
                    mm_last[nm] = mm
                sts.append(st)
            if stage == "B":
                for sl, nm in enumerate(names):
                    for st in sts:
                        bq = st["b"]
                        ps_ = pup3[bq] if nm == "up" else pgt3[bq]
                        r1 = p.v(lambda E, ps_=ps_, bq=bq: E.tensor_reduce(
                            mcol2[bq][:], ps_[:, 0:512],
                            AXX, ALU.max, apply_absolute_value=True),
                            waits=[st["mm_" + nm]] + list(pt[bq].get("mc", ())))
                        psum_free[nm][bq] = (r1,)
                        r2 = p.v(lambda E, sl=sl, bq=bq: E.tensor_tensor(
                            mxcol[:, sl:sl + 1], mxcol[:, sl:sl + 1],
                            mcol2[bq][:], ALU.max),
                            waits=[r1] + list(r2_last[sl]))
                        r2_last[sl] = (r2,)
                        pt[bq]["mc"] = (r2,)
                continue
            if stage == "D":
                for st in sts:
                    bq = st["b"]
                    st["evu"] = p.s(lambda E, bq=bq: E.activation(
                        up32b[bq][:], pup3[bq][:, 0:512],
                        AF.Identity),
                        waits=[st["mm_up"]] + list(pt[bq].get("up32", ())))
                    psum_free["up"][bq] = (st["evu"],)
            for st in sts:
                bq = st["b"]
                st["evg"] = p.s(lambda E, bq=bq: E.activation(
                    raw32b[bq][:], pgt3[bq][:, 0:512],
                    AF.Identity),
                    waits=[st["mm_gate"]] + list(pt[bq].get("raw32", ())))
                psum_free["gate"][bq] = (st["evg"],)
            for st in sts:
                bq = st["b"]
                st["g1a"] = p.s(lambda E, bq=bq: E.activation(
                    c1b[bq][:], raw32b[bq][:], AF.Identity,
                    scale=cols["invs_gate"][:, 0:1], bias=mg),
                    waits=[st["evg"]] + list(pt[bq].get("c1", ())))
            for st in sts:
                bq = st["b"]
                st["g1b"] = p.v(lambda E, bq=bq: E.tensor_scalar(
                    raw32b[bq][:], c1b[bq][:], -MAGIC,
                    cols["s_gate"][:, 0:1], ALU.add, ALU.mult),
                    waits=[st["g1a"]])
            for st in sts:
                bq = st["b"]
                st["sg"] = p.s(lambda E, bq=bq: E.activation(
                    c2b[bq][:], raw32b[bq][:], AF.Sigmoid),
                    waits=[st["g1b"]] + list(pt[bq].get("c2", ())))
            for st in sts:
                bq = st["b"]
                st["f1a"] = p.s(lambda E, bq=bq: E.activation(
                    c3b[bq][:], c2b[bq][:], AF.Identity, scale=65536.0,
                    bias=mg), waits=[st["sg"]] + list(pt[bq].get("c3", ())))
            for st in sts:
                bq = st["b"]
                st["f1b"] = p.v(lambda E, bq=bq: E.tensor_scalar(
                    c2b[bq][:], c3b[bq][:], -MAGIC, 65535.0, ALU.add,
                    ALU.min), waits=[st["f1a"]])
            for st in sts:
                bq = st["b"]
                st["pm"] = p.v(lambda E, bq=bq: E.scalar_tensor_tensor(
                    raw32b[bq][:], c2b[bq][:], 1.0 / 65536.0, raw32b[bq][:],
                    ALU.mult, ALU.mult), waits=[st["f1b"]])
                pt[bq]["c2"] = (st["pm"],)
            if stage == "C":
                for st in sts:
                    bq = st["b"]
                    pt[bq]["c1"] = (st["g1b"],)
                    pt[bq]["c3"] = (st["f1b"],)
                    r1 = p.v(lambda E, bq=bq: E.tensor_reduce(
                        mcol2[bq][:], raw32b[bq][:], AXX, ALU.max,
                        apply_absolute_value=True),
                        waits=[st["pm"]] + list(pt[bq].get("mc", ())))
                    pt[bq]["raw32"] = (r1,)
                    r2 = p.v(lambda E, bq=bq: E.tensor_tensor(
                        mxcol[:, 0:1], mxcol[:, 0:1], mcol2[bq][:],
                        ALU.max), waits=[r1] + list(r2_last[0]))
                    r2_last[0] = (r2,)
                    pt[bq]["mc"] = (r2,)
                continue
            for st in sts:
                bq = st["b"]
                st["g2a"] = p.s(lambda E, bq=bq: E.activation(
                    c1b[bq][:], raw32b[bq][:], AF.Identity,
                    scale=cols["invs_g2"][:, 0:1], bias=mg),
                    waits=[st["pm"]] + list(pt[bq].get("c1", ())))
            for st in sts:
                bq = st["b"]
                st["g2b"] = p.v(lambda E, bq=bq: E.tensor_scalar(
                    raw32b[bq][:], c1b[bq][:], -MAGIC, cols["s_g2"][:, 0:1],
                    ALU.add, ALU.mult), waits=[st["g2a"]])
                pt[bq]["c1"] = (st["g2b"],)
            for st in sts:
                bq = st["b"]
                st["uqa"] = p.s(lambda E, bq=bq: E.activation(
                    c3b[bq][:], up32b[bq][:], AF.Identity,
                    scale=cols["invs_up"][:, 0:1], bias=mg),
                    waits=[st["evu"], st["f1b"]])
                pt[bq]["up32"] = (st["uqa"],)
            for st in sts:
                bq = st["b"]
                st["uqb"] = p.v(lambda E, bq=bq: E.tensor_scalar(
                    c3b[bq][:], c3b[bq][:], -MAGIC, cols["s_up"][:, 0:1],
                    ALU.add, ALU.mult), waits=[st["uqa"]])
            for st in sts:
                bq = st["b"]
                st["orw"] = p.v(lambda E, bq=bq: E.tensor_tensor(
                    raw32b[bq][:], raw32b[bq][:], c3b[bq][:], ALU.mult),
                    waits=[st["uqb"], st["g2b"]])
                pt[bq]["c3"] = (st["orw"],)
            for st in sts:
                bq = st["b"]
                r1 = p.v(lambda E, bq=bq: E.tensor_reduce(
                    mcol2[bq][:], raw32b[bq][:], AXX, ALU.max,
                    apply_absolute_value=True),
                    waits=[st["orw"]] + list(pt[bq].get("mc", ())))
                pt[bq]["raw32"] = (r1,)
                r2 = p.v(lambda E, bq=bq: E.tensor_tensor(
                    mxcol[:, 0:1], mxcol[:, 0:1], mcol2[bq][:], ALU.max),
                    waits=[r1] + list(r2_last[0]))
                r2_last[0] = (r2,)
                pt[bq]["mc"] = (r2,)
        for nm in names:
            ltok[nm][bb] = (mm_last[nm],)
    mrow = bufs["mrow"]
    res = []
    for sl in range(nout):
        d1 = p.gdma(mrow[0:1, 0:128], mxcol[:, sl:sl + 1],
                    waits=[p.now("ve")], ch=1)
        r1 = p.v(lambda E, sl=sl: E.tensor_reduce(
            bufs["s1"][0:1, sl:sl + 1], mrow[0:1, 0:128], AXX, ALU.max),
            waits=[d1])
        res.append(r1)
    st = p.gdma(o_max[:], bufs["s1"][0:1, 0:nout], waits=res, ch=2)
    p.wait_on("sync", p.alldma())
    p.flush()
    es.close()
    return nc


def fq16_ops_inline(p, dst, src, tmp, waits=(), magic_col=None):
    if magic_col is not None:
        t1 = p.s(lambda E: E.activation(tmp[:], src[:], AF.Identity,
                                        scale=65536.0, bias=magic_col),
                 waits=waits)
    else:
        t1 = p.v(lambda E: E.tensor_scalar(tmp[:], src[:], 65536.0, MAGIC,
                                           ALU.mult, ALU.add), waits=waits)
    t2 = p.v(lambda E: E.tensor_scalar(dst[:], tmp[:], -MAGIC, 65535.0,
                                       ALU.add, ALU.min), waits=[t1])
    return t2




def build_E(S, n_cores):
    nc = bass.Bass("TRN2", target_bir_lowering=False)
    TC = S // 512
    TT = S // 128
    x = nc.dram_tensor("x", [D, S], F32, kind="ExternalInput")
    ab = nc.dram_tensor("ab", [2, S], F32, kind="ExternalInput")
    shat = nc.dram_tensor("shat", [128, 10], F32, kind="ExternalInput")
    NC = n_cores
    m_in = {}
    for nm in ("mxn", "mgate", "mup", "mg2", "mo"):
        m_in[nm] = nc.dram_tensor(nm, [1, NC], F32, kind="ExternalInput")
    wqT = nc.dram_tensor("wqT", [D, D], F32, kind="ExternalInput")
    bqc = nc.dram_tensor("bqc", [128, 8], F32, kind="ExternalInput")
    fpad_h = nc.dram_tensor("fpad", [D, 2 * NF], BF16, kind="ExternalInput")
    cmat_h = nc.dram_tensor("cmat", [NF, 2 * D], BF16, kind="ExternalInput")
    ident = nc.dram_tensor("ident", [128, 128], BF16, kind="ExternalInput")
    wup = nc.dram_tensor("wup", [3072, D], F32, kind="ExternalInput")
    wgate = nc.dram_tensor("wgate", [3072, D], F32, kind="ExternalInput")
    wdown = nc.dram_tensor("wdown", [D, 3072], F32, kind="ExternalInput")
    awq = nc.dram_tensor("awq", [D, 256], F32, kind="ExternalInput")
    awk = nc.dram_tensor("awk", [D, 256], F32, kind="ExternalInput")
    wvdT = nc.dram_tensor("wvdT", [256, D], F32, kind="ExternalInput")
    wvup = nc.dram_tensor("wvup", [256, D], F32, kind="ExternalInput")
    rwqT = nc.dram_tensor("rwqT", [D, D], F32, kind="ExternalInput")
    rwkT = nc.dram_tensor("rwkT", [D, D], F32, kind="ExternalInput")
    rbqc = nc.dram_tensor("rbqc", [128, 8], F32, kind="ExternalInput")
    rbkc = nc.dram_tensor("rbkc", [128, 8], F32, kind="ExternalInput")
    ewqT = nc.dram_tensor("ewqT", [128, 8 * 128], F32, kind="ExternalInput")
    ewkT = nc.dram_tensor("ewkT", [128, 8 * 128], F32, kind="ExternalInput")
    ebqc = nc.dram_tensor("ebqc", [128, 8], F32, kind="ExternalInput")
    ebkc = nc.dram_tensor("ebkc", [128, 8], F32, kind="ExternalInput")
    wexp = nc.dram_tensor("wexp", [8, D, D], F32, kind="ExternalInput")
    bexp = nc.dram_tensor("bexp", [8, D], F32, kind="ExternalInput")
    o_x2 = nc.dram_tensor("o_x2", [D, S], F32, kind="ExternalOutput")
    xn_d = nc.dram_tensor("xn_d", [128, 8 * S], BF16, kind="Internal")
    xnq_d = nc.dram_tensor("xnq_d", [128, 8 * S], BF16, kind="Internal")
    sfm_d = nc.dram_tensor("sfm_d", [D, S], F32, kind="Internal")
    wqlu_d = nc.dram_tensor("wqlu_d", [128, 24 * 1024], BF16, kind="Internal")
    wqlg_d = nc.dram_tensor("wqlg_d", [128, 24 * 1024], BF16, kind="Internal")
    wqld_d = nc.dram_tensor("wqld_d", [128, 8 * 24 * 128], BF16, kind="Internal")
    x1_d = nc.dram_tensor("x1_d", [D, S], F32, kind="Internal")
    xn1_d = nc.dram_tensor("xn1_d", [128, 8 * S], BF16, kind="Internal")
    s1_d = nc.dram_tensor("s1_d", [D, S], F32, kind="Internal")
    m_d = nc.dram_tensor("m_d", [8, S], F32, kind="Internal")

    es = ExitStack()
    sb = lambda n, sh, dt=F32: es.enter_context(nc.sbuf_tensor(n, list(sh), dt))
    ps_ = lambda n, sh, dt=F32: es.enter_context(nc.psum_tensor(n, list(sh), dt))
    bufs = {}
    bufs["pA"] = ps_("pA", [128, 1024])
    bufs["pB"] = ps_("pB", [128, 1024])
    bufs["pCD"] = ps_("pCD", [128, 1024])
    bufs["pC"] = bufs["pCD"][:, 0:512]
    bufs["pD"] = bufs["pCD"][:, 512:1024]
    pre_ps = ExitStack()
    bufs["pT"] = pre_ps.enter_context(nc.psum_tensor("pT", [128, 128], BF16))
    bufs["pT2"] = pre_ps.enter_context(nc.psum_tensor("pT2", [128, 128],
                                                      BF16))
    for nm in ("onec", "mcol1"):
        bufs[nm] = sb(nm, [128, 1])
    bufs["oner"] = sb("oner", [1, 128])
    bufs["eps1"] = sb("eps1", [1, 1])
    bufs["mrow"] = sb("mrow", [1, 128])
    bufs["s1"] = sb("s1", [1, 2])
    bufs["xt2"] = sb("xt2", [128, 1024])
    bufs["sq"] = sb("sq", [128, 512])
    m1a = sb("m1a", [1, 1])
    magicc = sb("magicc", [128, 1])
    bufs["magicc"] = magicc
    ident_bf = sb("ident_bf", [128, 128], BF16)
    cols = {}
    for nm in ("xn", "gate", "up", "g2", "o"):
        cols["invs_" + nm] = sb("invs_" + nm, [128, 1])
        cols["s_" + nm] = sb("s_" + nm, [128, 1])

    p = Prog(nc, es)
    p.v(lambda E: E.memset(bufs["onec"][:], 1.0))
    p.v(lambda E: E.memset(bufs["oner"][:], 1.0))
    p.v(lambda E: E.memset(magicc[:], MAGIC))
    lid = p.sdma(ident_bf[:], ident[:])
    p.wait_on("tensor", [lid])
    m8row = sb("m8row", [1, NC])
    for nm in ("xn", "gate", "up", "g2", "o"):
        lm = em_recv_max(p, m_in[{"xn": "mxn", "gate": "mgate", "up": "mup",
                                  "g2": "mg2", "o": "mo"}[nm]], m8row,
                         m1a[0:1, 0:1], waits=[p.now("ve")])
        em_scalar_cols(p, bufs, m1a[0:1, 0:1], cols["invs_" + nm][:],
                       cols["s_" + nm][:], waits=[lm])
    p.flush()

    with ExitStack() as e1:
        sb1 = lambda n, sh, dt=F32: e1.enter_context(
            nc.sbuf_tensor(n, list(sh), dt))
        bufs["a_row"] = sb1("a_row", [1, S])
        bufs["b_row"] = sb1("b_row", [1, S])
        bufs["a_bc"] = sb1("a_bc", [128, S])
        bufs["b_bc"] = sb1("b_bc", [128, S])
        xr2 = [sb1("xrow%d" % i, [128, S]) for i in range(2)]
        tp2 = [sb1("tmpr%d" % i, [128, S]) for i in range(2)]
        ob2 = [sb1("obf%d" % i, [128, S], BF16) for i in range(2)]
        oq2 = [sb1("obfq%d" % i, [128, S], BF16) for i in range(2)]
        lab = p.sdma(bufs["a_row"][:], ab[0:1, :], ch=1)
        lab2 = p.sdma(bufs["b_row"][:], ab[1:2, :], ch=1)
        em_ln_bcast(p, bufs, S, waits=[lab, lab2])
        bw = (p.now("ve"),)
        rd1 = [dict(), dict()]
        st1 = st2 = None
        for dc in range(8):
            bq = dc % 2
            xr = xr2[bq]
            ld = p.sdma(xr[:], x[dc * 128:(dc + 1) * 128, :],
                        waits=list(rd1[bq].get("xr", ())), ch=1)
            t1 = p.v(lambda E, xr=xr: E.tensor_tensor(
                xr[:], xr[:], bufs["a_bc"][:], ALU.mult),
                waits=[ld] + list(bw))
            t2 = p.v(lambda E, xr=xr: E.tensor_tensor(
                xr[:], xr[:], bufs["b_bc"][:], ALU.add), waits=[t1])
            c1t = p.v(lambda E, xr=xr, bq=bq: E.tensor_copy(
                ob2[bq][:], xr[:]),
                waits=[t2] + list(rd1[bq].get("obf", ())))
            st1 = p.sdma(xn_d[:, dc * S:(dc + 1) * S], ob2[bq][:],
                         waits=[c1t], ch=2)
            rd1[bq]["obf"] = (st1,)
            q3 = qdq_chain(p, oq2[bq][:], xr[:], cols["invs_xn"][:, 0:1],
                           cols["s_xn"][:, 0:1], tp2[bq][:],
                           waits=[t2] + list(rd1[bq].get("oq", ())))
            st2 = p.sdma(xnq_d[:, dc * S:(dc + 1) * S], oq2[bq][:],
                         waits=[q3], ch=3)
            rd1[bq]["oq"] = (st2,)
            rd1[bq]["xr"] = (c1t, q3)
        p.wait_on("vector", [st1, st2])
        p.flush()

    with ExitStack() as e2:
        sb2 = lambda n, sh, dt=F32: e2.enter_context(
            nc.sbuf_tensor(n, list(sh), dt))
        bufs["Fpad"] = sb2("Fpad", [128, 8 * 2 * NF], BF16)
        WqF = sb2("WqF", [128, 8 * 2 * NF], BF16)
        wT_bf = sb2("wT_bf", [128, 8 * D], BF16)
        bc_bf = sb2("bc_bf", [128, 8], BF16)
        bqF = sb2("bqF", [128, 10])
        Cmat = sb2("Cmat", [128, 5 * 2048], BF16)
        Scol = sb2("Scol", [128, 10])
        xnsl = sb2("xnsl", [128, 8 * 512], BF16)
        Q32 = sb2("Q32", [128, 10 * 512])
        Pbuf = sb2("Pbuf", [128, 10 * 512], BF16)
        ctmp = sb2("ctmp", [128, 512])
        sfmt = sb2("sfmt", [128, 512])
        lf = p.sdma(bufs["Fpad"][:].rearrange("p (c f) -> p c f", c=8),
                    fpad_h[:].rearrange("(c p) f -> p c f", p=128))
        lw = p.gdma(wT_bf[:].rearrange("p (c f) -> p c f", c=8),
                    wqT[:].rearrange("(c p) f -> p c f", p=128))
        lb = p.gdma(bc_bf[:], bqc[:])
        lc = p.sdma(Cmat[:].rearrange("p (c f) -> p c f", c=5),
                    cmat_h[:].rearrange("(c p) f -> p c f", p=128), ch=1)
        lsh = p.sdma(Scol[:], shat[:], ch=1)
        em_fold(p, bufs, wT_bf, WqF, bc_bf, bqF, waits=[lf, lw, lb])
        ldx = None
        for tc in range(TC):
            ldx = p.sdma(
                xnsl[:].rearrange("p (c s) -> p c s", c=8),
                xn_d[:].rearrange("p (c s) -> p c s", c=8)[
                    :, :, tc * 512:(tc + 1) * 512],
                waits=[p.now("te")], ch=2)
            em_rfft(p, bufs, WqF, bqF, xnsl, 0, 512, Q32,
                    waits=[ldx, p.now("ve")])
            tq = p.now("sc")
            last = []
            for m in range(5):
                Qr = Q32[:, m * 512:(m + 1) * 512]
                Qi = Q32[:, (m + 5) * 512:(m + 6) * 512]
                Pr = Pbuf[:, m * 512:(m + 1) * 512]
                Pi = Pbuf[:, (m + 5) * 512:(m + 6) * 512]
                u1 = p.v(lambda E, Qr=Qr, m=m: E.tensor_scalar(
                    ctmp[:], Qr, Scol[:, m:m + 1], None, ALU.mult),
                    waits=[tq, lsh, p.now("ve"), p.now("te")])
                u2 = p.v(lambda E, Qi=Qi, Pr=Pr, m=m: E.scalar_tensor_tensor(
                    Pr, Qi, Scol[:, m + 5:m + 6], ctmp[:], ALU.mult, ALU.add),
                    waits=[u1])
                u3 = p.v(lambda E, Qi=Qi, m=m: E.tensor_scalar(
                    ctmp[:], Qi, Scol[:, m:m + 1], -1.0, ALU.mult, ALU.mult),
                    waits=[u2])
                u4 = p.v(lambda E, Qr=Qr, Pi=Pi, m=m: E.scalar_tensor_tensor(
                    Pi, Qr, Scol[:, m + 5:m + 6], ctmp[:], ALU.mult, ALU.add),
                    waits=[u3])
                last.append(u4)
            for dm in range(8):
                mm = None
                for fc in range(5):
                    w = [last[-1]] if fc == 0 else ()
                    mm = p.t(lambda E, dm=dm, fc=fc: E.matmul(
                        bufs["pC"][:, 0:512],
                        Cmat[:, fc * 2048 + dm * 128:fc * 2048 + (dm + 1) * 128],
                        Pbuf[:, fc * 512:(fc + 1) * 512],
                        start=(fc == 0), stop=False), waits=w)
                for fc in range(5):
                    mm = p.t(lambda E, dm=dm, fc=fc: E.matmul(
                        bufs["pC"][:, 0:512],
                        Cmat[:, fc * 2048 + 1024 + dm * 128:
                             fc * 2048 + 1024 + (dm + 1) * 128],
                        Pbuf[:, (fc + 5) * 512:(fc + 6) * 512],
                        start=False, stop=(fc == 4)))
                ev = p.v(lambda E: E.tensor_copy(sfmt[:], bufs["pC"][:, 0:512]),
                         waits=[mm, p.now("sd3")])
                p.wait_on("tensor", [ev])
                p.sdma(sfm_d[dm * 128:(dm + 1) * 128, tc * 512:(tc + 1) * 512],
                       sfmt[:], waits=[ev], ch=3)
        p.flush()

    with ExitStack() as e3:
        sb3 = lambda n, sh, dt=F32: e3.enter_context(
            nc.sbuf_tensor(n, list(sh), dt))
        for pre, I in (("u0_", D), ("u1_", D), ("d0_", 3072), ("d1_", 3072)):
            bufs[pre + "wtile"] = sb3(pre + "wtile", [128, I])
            bufs[pre + "wq_bf"] = sb3(pre + "wq_bf", [128, I], BF16)
            bufs[pre + "scl"] = sb3(pre + "scl", [128, I // 32])
            bufs[pre + "invscl"] = sb3(pre + "invscl", [128, I // 32])
            bufs[pre + "tmpw"] = sb3(pre + "tmpw", [128, I])
        lhsTa = [sb3("lhsTa0", [128, 1024], BF16),
                 sb3("lhsTa1", [128, 1024], BF16)]
        lhsTd = [sb3("lhsTd0", [128, 3072], BF16),
                 sb3("lhsTd1", [128, 3072], BF16)]
        stok = {0: (), 1: ()}
        ctok = {0: (), 1: ()}
        k = 0
        for nm, wh, dst in (("up", wup, wqlu_d), ("gate", wgate, wqlg_d)):
            for it in range(24):
                bb = k % 2
                k += 1
                tl, ct = em_lpbq_tile(p, bufs, wh, it, D, lhsTa[bb], ident_bf,
                                      cons=stok[bb], ch=2 + bb,
                                      pre="u%d_" % bb, chain_tok=ctok[bb])
                ctok[bb] = (ct,)
                st = p.sdma(dst[:, it * 1024:(it + 1) * 1024], lhsTa[bb][:],
                            waits=[tl], ch=bb)
                stok[bb] = (st,)
        stok = {0: (), 1: ()}
        ctok = {0: (), 1: ()}
        for ot in range(8):
            bb = ot % 2
            tl, ct = em_lpbq_tile(p, bufs, wdown, ot, 3072, lhsTd[bb],
                                  ident_bf, cons=stok[bb], ch=2 + bb,
                                  pre="d%d_" % bb, nb_width=96,
                                  chain_tok=ctok[bb])
            ctok[bb] = (ct,)
            st = p.sdma(wqld_d[:, ot * 3072:(ot + 1) * 3072], lhsTd[bb][:],
                        waits=[tl], ch=bb)
            stok[bb] = (st,)
        p.wait_on("vector", p.alldma())
        p.flush()

    NQ = max(1, TC // 2)
    with ExitStack() as e4:
        sb4 = lambda n, sh, dt=F32: e4.enter_context(
            nc.sbuf_tensor(n, list(sh), dt))
        xnqq = sb4("xnqq", [128, 8 * 1024], BF16)
        oqq = sb4("oqq", [128, 24 * 1024], BF16)
        lu = [sb4("lu%d" % i, [128, 1024], BF16) for i in range(2)]
        lg = [sb4("lg%d" % i, [128, 1024], BF16) for i in range(2)]
        ldn = [sb4("ldn%d" % i, [128, 3072], BF16) for i in range(2)]
        up32b = [sb4("up32_%d" % i, [128, 512]) for i in range(2)]
        g32b = [sb4("g32_%d" % i, [128, 512]) for i in range(2)]
        c1b = [sb4("cc1_%d" % i, [128, 512]) for i in range(2)]
        c2b = [sb4("cc2_%d" % i, [128, 512]) for i in range(2)]
        c3b = [sb4("cc3_%d" % i, [128, 512]) for i in range(2)]
        sftb = [sb4("sft_%d" % i, [128, 512]) for i in range(2)]
        xtib = [sb4("xti_%d" % i, [128, 512]) for i in range(2)]
        x1tb = [sb4("x1t_%d" % i, [128, 512]) for i in range(2)]
        mg = magicc[:, 0:1]
        pt = [dict(), dict()]
        psum_free = {"up": [(), ()], "gate": [(), ()]}
        pt2 = [dict(), dict()]
        lx_prev = ()
        dn_prev = ()
        for q in range(NQ):
            nch = min(2, TC - q * 2)
            lx = p.sdma(
                xnqq[:].rearrange("p (c s) -> p c s", c=8)[:, :, 0:nch * 512],
                xnq_d[:].rearrange("p (c s) -> p c s", c=8)[
                    :, :, q * 1024:q * 1024 + nch * 512],
                waits=list(lx_prev), ch=1)
            ltoks = {0: (), 1: ()}
            mm_all = []
            for it in range(24):
                bb = it % 2
                l1 = p.sdma(lu[bb][:], wqlu_d[:, it * 1024:(it + 1) * 1024],
                            waits=ltoks[bb], ch=2)
                l2 = p.sdma(lg[bb][:], wqlg_d[:, it * 1024:(it + 1) * 1024],
                            waits=(), ch=2)
                sts = []
                for tcq in range(nch):
                    bq = tcq % 2
                    st = {"b": bq}
                    for nm, ps_, lt, lhs in (("up", bufs["pA"], l1, lu),
                                             ("gate", bufs["pB"], l2, lg)):
                        mm = None
                        for kc in range(8):
                            w = ([lt, lx] + list(psum_free[nm][bq])) \
                                if kc == 0 else ()
                            mm = p.t(lambda E, kc=kc, tcq=tcq, bb=bb, bq=bq,
                                     ps_=ps_, lhs=lhs: E.matmul(
                                ps_[:, bq * 512:bq * 512 + 512],
                                lhs[bb][:, kc * 128:(kc + 1) * 128],
                                xnqq[:, kc * 1024 + tcq * 512:
                                     kc * 1024 + (tcq + 1) * 512],
                                start=(kc == 0), stop=(kc == 7)), waits=w)
                        st["mm_" + nm] = mm
                        mm_all.append(mm)
                    sts.append(st)
                ltoks[bb] = (sts[-1]["mm_up"], sts[-1]["mm_gate"])
                for st in sts:
                    bq = st["b"]
                    st["evu"] = p.s(lambda E, bq=bq: E.activation(
                        up32b[bq][:], bufs["pA"][:, bq * 512:bq * 512 + 512],
                        AF.Identity),
                        waits=[st["mm_up"]] + list(pt[bq].get("up32", ())))
                    psum_free["up"][bq] = (st["evu"],)
                for st in sts:
                    bq = st["b"]
                    st["evg"] = p.s(lambda E, bq=bq: E.activation(
                        g32b[bq][:], bufs["pB"][:, bq * 512:bq * 512 + 512],
                        AF.Identity),
                        waits=[st["mm_gate"]] + list(pt[bq].get("g32", ())))
                    psum_free["gate"][bq] = (st["evg"],)
                for st in sts:
                    bq = st["b"]
                    st["g1a"] = p.s(lambda E, bq=bq: E.activation(
                        c1b[bq][:], g32b[bq][:], AF.Identity,
                        scale=cols["invs_gate"][:, 0:1], bias=mg),
                        waits=[st["evg"]] + list(pt[bq].get("c1", ())))
                for st in sts:
                    bq = st["b"]
                    st["g1b"] = p.v(lambda E, bq=bq: E.tensor_scalar(
                        g32b[bq][:], c1b[bq][:], -MAGIC,
                        cols["s_gate"][:, 0:1], ALU.add, ALU.mult),
                        waits=[st["g1a"]])
                for st in sts:
                    bq = st["b"]
                    st["sg"] = p.s(lambda E, bq=bq: E.activation(
                        c2b[bq][:], g32b[bq][:], AF.Sigmoid),
                        waits=[st["g1b"]] + list(pt[bq].get("c2", ())))
                for st in sts:
                    bq = st["b"]
                    st["f1a"] = p.s(lambda E, bq=bq: E.activation(
                        c3b[bq][:], c2b[bq][:], AF.Identity, scale=65536.0,
                        bias=mg),
                        waits=[st["sg"]] + list(pt[bq].get("c3", ())))
                for st in sts:
                    bq = st["b"]
                    st["f1b"] = p.v(lambda E, bq=bq: E.tensor_scalar(
                        c2b[bq][:], c3b[bq][:], -MAGIC, 65535.0, ALU.add,
                        ALU.min), waits=[st["f1a"]])
                for st in sts:
                    bq = st["b"]
                    st["pm"] = p.v(lambda E, bq=bq: E.scalar_tensor_tensor(
                        g32b[bq][:], c2b[bq][:], 1.0 / 65536.0, g32b[bq][:],
                        ALU.mult, ALU.mult), waits=[st["f1b"]])
                    pt[bq]["c2"] = (st["pm"],)
                for st in sts:
                    bq = st["b"]
                    st["g2a"] = p.s(lambda E, bq=bq: E.activation(
                        c1b[bq][:], g32b[bq][:], AF.Identity,
                        scale=cols["invs_g2"][:, 0:1], bias=mg),
                        waits=[st["pm"]])
                for st in sts:
                    bq = st["b"]
                    st["g2b"] = p.v(lambda E, bq=bq: E.tensor_scalar(
                        g32b[bq][:], c1b[bq][:], -MAGIC, cols["s_g2"][:, 0:1],
                        ALU.add, ALU.mult), waits=[st["g2a"]])
                for st in sts:
                    bq = st["b"]
                    st["uqa"] = p.s(lambda E, bq=bq: E.activation(
                        c3b[bq][:], up32b[bq][:], AF.Identity,
                        scale=cols["invs_up"][:, 0:1], bias=mg),
                        waits=[st["evu"], st["f1b"]])
                    pt[bq]["up32"] = (st["uqa"],)
                for st in sts:
                    bq = st["b"]
                    st["uqb"] = p.v(lambda E, bq=bq: E.tensor_scalar(
                        c3b[bq][:], c3b[bq][:], -MAGIC, cols["s_up"][:, 0:1],
                        ALU.add, ALU.mult), waits=[st["uqa"]])
                for st in sts:
                    bq = st["b"]
                    st["orw"] = p.v(lambda E, bq=bq: E.tensor_tensor(
                        g32b[bq][:], g32b[bq][:], c3b[bq][:], ALU.mult),
                        waits=[st["uqb"], st["g2b"]])
                    pt[bq]["c3"] = (st["orw"],)
                for st in sts:
                    bq = st["b"]
                    st["oqa"] = p.s(lambda E, bq=bq: E.activation(
                        c1b[bq][:], g32b[bq][:], AF.Identity,
                        scale=cols["invs_o"][:, 0:1], bias=mg),
                        waits=[st["orw"]])
                    pt[bq]["g32"] = (st["oqa"],)
                for st in sts:
                    bq = st["b"]
                    tcq = sts.index(st)
                    st["oqb"] = p.v(lambda E, bq=bq, it=it, tcq=tcq:
                                    E.tensor_scalar(
                        oqq[:, it * 1024 + tcq * 512:
                            it * 1024 + (tcq + 1) * 512],
                        c1b[bq][:], -MAGIC, cols["s_o"][:, 0:1], ALU.add,
                        ALU.mult), waits=[st["oqa"]] +
                        (list(dn_prev) if it == 0 else []))
                    pt[bq]["c1"] = (st["oqb"],)
            lx_prev = tuple(mm_all[-4:])
            stok2 = {0: (), 1: ()}
            oq_done = [st["oqb"] for st in sts]
            for ot in range(8):
                bb = ot % 2
                l3 = p.sdma(ldn[bb][:], wqld_d[:, ot * 3072:(ot + 1) * 3072],
                            waits=stok2[bb], ch=2)
                sts2 = []
                for tcq in range(nch):
                    bq = tcq % 2
                    s2 = {"b": bq}
                    mm = None
                    for ic in range(24):
                        w = ([l3] + oq_done + list(pt2[bq].get("ps", ()))) \
                            if ic == 0 else ()
                        mm = p.t(lambda E, ic=ic, tcq=tcq, bb=bb, bq=bq:
                                 E.matmul(
                            bufs["pA"][:, bq * 512:bq * 512 + 512],
                            ldn[bb][:, ic * 128:(ic + 1) * 128],
                            oqq[:, ic * 1024 + tcq * 512:
                                ic * 1024 + (tcq + 1) * 512],
                            start=(ic == 0), stop=(ic == 23)), waits=w)
                    s2["mm"] = mm
                    tcg = q * 2 + tcq
                    s2["ls"] = p.sdma(
                        sftb[bq][:], sfm_d[ot * 128:(ot + 1) * 128,
                                           tcg * 512:(tcg + 1) * 512],
                        waits=list(pt2[bq].get("sft", ())), ch=1)
                    s2["lxi"] = p.sdma(
                        xtib[bq][:], x[ot * 128:(ot + 1) * 128,
                                       tcg * 512:(tcg + 1) * 512],
                        waits=list(pt2[bq].get("xti", ())), ch=3)
                    s2["tcg"] = tcg
                    sts2.append(s2)
                stok2[bb] = (sts2[-1]["mm"],)
                for s2 in sts2:
                    bq = s2["b"]
                    s2["gv"] = p.s(lambda E, bq=bq: E.activation(
                        g32b[bq][:], bufs["pA"][:, bq * 512:bq * 512 + 512],
                        AF.Silu),
                        waits=[s2["mm"]] + list(pt2[bq].get("g32", ())))
                    pt2[bq]["ps"] = (s2["gv"],)
                for s2 in sts2:
                    bq = s2["b"]
                    s2["v1"] = p.v(lambda E, bq=bq: E.tensor_tensor(
                        x1tb[bq][:], sftb[bq][:], g32b[bq][:], ALU.mult),
                        waits=[s2["gv"], s2["ls"], s2["lxi"]] +
                        list(pt2[bq].get("x1t", ())))
                    pt2[bq]["g32"] = (s2["v1"],)
                    pt2[bq]["sft"] = (s2["v1"],)
                for s2 in sts2:
                    bq = s2["b"]
                    s2["v2"] = p.v(lambda E, bq=bq: E.tensor_tensor(
                        x1tb[bq][:], x1tb[bq][:], xtib[bq][:], ALU.add),
                        waits=[s2["v1"]])
                    pt2[bq]["xti"] = (s2["v2"],)
                for s2 in sts2:
                    bq = s2["b"]
                    so = p.sdma(x1_d[ot * 128:(ot + 1) * 128,
                                     s2["tcg"] * 512:(s2["tcg"] + 1) * 512],
                                x1tb[bq][:], waits=[s2["v2"]], ch=3)
                    pt2[bq]["x1t"] = (so,)
            dn_prev = (sts2[-1]["mm"],)
            psum_free["up"] = [tuple(pt2[bq2].get("ps", ()))
                               for bq2 in range(2)]
        p.wait_on("vector", p.alldma())
        p.flush()

    with ExitStack() as e5:
        sb5 = lambda n, sh, dt=F32: e5.enter_context(
            nc.sbuf_tensor(n, list(sh), dt))
        bufs["mu_row"] = sb5("mu_row", [1, S])
        bufs["sq_row"] = sb5("sq_row", [1, S])
        bufs["a_row"] = sb5("a_row5", [1, S])
        bufs["b_row"] = bufs["mu_row"]
        bufs["a_bc"] = sb5("a_bc2", [128, S])
        bufs["b_bc"] = sb5("b_bc2", [128, S])
        xr2 = [sb5("xrow5%d" % i, [128, S]) for i in range(2)]
        ob2 = [sb5("obf5%d" % i, [128, S], BF16) for i in range(2)]
        em_ln_stats(p, bufs, x1_d, S)
        em_ln_bcast(p, bufs, S, waits=[p.now("ve"), p.now("sc")])
        bw = (p.now("ve"),)
        rd5 = [dict(), dict()]
        st1 = None
        for dc in range(8):
            bq = dc % 2
            xr = xr2[bq]
            ld = p.sdma(xr[:], x1_d[dc * 128:(dc + 1) * 128, :],
                        waits=list(rd5[bq].get("xr", ())), ch=1)
            t1 = p.v(lambda E, xr=xr: E.tensor_tensor(
                xr[:], xr[:], bufs["a_bc"][:], ALU.mult),
                waits=[ld] + list(bw))
            t2 = p.v(lambda E, xr=xr: E.tensor_tensor(
                xr[:], xr[:], bufs["b_bc"][:], ALU.add), waits=[t1])
            c1t = p.v(lambda E, xr=xr, bq=bq: E.tensor_copy(
                ob2[bq][:], xr[:]),
                waits=[t2] + list(rd5[bq].get("obf", ())))
            st1 = p.sdma(xn1_d[:, dc * S:(dc + 1) * S], ob2[bq][:],
                         waits=[c1t], ch=2)
            rd5[bq]["obf"] = (st1,)
            rd5[bq]["xr"] = (c1t,)
        p.wait_on("vector", [st1])
        p.flush()

    with ExitStack() as e8:
        sb8 = lambda n, sh, dt=F32: e8.enter_context(
            nc.sbuf_tensor(n, list(sh), dt))
        Vtok = sb8("Vtok", [128, TT * 1024], BF16)
        Qf = sb8("Qf", [128, 2 * S], BF16)
        Kf = sb8("Kf", [128, 2 * S], BF16)
        with ExitStack() as e8v:
            sb8v = lambda n, sh, dt=F32: e8v.enter_context(
                nc.sbuf_tensor(n, list(sh), dt))
            veff = sb8v("veff", [128, 8 * 1024], BF16)
            with ExitStack() as e8w:
                sb8w = lambda n, sh, dt=F32: e8w.enter_context(
                    nc.sbuf_tensor(n, list(sh), dt))
                wvdT_bf = sb8w("wvdT_bf", [128, 2 * D], BF16)
                wvup_bf = sb8w("wvup_bf", [128, 2 * D], BF16)
                l3_ = p.gdma(wvdT_bf[:].rearrange("p (c f) -> p c f", c=2),
                             wvdT[:].rearrange("(c p) f -> p c f", p=128),
                             ch=1)
                l4_ = p.gdma(wvup_bf[:].rearrange("p (c f) -> p c f", c=2),
                             wvup[:].rearrange("(c p) f -> p c f", p=128),
                             ch=1)
                for dm in range(8):
                    for ocn in range(2):
                        mm = None
                        for rc in range(2):
                            w = [l3_, l4_, p.now("ve")] if rc == 0 else ()
                            mm = p.t(lambda E, dm=dm, ocn=ocn, rc=rc: E.matmul(
                                bufs["pD"][:, 0:512],
                                wvdT_bf[:, rc * D + dm * 128:
                                        rc * D + (dm + 1) * 128],
                                wvup_bf[:, rc * D + ocn * 512:
                                        rc * D + (ocn + 1) * 512],
                                start=(rc == 0), stop=(rc == 1)), waits=w)
                        p.v(lambda E, dm=dm, ocn=ocn: E.tensor_copy(
                            veff[:, dm * 1024 + ocn * 512:
                                 dm * 1024 + (ocn + 1) * 512],
                            bufs["pD"][:, 0:512]), waits=[mm])
                p.flush()
            with ExitStack() as e8x:
                sb8x = lambda n, sh, dt=F32: e8x.enter_context(
                    nc.sbuf_tensor(n, list(sh), dt))
                xn1 = sb8x("xn1", [128, 8 * S], BF16)
                awq_bf = sb8x("awq_bf", [128, 8 * 256], BF16)
                awk_bf = sb8x("awk_bf", [128, 8 * 256], BF16)
                lx1 = p.sdma(xn1[:], xn1_d[:], ch=1)
                l1_ = p.gdma(awq_bf[:].rearrange("p (c f) -> p c f", c=8),
                             awq[:].rearrange("(c p) f -> p c f", p=128))
                l2_ = p.gdma(awk_bf[:].rearrange("p (c f) -> p c f", c=8),
                             awk[:].rearrange("(c p) f -> p c f", p=128))
                for dst, wbf, ltok in ((Qf, awq_bf, l1_), (Kf, awk_bf, l2_)):
                    for ot in range(2):
                        for tc in range(S // 512):
                            mm = None
                            for kc in range(8):
                                w = [ltok, lx1, p.now("ve")] if kc == 0 else ()
                                mm = p.t(lambda E, kc=kc, ot=ot, tc=tc,
                                         wbf=wbf: E.matmul(
                                    bufs["pC"][:, 0:512],
                                    wbf[:, kc * 256 + ot * 128:
                                        kc * 256 + (ot + 1) * 128],
                                    xn1[:, kc * S + tc * 512:
                                        kc * S + (tc + 1) * 512],
                                    start=(kc == 0), stop=(kc == 7)), waits=w)
                            p.v(lambda E, dst=dst, ot=ot, tc=tc: E.tensor_copy(
                                dst[:, ot * S + tc * 512:
                                    ot * S + (tc + 1) * 512],
                                bufs["pC"][:, 0:512]), waits=[mm])
                for tm in range(TT):
                    for ocn in range(2):
                        mm = None
                        for kc in range(8):
                            w = [p.now("ve")] if kc == 0 else ()
                            mm = p.t(lambda E, tm=tm, ocn=ocn, kc=kc: E.matmul(
                                bufs["pC"][:, 0:512],
                                xn1[:, kc * S + tm * 128:kc * S + (tm + 1) * 128],
                                veff[:, kc * 1024 + ocn * 512:
                                     kc * 1024 + (ocn + 1) * 512],
                                start=(kc == 0), stop=(kc == 7)), waits=w)
                        p.v(lambda E, tm=tm, ocn=ocn: E.tensor_copy(
                            Vtok[:, tm * 1024 + ocn * 512:
                                 tm * 1024 + (ocn + 1) * 512],
                            bufs["pC"][:, 0:512]), waits=[mm])
                p.flush()
        logits2 = [sb8("logits_%d" % i, [128, S]) for i in range(2)]
        attnP2 = [sb8("attnP_%d" % i, [128, S], BF16) for i in range(2)]
        attnT2 = [sb8("attnT_%d" % i, [128, S], BF16) for i in range(2)]
        mtmp2 = [sb8("mtmp_%d" % i, [128, 512]) for i in range(2)]
        zb12 = [sb8("zb1_%d" % i, [128, S]) for i in range(2)]
        negmax2 = [sb8("negmax_%d" % i, [128, 1]) for i in range(2)]
        Zc2 = [sb8("Zc_%d" % i, [128, 1]) for i in range(2)]
        rZ2 = [sb8("rZ_%d" % i, [128, 1]) for i in range(2)]
        s1t2 = [sb8("s1t_%d" % i, [128, 128]) for i in range(2)]
        pT2 = [bufs["pT"], bufs["pT2"]]
        avps = [(bufs["pA"], 0), (bufs["pA"], 512), (bufs["pB"], 0),
                (bufs["pB"], 512)]
        pt8 = [dict(), dict()]
        qk_free = {0: (), 1: ()}
        av_free = {0: (), 1: (), 2: (), 3: ()}
        st_free = {0: (), 1: ()}
        ptr_free = {0: (), 1: ()}

        def emit_av(st, bi):
            ntt = st + 1
            ttoks = []
            avlast = list(pt8[bi].get("avlast", ()))
            for tt in range(ntt):
                sp = tt % 2
                w = [pt8[bi]["sc1"]] + list(ptr_free[sp])
                ttk = p.t(lambda E, tt=tt, sp=sp, bi=bi: E.transpose(
                    pT2[sp][:, 0:128], attnP2[bi][:, tt * 128:(tt + 1) * 128],
                    ident_bf[:]), waits=w)
                ev = p.v(lambda E, tt=tt, sp=sp, bi=bi: E.tensor_copy(
                    attnT2[bi][:, tt * 128:(tt + 1) * 128], pT2[sp][:, 0:128]),
                    waits=[ttk] + (avlast if tt == 0 else []))
                ptr_free[sp] = (ev,)
                ttoks.append(ev)
            pt8[bi]["attnP"] = (ttk,)
            mm = None
            for dm in range(8):
                ap = dm % 4
                ps2, off = avps[ap]
                mm = None
                for tt in range(ntt):
                    w = []
                    if tt == 0:
                        w += list(av_free[ap % 2 if False else ap])
                    if dm == 0:
                        w.append(ttoks[tt])
                    mm = p.t(lambda E, dm=dm, tt=tt, ps2=ps2, off=off,
                             ntt=ntt, bi=bi, st=st: E.matmul(
                        ps2[:, off:off + 128],
                        Vtok[:, tt * 1024 + dm * 128:tt * 1024 + (dm + 1) * 128],
                        attnT2[bi][:, tt * 128:(tt + 1) * 128],
                        start=(tt == 0), stop=(tt == ntt - 1)), waits=w)
                sp2 = dm % 2
                ev2 = p.v(lambda E, ps2=ps2, off=off, sp2=sp2: E.tensor_copy(
                    s1t2[sp2][:], ps2[:, off:off + 128]),
                    waits=[mm] + list(st_free[sp2]))
                av_free[ap] = (ev2,)
                so = p.sdma(s1_d[dm * 128:(dm + 1) * 128,
                                 st * 128:(st + 1) * 128],
                            s1t2[sp2][:], waits=[ev2], ch=3)
                st_free[sp2] = (so,)
            pt8[bi]["avlast"] = (mm,)

        pending = None
        for st in range(TT):
            bi = st % 2
            L = st * 128 + 128
            ncc = (L + 511) // 512
            evl = None
            for c in range(ncc):
                cp = c % 2
                ps2 = bufs["pC"] if cp == 0 else bufs["pD"]
                mm = None
                for ot in range(2):
                    w = list(qk_free[cp]) if ot == 0 else ()
                    mm = p.t(lambda E, st=st, c=c, ot=ot, ps2=ps2: E.matmul(
                        ps2[:, 0:512],
                        Qf[:, ot * S + st * 128:ot * S + (st + 1) * 128],
                        Kf[:, ot * S + c * 512:ot * S + (c + 1) * 512],
                        start=(ot == 0), stop=(ot == 1)), waits=w)
                evl = p.s(lambda E, c=c, ps2=ps2, bi=bi: E.activation(
                    logits2[bi][:, c * 512:(c + 1) * 512], ps2[:, 0:512],
                    AF.Identity, scale=0.0625),
                    waits=[mm] + (list(pt8[bi].get("log", ()))
                                  if c == 0 else []))
                qk_free[cp] = (evl,)
            cd = ncc - 1
            a1 = p.g(lambda E, st=st, cd=cd, bi=bi: E.affine_select(
                mtmp2[bi][:], logits2[bi][:, cd * 512:(cd + 1) * 512],
                [[-1, 512]], ALU.is_ge, NEG_FILL, base=st * 128 - cd * 512,
                channel_multiplier=1),
                waits=[evl] + list(pt8[bi].get("mtmp", ())))
            a2 = p.v(lambda E, cd=cd, bi=bi: E.tensor_copy(
                logits2[bi][:, cd * 512:(cd + 1) * 512], mtmp2[bi][:]),
                waits=[a1])
            pt8[bi]["mtmp"] = (a2,)
            Lr = cd * 512 + 512
            z1 = p.v(lambda E, Lr=Lr, bi=bi: E.tensor_scalar(
                zb12[bi][:, 0:Lr], logits2[bi][:, 0:Lr], 0.0, None,
                ALU.is_equal),
                waits=[a2, evl] + list(pt8[bi].get("zb1", ())))
            z2 = p.v(lambda E, Lr=Lr, bi=bi: E.scalar_tensor_tensor(
                logits2[bi][:, 0:Lr], zb12[bi][:, 0:Lr], NEG_FILL,
                logits2[bi][:, 0:Lr], ALU.mult, ALU.add), waits=[z1])
            pt8[bi]["zb1"] = (z2,)
            r1 = p.v(lambda E, Lr=Lr, bi=bi: E.tensor_reduce(
                negmax2[bi][:], logits2[bi][:, 0:Lr], AXX, ALU.max,
                negate=True), waits=[z2])
            ex = p.s(lambda E, Lr=Lr, bi=bi: E.activation(
                attnP2[bi][:, 0:Lr], logits2[bi][:, 0:Lr], AF.Exp,
                bias=negmax2[bi][:, 0:1], accum_out=Zc2[bi][:]),
                waits=[r1] + list(pt8[bi].get("attnP", ())))
            pt8[bi]["log"] = (ex,)
            rc1 = p.v(lambda E, bi=bi: E.reciprocal(rZ2[bi][:], Zc2[bi][:]),
                      waits=[ex])
            sc1 = p.v(lambda E, Lr=Lr, bi=bi: E.tensor_scalar(
                attnP2[bi][:, 0:Lr], attnP2[bi][:, 0:Lr], rZ2[bi][:, 0:1],
                None, ALU.mult), waits=[rc1])
            pt8[bi]["sc1"] = sc1
            if pending is not None:
                emit_av(*pending)
            pending = (st, bi)
        emit_av(*pending)
        p.wait_on("vector", p.alldma())
        p.flush()

    pre_ps.close()
    SCL = float(1.0 / np.sqrt(128.0))
    with ExitStack() as e6:
        sb6 = lambda n, sh, dt=F32: e6.enter_context(
            nc.sbuf_tensor(n, list(sh), dt))
        bufs["pSc"] = e6.enter_context(nc.psum_tensor("pSc", [8, 512], F32))
        pE = e6.enter_context(nc.psum_tensor("pE", [128, 512], F32))
        xn1 = sb6("xn1r", [128, 8 * S], BF16)
        rwq_bf = sb6("rwq_bf", [128, 8 * D], BF16)
        rwk_bf = sb6("rwk_bf", [128, 8 * D], BF16)
        rbq_sb = sb6("rbq_sb", [128, 8])
        rbk_sb = sb6("rbk_sb", [128, 8])
        ewq_bf = sb6("ewq_bf", [128, 8 * 128], BF16)
        ewk_bf = sb6("ewk_bf", [128, 8 * 128], BF16)
        ebqs = sb6("ebqs", [128, 8])
        ebk_sb = sb6("ebk_sb", [128, 8])
        qh = sb6("qh", [128, S], BF16)
        kh = sb6("kh", [128, S], BF16)
        qe = sb6("qe", [128, S], BF16)
        ke = sb6("ke", [128, S], BF16)
        expL = sb6("expL", [128, S], BF16)
        Zcol = sb6("Zcol", [128, 1])
        wcol = sb6("wcol", [128, 1])
        ones8 = sb6("ones8", [128, 8])
        scores_sb = sb6("scores_sb", [8, 8 * 512])
        m8 = scores_sb
        mx8 = sb6("mx8", [8, 512])
        tak8 = sb6("tak8", [8, 512])
        eq8 = sb6("eq8", [8, 512])
        ngr = max(1, TC // 2)
        gw = min(1024, S)
        lx1 = p.sdma(xn1[:], xn1_d[:], ch=1)
        lq = p.gdma(rwq_bf[:].rearrange("p (c f) -> p c f", c=8),
                    rwqT[:].rearrange("(c p) f -> p c f", p=128))
        lk = p.gdma(rwk_bf[:].rearrange("p (c f) -> p c f", c=8),
                    rwkT[:].rearrange("(c p) f -> p c f", p=128))
        leq = p.gdma(ewq_bf[:], ewqT[:], ch=1)
        lek = p.gdma(ewk_bf[:], ewkT[:], ch=1)
        lb1 = p.sdma(rbq_sb[:], rbqc[:], ch=2)
        lb2 = p.sdma(rbk_sb[:], rbkc[:], ch=2)
        lb3 = p.sdma(ebqs[:], ebqc[:], ch=2)
        lb4 = p.sdma(ebk_sb[:], ebkc[:], ch=2)
        w0 = p.v(lambda E: E.memset(ones8[:], 1.0))

        w2 = p.v(lambda E: E.memset(scores_sb[:], 0.0))

        expL2 = [expL, sb6("expL_b", [128, S], BF16),
                 sb6("expL_c", [128, S], BF16)]
        zpart2 = [sb6("zpart_%d" % i, [128, 4]) for i in range(2)]
        Zcol2 = [Zcol, sb6("Zcol_b", [128, 1])]
        wcol2 = [wcol, sb6("wcol_b", [128, 1])]
        wledAB = sb6("wledAB", [128, 128], BF16)
        wled2 = [wledAB[:, 0:64], wledAB[:, 64:128]]
        w1b = p.v(lambda E: E.memset(wledAB[:], 0.0))
        pairs3 = [bufs["pA"], bufs["pB"], bufs["pCD"]]
        bank_tok = {0: (), 1: (), 2: ()}
        bcnt = 0
        zp_tok = {0: (), 1: ()}
        mv_tok3 = {0: (), 1: (), 2: ()}
        mv_tokw = {0: (), 1: ()}
        prev_ea = ()
        pendq = []
        gcnt = 0

        def emit_matvecs(pend_info, prev_ea):
            ei, bi, wl_tok, e_, first, laststl, h_ = pend_info
            tok = None
            for c in range(TC):
                w = ([wl_tok] + list(prev_ea)) if c == 0 else ()
                tok = p.t(lambda E, c=c, ei=ei, bi=bi, first=first,
                          laststl=laststl: E.matmul(
                    bufs["pSc"][0:8, 0:512],
                    wled2[bi][:, c * 8:(c + 1) * 8],
                    expL2[ei][:, c * 512:(c + 1) * 512],
                    start=(first and c == 0),
                    stop=(laststl and c == TC - 1)), waits=w)
            mv_tok3[ei] = (tok,)
            mv_tokw[bi] = (tok,)
            if laststl:
                if h_ == 0:
                    ea = p.v(lambda E, e_=e_: E.tensor_copy(
                        scores_sb[0:8, e_ * 512:(e_ + 1) * 512],
                        bufs["pSc"][0:8, 0:512]), waits=[tok])
                else:
                    ea = p.v(lambda E, e_=e_: E.tensor_tensor(
                        scores_sb[0:8, e_ * 512:(e_ + 1) * 512],
                        scores_sb[0:8, e_ * 512:(e_ + 1) * 512],
                        bufs["pSc"][0:8, 0:512], ALU.add), waits=[tok])
                return (ea,)
            return prev_ea

        pc_tok = ()
        pd_tok = ()
        qh2 = [qh, qh]
        kh2 = [kh, kh]
        qe2 = [qe, sb6("qe_b", [128, S], BF16)]
        ke2 = [ke, sb6("ke_b", [128, S], BF16)]
        state = {"pe": ()}
        qh_done = {}
        qe_ready = {}
        qe_lastread = [(), ()]
        qh_lastread = [(), ()]

        def qhkh_steps(h):
            hb = h % 2
            out = []
            for tc in range(TC):
                for key in ("q", "k"):
                    def step(key=key, tc=tc, h=h, hb=hb):
                        dst = qh2[hb] if key == "q" else kh2[hb]
                        wbf = rwq_bf if key == "q" else rwk_bf
                        bias_sb = rbq_sb if key == "q" else rbk_sb
                        ltok = lq if key == "q" else lk
                        mm = None
                        for kc in range(8):
                            w = ([ltok, lx1, lb1, lb2] + list(state["pe"]) +
                                 list(qh_lastread[hb])) if kc == 0 else ()
                            mm = p.t(lambda E, kc=kc, tc=tc, h=h, wbf=wbf:
                                     E.matmul(
                                pE[:, 0:512],
                                wbf[:, kc * D + h * 128:
                                    kc * D + (h + 1) * 128],
                                xn1[:, kc * S + tc * 512:
                                    kc * S + (tc + 1) * 512],
                                start=(kc == 0), stop=(kc == 7)), waits=w)
                        ev = p.v(lambda E, dst=dst, tc=tc, h=h,
                                 bias_sb=bias_sb: E.tensor_scalar(
                            dst[:, tc * 512:(tc + 1) * 512], pE[:, 0:512],
                            bias_sb[:, h:h + 1], None, ALU.add), waits=[mm])
                        state["pe"] = (ev,)
                        cur = qh_done.get(h, ())
                        qh_done[h] = (cur + (ev,))[-2:]
                    out.append(step)
            return out

        def qeke_steps(h, e):
            hb = h % 2
            be = e % 2
            out = []
            for tc in range(TC):
                def stepQ(tc=tc, h=h, e=e, hb=hb, be=be):
                    qw = list(qh_done.get(h, ()))
                    mq = p.t(lambda E, e=e, tc=tc, hb=hb: E.matmul(
                        pE[:, 0:512],
                        ewq_bf[:, e * 128:(e + 1) * 128],
                        qh2[hb][:, tc * 512:(tc + 1) * 512],
                        start=True, stop=True),
                        waits=[leq] + qw + list(state["pe"]) +
                        list(qe_lastread[be]))
                    evq = p.v(lambda E, e=e, tc=tc, be=be: E.tensor_scalar(
                        qe2[be][:, tc * 512:(tc + 1) * 512],
                        pE[:, 0:512],
                        ebqs[:, e:e + 1], SCL, ALU.add, ALU.mult),
                        waits=[mq, lb3])
                    state["pe"] = (evq,)
                    qh_lastread[hb] = qh_lastread[hb] + (mq,)
                    qe_ready[(h, e)] = (evq,)

                def stepK(tc=tc, h=h, e=e, hb=hb, be=be):
                    qw = list(qh_done.get(h, ()))
                    mk = p.t(lambda E, e=e, tc=tc, hb=hb: E.matmul(
                        pE[:, 0:512],
                        ewk_bf[:, e * 128:(e + 1) * 128],
                        kh2[hb][:, tc * 512:(tc + 1) * 512],
                        start=True, stop=True),
                        waits=[lek] + qw + list(state["pe"]) +
                        list(qe_lastread[be]))
                    evk = p.v(lambda E, e=e, tc=tc, be=be: E.tensor_scalar(
                        ke2[be][:, tc * 512:(tc + 1) * 512],
                        pE[:, 0:512],
                        ebk_sb[:, e:e + 1], None, ALU.add), waits=[mk, lb4])
                    state["pe"] = (evk,)
                    qh_lastread[hb] = (mk,)
                    qe_ready[(h, e)] = qe_ready.get((h, e), ()) + (evk,)
                out.append(stepQ)
                out.append(stepK)
            return out

        bg = []
        for f in qhkh_steps(0):
            f()
        for f in qeke_steps(0, 0):
            f()
        for h in range(8):
            for e in range(8):
                be = e % 2
                if e < 7:
                    bg += qeke_steps(h, e + 1)
                if e == 6 and h < 7:
                    bg += qhkh_steps(h + 1)
                if e == 7 and h < 7:
                    bg += qeke_steps(h + 1, 0)
                lg_last = None
                for stl in range(TT):
                    bi = gcnt % 2
                    ei = gcnt % 3
                    gcnt += 1
                    exp_toks = []
                    for g in range(ngr):
                        gb = bcnt % 3
                        bcnt += 1
                        psg = pairs3[gb]
                        nsub = min(2, TC - g * 2)
                        mm = None
                        for sub in range(nsub):
                            w = list(bank_tok[gb]) if sub == 0 else []
                            if g == 0 and sub == 0 and stl == 0:
                                w = w + list(qe_ready[(h, e)])
                            mm = p.t(lambda E, stl=stl, g=g, sub=sub, psg=psg,
                                     be=be: E.matmul(
                                psg[:, sub * 512:(sub + 1) * 512],
                                qe2[be][:, stl * 128:(stl + 1) * 128],
                                ke2[be][:, (g * 2 + sub) * 512:
                                   (g * 2 + sub + 1) * 512],
                                start=True, stop=True), waits=w)
                        lg_last = mm
                        ew = [mm]
                        if g == 0:
                            ew += list(zp_tok[bi]) + list(mv_tok3[ei])
                        if g == 0:
                            ew += list(zp_tok[bi])
                        et = p.s(lambda E, g=g, psg=psg, nsub=nsub, ei=ei,
                                 bi=bi: E.activation(
                            expL2[ei][:, g * 1024:g * 1024 + nsub * 512],
                            psg[:, 0:nsub * 512], AF.Exp,
                            accum_out=zpart2[bi][:, g:g + 1]), waits=ew)
                        bank_tok[gb] = (et,)
                        exp_toks.append(et)
                        if g == min(1, ngr - 1) and len(pendq) >= 2:
                            prev_ea = emit_matvecs(pendq.pop(0), prev_ea)
                    zz = p.v(lambda E, bi=bi: E.tensor_reduce(
                        Zcol2[bi][:], zpart2[bi][:, 0:ngr], AXX, ALU.add),
                        waits=exp_toks)
                    zp_tok[bi] = (zz,)
                    zr = p.v(lambda E, bi=bi: E.reciprocal(
                        wcol2[bi][:], Zcol2[bi][:]), waits=[zz])
                    wl = p.v(lambda E, bi=bi: E.tensor_scalar(
                        bass.AP(wledAB, bi * 64, [[128, 128], [9, TC]]),
                        ones8[:, 0:TC], wcol2[bi][:, 0:1], None, ALU.mult),
                        waits=[zr, w0, w1b] + list(mv_tokw[bi]))
                    pendq.append((ei, bi, wl, e, stl == 0, stl == TT - 1, h))
                    if bg:
                        bg.pop(0)()
                qe_lastread[be] = (lg_last,)
        while pendq:
            prev_ea = emit_matvecs(pendq.pop(0), prev_ea)
        tvv = p.now("ve")
        sl = lambda t, e: t[0:TC, e * 512:(e + 1) * 512]
        mx1 = p.v(lambda E: E.tensor_copy(mx8[0:TC, :], sl(scores_sb, 0)),
                  waits=[tvv])
        for e in range(1, 8):
            mx1 = p.v(lambda E, e=e: E.tensor_tensor(
                mx8[0:TC, :], mx8[0:TC, :], sl(scores_sb, e), ALU.max),
                waits=[mx1])
        prev = p.v(lambda E: E.memset(tak8[:], 0.0), waits=[mx1])
        for e in range(8):
            q1 = p.v(lambda E, e=e: E.tensor_tensor(
                eq8[0:TC, :], sl(scores_sb, e), mx8[0:TC, :], ALU.is_equal),
                waits=[prev])
            q4 = p.v(lambda E, e=e: E.tensor_scalar(
                sl(m8, e), tak8[0:TC, :], -1.0, 1.0, ALU.mult, ALU.add),
                waits=[q1])
            q5 = p.v(lambda E, e=e: E.tensor_tensor(
                sl(m8, e), sl(m8, e), eq8[0:TC, :], ALU.mult), waits=[q4])
            prev = p.v(lambda E, e=e: E.tensor_tensor(
                tak8[0:TC, :], tak8[0:TC, :], sl(m8, e), ALU.add), waits=[q5])
        de = []
        for e in range(8):
            de.append(p.gdma(
                m_d[e:e + 1, :].rearrange("o (c s) -> (o c) s", c=TC),
                m8[0:TC, e * 512:(e + 1) * 512], waits=[prev], ch=2))
        p.wait_on("vector", [de[-1]])
        p.flush()

    with ExitStack() as e9:
        sb9 = lambda n, sh, dt=F32: e9.enter_context(
            nc.sbuf_tensor(n, list(sh), dt))
        m_fm = sb9("m_fm", [8, S], BF16)
        lmf = p.gdma(m_fm[:], m_d[:], ch=3)
        wexp_bf = [sb9("wexp%d" % e, [128, 8 * 512], BF16) for e in range(8)]
        bexp_bf = sb9("bexp_bf", [8, D], BF16)
        xn1sl = [sb9("xn1sl%d" % i, [128, 8 * 512], BF16) for i in range(2)]
        mrow_t = sb9("mrow_t", [1, 8 * 512])
        maskb = sb9("maskb", [128, 512], BF16)
        xnm = sb9("xnm", [128, 8 * 512], BF16)
        g32b = sb9("g32b", [128, 512])
        s1tt = sb9("s1tt", [128, 512])
        x1tt = sb9("x1tt", [128, 512])
        x2t = sb9("x2t", [128, 512])
        lbx = p.gdma(bexp_bf[:], bexp[:], ch=2)
        for grp in range(2):
            lwe = []
            for e in range(8):
                lwe.append(p.gdma(
                    wexp_bf[e][:].rearrange("p (c f) -> p c f", c=8),
                    wexp[e][:, grp * 512:(grp + 1) * 512].rearrange(
                        "(c p) f -> p c f", p=128),
                    waits=[p.now("te")] if e == 0 else ()))
            for tc in range(TC):
                bb = tc % 2
                lxn = p.sdma(
                    xn1sl[bb][:].rearrange("p (c s) -> p c s", c=8),
                    xn1_d[:].rearrange("p (c s) -> p c s", c=8)[
                        :, :, tc * 512:(tc + 1) * 512],
                    waits=[p.now("te")], ch=1)
                lmr = p.gdma(
                    mrow_t[0:1, :].rearrange("o (e s) -> o e s", e=8),
                    m_d[:, tc * 512:(tc + 1) * 512].rearrange(
                        "e s -> z e s", z=1) if False else
                    bass.AP(m_d, tc * 512,
                            [[S, 1], [S, 8], [1, 512]]),
                    waits=[p.now("te")], ch=2)
                for e in range(8):
                    bc = p.t(lambda E, e=e, tc=tc: E.matmul(
                        bufs["pD"][:, 0:512], bufs["oner"][:],
                        mrow_t[0:1, e * 512:(e + 1) * 512],
                        start=True, stop=True),
                        waits=[lmr, p.now("ve"), p.now("sc")])
                    bv = p.v(lambda E: E.tensor_copy(maskb[:],
                                                     bufs["pD"][:, 0:512]),
                             waits=[bc])
                    xms = []
                    for kc in range(8):
                        xms.append(p.v(lambda E, kc=kc, bb=bb: E.tensor_tensor(
                            xnm[:, kc * 512:(kc + 1) * 512],
                            xn1sl[bb][:, kc * 512:(kc + 1) * 512],
                            maskb[:], ALU.mult),
                            waits=[bv, lxn, p.now("te")]))
                    for ot in range(4):
                        psg = bufs["pA"] if ot < 2 else bufs["pB"]
                        off = (ot % 2) * 512
                        for kc in range(8):
                            w = [xms[-1], lwe[e], p.now("sc")] \
                                if kc == 0 else ()
                            p.t(lambda E, e=e, ot=ot, kc=kc, psg=psg, off=off:
                                E.matmul(
                                psg[:, off:off + 512],
                                wexp_bf[e][:, kc * 512 + ot * 128:
                                            kc * 512 + (ot + 1) * 128],
                                xnm[:, kc * 512:(kc + 1) * 512],
                                start=(e == 0 and kc == 0), stop=False),
                                waits=w)
                for ot in range(4):
                    psg = bufs["pA"] if ot < 2 else bufs["pB"]
                    off = (ot % 2) * 512
                    mb = p.t(lambda E, ot=ot, psg=psg, off=off, grp=grp, tc=tc:
                             E.matmul(
                        psg[:, off:off + 512],
                        bexp_bf[0:8, grp * 512 + ot * 128:
                                grp * 512 + (ot + 1) * 128],
                        m_fm[0:8, tc * 512:(tc + 1) * 512],
                        start=False, stop=True), waits=[lbx, lmf])
                    gv = p.s(lambda E, psg=psg, off=off: E.activation(
                        g32b[:], psg[:, off:off + 512], AF.Silu),
                        waits=[mb, p.now("ve")])
                    rr = grp * 512 + ot * 128
                    ls = p.sdma(s1tt[:], s1_d[rr:rr + 128,
                                              tc * 512:(tc + 1) * 512],
                                waits=[p.now("ve")], ch=2)
                    lx = p.sdma(x1tt[:], x1_d[rr:rr + 128,
                                              tc * 512:(tc + 1) * 512],
                                waits=[p.now("ve")], ch=3)
                    v1 = p.v(lambda E: E.tensor_tensor(x2t[:], s1tt[:],
                                                       g32b[:], ALU.mult),
                             waits=[gv, ls, lx, p.now("sd0")])
                    v2 = p.v(lambda E: E.tensor_tensor(x2t[:], x2t[:],
                                                       x1tt[:], ALU.add),
                             waits=[v1])
                    so = p.sdma(o_x2[rr:rr + 128, tc * 512:(tc + 1) * 512],
                                x2t[:], waits=[v2], ch=0)
                    p.wait_on("tensor", [v2])
        p.wait_on("sync", p.alldma())
        p.flush()
    es.close()
    return nc


def _host_prep(inputs):
    T = lambda a: np.ascontiguousarray(np.asarray(a, np.float32).T)
    st = {}
    x = np.asarray(inputs["L0_attn_x"], np.float32)
    st["x"] = [np.ascontiguousarray(x[b].T) for b in range(x.shape[0])]
    st["wkT"] = T(inputs["L0_attn_Wk"])
    st["wvT"] = T(inputs["L0_attn_Wv"])
    st["wqT"] = T(inputs["L0_attn_Wq"])
    st["bkc"] = _col(inputs["L0_attn_bk"])
    st["bvc"] = _col(inputs["L0_attn_bv"])
    st["bqc"] = _col(inputs["L0_attn_bq"])
    st["fpad"], st["cmat"] = _dft_consts()
    st["ident"] = np.eye(128, dtype=np.float32).astype(BF)
    st["wup"] = np.asarray(inputs["L0_ffn_w_up"], np.float32)
    st["wgate"] = np.asarray(inputs["L0_ffn_w_gate"], np.float32)
    st["wdown"] = np.asarray(inputs["L0_ffn_w_down"], np.float32)
    st["awq"] = np.asarray(inputs["L1_attn_Wq"], np.float32)
    st["awk"] = np.asarray(inputs["L1_attn_Wk"], np.float32)
    st["wvdT"] = T(inputs["L1_attn_Wvdown"])
    st["wvup"] = np.asarray(inputs["L1_attn_Wvup"], np.float32)
    st["rwqT"] = T(inputs["L1_ffn_Wq"])
    st["rwkT"] = T(inputs["L1_ffn_Wk"])
    st["rbqc"] = _col(inputs["L1_ffn_bq"])
    st["rbkc"] = _col(inputs["L1_ffn_bk"])
    st["ewqT"] = np.ascontiguousarray(
        np.asarray(inputs["L1_ffn_eWq"], np.float32).transpose(2, 0, 1)
        .reshape(128, 8 * 128))
    st["ewkT"] = np.ascontiguousarray(
        np.asarray(inputs["L1_ffn_eWk"], np.float32).transpose(2, 0, 1)
        .reshape(128, 8 * 128))
    st["ebqc"] = np.ascontiguousarray(
        np.asarray(inputs["L1_ffn_ebq"], np.float32).T)
    st["ebkc"] = np.ascontiguousarray(
        np.asarray(inputs["L1_ffn_ebk"], np.float32).T)
    st["wexp"] = np.asarray(inputs["L1_ffn_Wexp"], np.float32)
    st["bexp"] = np.asarray(inputs["L1_ffn_bexp"], np.float32)
    return st


def kernel(**inputs):
    st = _host_prep(inputs)
    n = len(st["x"])
    S = st["x"][0].shape[1]
    shared_A = {k: st[k] for k in ("wkT", "wvT", "bkc", "bvc", "fpad")}
    rA = run_bass_kernel_spmd(
        build_A(S, n), [dict(shared_A, x=st["x"][b]) for b in range(n)],
        core_ids=list(range(n)))
    ab = [rA.results[b]["o_ab"] for b in range(n)]
    sh = [rA.results[b]["o_shat"] for b in range(n)]
    mxn = np.concatenate([rA.results[b]["o_mxn"] for b in range(n)], axis=1)
    shared_B = {k: st[k] for k in ("wup", "wgate", "ident")}
    rB = run_bass_kernel_spmd(
        build_ffn_stage("B", S, n),
        [dict(shared_B, x=st["x"][b], ab=ab[b], mxn=mxn) for b in range(n)],
        core_ids=list(range(n)))
    mup = np.concatenate([rB.results[b]["o_max"][:, 0:1] for b in range(n)],
                         axis=1)
    mgate = np.concatenate([rB.results[b]["o_max"][:, 1:2] for b in range(n)],
                           axis=1)
    rC = run_bass_kernel_spmd(
        build_ffn_stage("C", S, n),
        [dict(shared_B, x=st["x"][b], ab=ab[b], mxn=mxn, mgate=mgate)
         for b in range(n)], core_ids=list(range(n)))
    mg2 = np.concatenate([rC.results[b]["o_max"] for b in range(n)], axis=1)
    rD = run_bass_kernel_spmd(
        build_ffn_stage("D", S, n),
        [dict(shared_B, x=st["x"][b], ab=ab[b], mxn=mxn, mgate=mgate,
              mup=mup, mg2=mg2) for b in range(n)], core_ids=list(range(n)))
    mo = np.concatenate([rD.results[b]["o_max"] for b in range(n)], axis=1)
    shared_E = {k: st[k] for k in (
        "wqT", "bqc", "fpad", "cmat", "ident", "wup", "wgate", "wdown",
        "awq", "awk", "wvdT", "wvup", "rwqT", "rwkT", "rbqc", "rbkc",
        "ewqT", "ewkT", "ebqc", "ebkc", "wexp", "bexp")}
    rE = run_bass_kernel_spmd(
        build_E(S, n),
        [dict(shared_E, x=st["x"][b], ab=ab[b], shat=sh[b], mxn=mxn,
              mgate=mgate, mup=mup, mg2=mg2, mo=mo) for b in range(n)],
        core_ids=list(range(n)))
    out = np.stack([np.asarray(rE.results[b]["o_x2"]).T for b in range(n)])
    return np.ascontiguousarray(out.astype(np.float32))


# version 26 (final)
# speedup vs baseline: 1.0431x; 1.0141x over previous
import numpy as np
import ml_dtypes
from contextlib import ExitStack
import concourse.bass as bass
import concourse.mybir as mybir
from concourse.bass_utils import run_bass_kernel_spmd

F32 = mybir.dt.float32
BF16 = mybir.dt.bfloat16
AF = mybir.ActivationFunctionType
ALU = mybir.AluOpType
AXX = mybir.AxisListType.X
MAGIC = 12582912.0
NEG_FILL = float(np.finfo(np.float32).min)
D = 1024
NF = 640
BF = ml_dtypes.bfloat16

ENGS = ("sync", "tensor", "scalar", "vector", "gpsimd")


def _dft_consts():
    f = np.arange(D // 2 + 1, dtype=np.float64)
    d = np.arange(D, dtype=np.float64)
    ang = 2.0 * np.pi * np.outer(d, f) / D
    Fpad = np.zeros((D, 2 * NF), np.float64)
    Fpad[:, 0:D // 2 + 1] = np.cos(ang)
    Fpad[:, NF:NF + D // 2 + 1] = -np.sin(ang)
    w = np.full(D // 2 + 1, 2.0)
    w[0] = 1.0
    w[-1] = 1.0
    ang2 = 2.0 * np.pi * np.outer(f, d) / D
    C = np.zeros((NF, 2 * D), np.float64)
    C[0:D // 2 + 1, 0:D] = (w[:, None] / D) * np.cos(ang2)
    C[0:D // 2 + 1, D:2 * D] = -(w[:, None] / D) * np.sin(ang2)
    return np.ascontiguousarray(Fpad.astype(BF)), np.ascontiguousarray(C.astype(BF))


def _col(v):
    v = np.asarray(v, np.float32)
    return np.ascontiguousarray(v.reshape(-1, 128).T)


class Prog:

    NSD = 4
    NGD = 4

    def __init__(self, nc, es):
        self.nc = nc
        names = ["te", "sc", "ve", "gp", "sy", "cc"]
        names += ["sd%d" % i for i in range(self.NSD)]
        names += ["gd%d" % i for i in range(self.NGD)]
        self.sems = {n: es.enter_context(nc.semaphore("S_" + n)) for n in names}
        self.cnt = {n: 0 for n in names}
        self.ops = {e: [] for e in ENGS}
        self.lastwait = {e: {n: 0 for n in names} for e in ENGS}
        self.chan_cons = {n: [] for n in names if n[:2] in ("sd", "gd") or n == "cc"}
        self.engsem = {"sync": "sy", "tensor": "te", "scalar": "sc",
                       "vector": "ve", "gpsimd": "gp"}

    def _sched(self, eng, waits, emit, inc_sem, inc_by):
        ws = []
        chans_waited = []
        seen = {}
        for (sn, v) in waits:
            if sn in self.chan_cons:
                v = self.cnt[sn]
                chans_waited.append(sn)
            seen[sn] = max(seen.get(sn, 0), v)
        for sn, v in seen.items():
            if v > self.lastwait[eng][sn]:
                ws.append((sn, v))
                self.lastwait[eng][sn] = v
        tok = None
        if inc_sem is not None:
            self.cnt[inc_sem] += inc_by
            tok = (inc_sem, self.cnt[inc_sem])
        my_tok = tok if tok is not None else \
            (self.engsem[eng], self.cnt[self.engsem[eng]])
        for sn in chans_waited:
            if my_tok[1] > 0:
                self.chan_cons[sn].append(my_tok)
        sems = self.sems

        def f(E, ws=tuple(ws), emit=emit, inc=inc_sem, by=inc_by):
            for (sn, v) in ws:
                E.wait_ge(sems[sn], v)
            ins = emit(E)
            if inc is not None and ins is not None:
                ins.then_inc(sems[inc], by)
        self.ops[eng].append(f)
        return tok

    def _dma(self, eng, sem, dst, src, waits):
        gate = self.chan_cons[sem]
        self.chan_cons[sem] = []
        return self._sched(eng, list(waits) + gate,
                           lambda E: E.dma_start(dst, src), sem, 16)

    def sdma(self, dst, src, waits=(), ch=0):
        return self._dma("sync", "sd%d" % ch, dst, src, waits)

    def gdma(self, dst, src, waits=(), ch=0):
        return self._dma("gpsimd", "gd%d" % ch, dst, src, waits)

    def t(self, emit, waits=()):
        return self._sched("tensor", waits, emit, "te", 1)

    def s(self, emit, waits=()):
        return self._sched("scalar", waits, emit, "sc", 1)

    def v(self, emit, waits=()):
        return self._sched("vector", waits, emit, "ve", 1)

    def g(self, emit, waits=()):
        return self._sched("gpsimd", waits, emit, "gp", 1)

    def wait_on(self, eng, waits):
        self._sched(eng, waits, lambda E: None, None, 0)

    def collective_max(self, cin, cout, n_cores, waits=()):
        rg = [list(range(n_cores))]
        gate = self.chan_cons["cc"]
        self.chan_cons["cc"] = []

        def emit(E):
            return E.collective_compute("AllReduce", ALU.max, rg, [cin], [cout])
        return self._sched("gpsimd", list(waits) + gate, emit, "cc", 16)

    def now(self, sem):
        return (sem, self.cnt[sem])

    def alldma(self):
        out = [("sd%d" % i, self.cnt["sd%d" % i]) for i in range(self.NSD)]
        out += [("gd%d" % i, self.cnt["gd%d" % i]) for i in range(self.NGD)]
        return [t for t in out if t[1] > 0]

    def flush(self):
        if not any(self.ops[e] for e in ENGS):
            return
        with self.nc.Block() as block:
            for eng in ENGS:
                if self.ops[eng]:
                    getattr(block, eng)(
                        lambda E, fs=tuple(self.ops[eng]): [f(E) for f in fs])
        self.ops = {e: [] for e in ENGS}
        for e in ENGS:
            for n in ("te", "sc", "ve", "gp", "sy"):
                self.lastwait[e][n] = self.cnt[n]


def em_ln_stats(p, bufs, x_hbm, S):
    TC = S // 512
    xt2, sq = bufs["xt2"], bufs["sq"]
    onec = bufs["onec"]
    mu_row, sq_row = bufs["mu_row"], bufs["sq_row"]
    a_row = bufs["a_row"]
    pC, pD = bufs["pC"], bufs["pD"]
    cons = {0: (), 1: ()}
    for tc in range(TC):
        for dc in range(8):
            h = dc % 2
            buf = xt2[:, h * 512:h * 512 + 512]
            ld = p.sdma(buf, x_hbm[dc * 128:(dc + 1) * 128,
                                   tc * 512:(tc + 1) * 512], waits=cons[h],
                        ch=1 + h)
            m1 = p.t(lambda E, b=buf, dc=dc: E.matmul(
                pC[0:1, 0:512], onec[:], b, start=(dc == 0), stop=(dc == 7)),
                waits=[ld])
            sv = p.v(lambda E, b=buf: E.tensor_tensor(sq[:], b, b, ALU.mult),
                     waits=[ld, p.now("te")])
            m2 = p.t(lambda E, dc=dc: E.matmul(
                pD[0:1, 0:512], onec[:], sq[:], start=(dc == 0), stop=(dc == 7)),
                waits=[sv])
            cons[h] = (m1, sv)
        tmm = p.now("te")
        e1 = p.v(lambda E, tc=tc: E.tensor_copy(
            mu_row[0:1, tc * 512:(tc + 1) * 512], pC[0:1, 0:512]), waits=[tmm])
        e2 = p.v(lambda E, tc=tc: E.tensor_copy(
            sq_row[0:1, tc * 512:(tc + 1) * 512], pD[0:1, 0:512]), waits=[tmm])
        p.wait_on("tensor", [e1, e2])
        last_ev = (e1, e2)
    t1 = p.v(lambda E: E.tensor_scalar(mu_row[:], mu_row[:], 1.0 / D, None,
                                       ALU.mult), waits=list(last_ev))
    t2 = p.v(lambda E: E.tensor_scalar(sq_row[:], sq_row[:], 1.0 / D, None,
                                       ALU.mult), waits=[t1])
    t3 = p.v(lambda E: E.scalar_tensor_tensor(a_row[:], mu_row[:], 1.0,
                                              mu_row[:], ALU.mult, ALU.mult),
             waits=[t2])
    t4 = p.v(lambda E: E.tensor_tensor(sq_row[:], sq_row[:], a_row[:],
                                       ALU.subtract), waits=[t3])
    te = p.v(lambda E: E.memset(bufs["eps1"][:], 1e-5))
    t5 = p.s(lambda E: E.activation(sq_row[:], sq_row[:], AF.Ln,
                                    bias=bufs["eps1"][0:1, 0:1]), waits=[t4, te])
    t6 = p.s(lambda E: E.activation(a_row[:], sq_row[:], AF.Exp, scale=-0.5),
             waits=[t5])
    t7 = p.v(lambda E: E.scalar_tensor_tensor(bufs["b_row"][:], mu_row[:],
                                              -1.0, a_row[:], ALU.mult,
                                              ALU.mult), waits=[t6])
    return t7


def em_ln_bcast(p, bufs, S, waits=()):
    a_row, b_row = bufs["a_row"], bufs["b_row"]
    a_bc, b_bc = bufs["a_bc"], bufs["b_bc"]
    oner = bufs["oner"]
    pC, pD = bufs["pC"], bufs["pD"]
    last = None
    for tc in range(S // 512):
        ta = p.t(lambda E, tc=tc: E.matmul(
            pC[:, 0:512], oner[:], a_row[0:1, tc * 512:(tc + 1) * 512],
            start=True, stop=True), waits=list(waits) + [p.now("ve")])
        va = p.v(lambda E, tc=tc: E.tensor_copy(
            a_bc[:, tc * 512:(tc + 1) * 512], pC[:, 0:512]), waits=[ta])
        tb = p.t(lambda E, tc=tc: E.matmul(
            pD[:, 0:512], oner[:], b_row[0:1, tc * 512:(tc + 1) * 512],
            start=True, stop=True), waits=waits)
        vb = p.v(lambda E, tc=tc: E.tensor_copy(
            b_bc[:, tc * 512:(tc + 1) * 512], pD[:, 0:512]), waits=[tb])
        p.wait_on("tensor", [va, vb])
        last = vb
    return last


def em_xn(p, bufs, x_hbm, S, xn_bf=None, xnq_bf=None, q_invs=None, q_s=None,
          max_col=None, waits=()):
    a_bc, b_bc = bufs["a_bc"], bufs["b_bc"]
    dbl = bufs.get("xrow_b") is not None
    xr2 = [bufs["xrow"], bufs.get("xrow_b", bufs["xrow"])]
    tp2 = [bufs["tmprow"], bufs.get("tmprow_b", bufs["tmprow"])]
    mc2 = [bufs["mcol1"], bufs.get("mcol_b", bufs["mcol1"])]
    rd = [dict(), dict()]
    acc_tok = ()
    last = None
    for dc in range(8):
        bq = (dc % 2) if dbl else 0
        xr = xr2[bq]
        ld = p.sdma(xr[:, 0:S], x_hbm[dc * 128:(dc + 1) * 128, :],
                    waits=list(rd[bq].get("xr", ())) + list(waits), ch=1)
        t1 = p.v(lambda E, xr=xr: E.tensor_tensor(
            xr[:, 0:S], xr[:, 0:S], a_bc[:], ALU.mult),
            waits=[ld] + list(waits))
        t2 = p.v(lambda E, xr=xr: E.tensor_tensor(
            xr[:, 0:S], xr[:, 0:S], b_bc[:], ALU.add), waits=[t1])
        readers = []
        if max_col is not None:
            t3 = p.v(lambda E, xr=xr, bq=bq: E.tensor_reduce(
                mc2[bq][:], xr[:, 0:S], AXX, ALU.max,
                apply_absolute_value=True),
                waits=[t2] + list(rd[bq].get("mc", ())))
            if dc == 0:
                acc = p.v(lambda E, bq=bq: E.tensor_copy(
                    max_col[:], mc2[bq][:]), waits=[t3])
            else:
                acc = p.v(lambda E, bq=bq: E.tensor_tensor(
                    max_col[:], max_col[:], mc2[bq][:], ALU.max),
                    waits=[t3] + list(acc_tok))
            acc_tok = (acc,)
            rd[bq]["mc"] = (acc,)
            readers.append(t3)
            last = acc
        if xn_bf is not None:
            cp = p.v(lambda E, dc=dc, xr=xr: E.tensor_copy(
                xn_bf[:, dc * S:(dc + 1) * S], xr[:, 0:S]), waits=[t2])
            readers.append(cp)
            last = cp
        if xnq_bf is not None:
            mgc = bufs.get("magicc")
            if mgc is not None:
                q1 = p.s(lambda E, xr=xr, bq=bq: E.activation(
                    tp2[bq][:, 0:S], xr[:, 0:S], AF.Identity, scale=q_invs,
                    bias=mgc[:, 0:1]),
                    waits=[t2] + list(rd[bq].get("tp", ())))
            else:
                q1 = p.v(lambda E, xr=xr, bq=bq: E.tensor_scalar(
                    tp2[bq][:, 0:S], xr[:, 0:S], q_invs, MAGIC, ALU.mult,
                    ALU.add), waits=[t2] + list(rd[bq].get("tp", ())))
            q2 = p.v(lambda E, dc=dc, bq=bq: E.tensor_scalar(
                xnq_bf[:, dc * S:(dc + 1) * S], tp2[bq][:, 0:S], -MAGIC, q_s,
                ALU.add, ALU.mult), waits=[q1])
            rd[bq]["tp"] = (q2,)
            readers.append(q1)
            last = q2
        rd[bq]["xr"] = tuple(readers) if readers else (t2,)
    return last


def em_fold(p, bufs, wT_bf, out_bf, bcol_bf, bFcol, waits=()):
    Fpad = bufs["Fpad"]
    pA, pC = bufs["pA"], bufs["pC"]
    pfree = [(), ()]
    g = 0
    for dm in range(8):
        for fc in range(5):
            sl = g % 2
            g += 1
            off = sl * 512
            mm = None
            for kc in range(8):
                w = (list(waits) + list(pfree[sl])) if kc == 0 else ()
                mm = p.t(lambda E, dm=dm, fc=fc, kc=kc, off=off: E.matmul(
                    pA[:, off:off + 256],
                    wT_bf[:, kc * 1024 + dm * 128:kc * 1024 + (dm + 1) * 128],
                    Fpad[:, kc * 1280 + fc * 256:kc * 1280 + (fc + 1) * 256],
                    start=(kc == 0), stop=(kc == 7)), waits=w)
            ev = p.v(lambda E, dm=dm, fc=fc, off=off: E.tensor_copy(
                out_bf[:, dm * 1280 + fc * 256:dm * 1280 + (fc + 1) * 256],
                pA[:, off:off + 256]), waits=[mm])
            pfree[sl] = (ev,)
    cfree = ()
    for ft in range(10):
        mm = None
        for kc in range(8):
            w = list(cfree) if kc == 0 else ()
            mm = p.t(lambda E, ft=ft, kc=kc: E.matmul(
                pC[:, 0:1],
                Fpad[:, kc * 1280 + ft * 128:kc * 1280 + (ft + 1) * 128],
                bcol_bf[:, kc:kc + 1],
                start=(kc == 0), stop=(kc == 7)), waits=w)
        ev = p.v(lambda E, ft=ft: E.tensor_copy(
            bFcol[:, ft:ft + 1], pC[:, 0:1]), waits=[mm])
        cfree = (ev,)
    return p.now("ve")


def em_rfft(p, bufs, WxF, bFcol, src_bf, tc, S, out32, waits=()):
    pA, pB = bufs["pA"], bufs["pB"]
    rfree = bufs.setdefault("_rfft_free", [(), (), (), ()])
    last = None
    for ft in range(10):
        sl = ft % 4
        ps = pA if sl < 2 else pB
        off = (sl % 2) * 512
        mm = None
        for kc in range(8):
            w = (list(waits) + list(rfree[sl])) if kc == 0 else ()
            mm = p.t(lambda E, ft=ft, kc=kc, ps=ps, off=off: E.matmul(
                ps[:, off:off + 512],
                WxF[:, kc * 1280 + ft * 128:kc * 1280 + (ft + 1) * 128],
                src_bf[:, kc * S + tc * 512:kc * S + (tc + 1) * 512],
                start=(kc == 0), stop=(kc == 7)), waits=w)
        last = p.s(lambda E, ft=ft, ps=ps, off=off: E.activation(
            out32[:, ft * 512:(ft + 1) * 512], ps[:, off:off + 512],
            AF.Identity, bias=bFcol[:, ft:ft + 1]), waits=[mm])
        rfree[sl] = (last,)
    return last


def em_scalar_to_col(p, bufs, scal_ap, col_ap, waits=()):
    pD = bufs["pD"]
    oner = bufs["oner"]
    mm = p.t(lambda E: E.matmul(pD[:, 0:1], oner[:], scal_ap, start=True,
                                stop=True), waits=list(waits) + [p.now("ve")])
    return p.v(lambda E: E.tensor_copy(col_ap, pD[:, 0:1]), waits=[mm])


def em_global_max(p, bufs, max_col, m1, waits=()):
    mrow = bufs["mrow"]
    d1 = p.gdma(mrow[0:1, 0:128], max_col, waits=waits)
    r1 = p.v(lambda E: E.tensor_reduce(m1[0:1, 0:1], mrow[0:1, 0:128], AXX,
                                       ALU.max), waits=[d1])
    return r1


def em_recv_max(p, m8_hbm, m8row, m1col_ap, waits=()):
    ld = p.gdma(m8row[0:1, :], m8_hbm[0:1, :], ch=3, waits=waits)
    return p.v(lambda E: E.tensor_reduce(m1col_ap, m8row[0:1, :], AXX,
                                         ALU.max), waits=[ld])


def common_bufs(nc, es, S, stats=True, big=True):
    sb = lambda n, sh, dt=F32: es.enter_context(nc.sbuf_tensor(n, list(sh), dt))
    ps = lambda n, sh, dt=F32: es.enter_context(nc.psum_tensor(n, list(sh), dt))
    bufs = {}
    if big:
        bufs["a_bc"] = sb("a_bc", [128, S])
        bufs["b_bc"] = sb("b_bc", [128, S])
        bufs["xrow"] = sb("xrow", [128, S])
        bufs["xn32"] = sb("xn32", [128, S])
        bufs["tmprow"] = sb("tmprow", [128, S])
    bufs["xt2"] = sb("xt2", [128, 1024])
    bufs["sq"] = sb("sq", [128, 512])
    if stats:
        bufs["mu_row"] = sb("mu_row", [1, S])
        bufs["sq_row"] = sb("sq_row", [1, S])
        bufs["b_row"] = bufs["mu_row"]
    else:
        bufs["b_row"] = sb("b_row", [1, S])
    bufs["a_row"] = sb("a_row", [1, S])
    bufs["onec"] = sb("onec", [128, 1])
    bufs["oner"] = sb("oner", [1, 128])
    bufs["mcol1"] = sb("mcol1", [128, 1])
    bufs["mrow"] = sb("mrow", [1, 128])
    bufs["eps1"] = sb("eps1", [1, 1])
    bufs["pA"] = ps("pA", [128, 1024])
    bufs["pB"] = ps("pB", [128, 1024])
    bufs["pC"] = ps("pC", [128, 512])
    bufs["pD"] = ps("pD", [128, 512])
    return bufs, sb, ps


def build_A(S, n_cores):
    nc = bass.Bass("TRN2", target_bir_lowering=False)
    TC = S // 512
    x = nc.dram_tensor("x", [D, S], F32, kind="ExternalInput")
    wkT = nc.dram_tensor("wkT", [D, D], F32, kind="ExternalInput")
    wvT = nc.dram_tensor("wvT", [D, D], F32, kind="ExternalInput")
    bkc = nc.dram_tensor("bkc", [128, 8], F32, kind="ExternalInput")
    bvc = nc.dram_tensor("bvc", [128, 8], F32, kind="ExternalInput")
    fpad_h = nc.dram_tensor("fpad", [D, 2 * NF], BF16, kind="ExternalInput")
    o_ab = nc.dram_tensor("o_ab", [2, S], F32, kind="ExternalOutput")
    o_shat = nc.dram_tensor("o_shat", [128, 10], F32, kind="ExternalOutput")
    o_mxn = nc.dram_tensor("o_mxn", [1, 1], F32, kind="ExternalOutput")

    es = ExitStack()
    bufs, sb, ps = common_bufs(nc, es, S, stats=True, big=False)
    xn_bf = sb("xn_bf", [128, 8 * S], BF16)
    bc_bf = sb("bc_bf", [128, 8], BF16)
    bkF = sb("bkF", [128, 10])
    bvF = sb("bvF", [128, 10])
    Sacc = sb("Sacc", [128, 10])
    mx = sb("mx", [128, 1])
    m1 = sb("m1", [1, 1])

    p = Prog(nc, es)
    p.v(lambda E: E.memset(bufs["onec"][:], 1.0))
    p.v(lambda E: E.memset(bufs["oner"][:], 1.0))
    p.v(lambda E: E.memset(Sacc[:], 0.0))
    p.flush()
    with ExitStack() as e1:
        sb1 = lambda n, sh, dt=F32: e1.enter_context(
            nc.sbuf_tensor(n, list(sh), dt))
        bufs["a_bc"] = sb1("a_bc", [128, S])
        bufs["b_bc"] = sb1("b_bc", [128, S])
        bufs["xrow"] = sb1("xrow", [128, S])
        bufs["xrow_b"] = sb1("xrow_b", [128, S])
        bufs["tmprow"] = sb1("tmprow", [128, S])
        bufs["mcol_b"] = sb1("mcol_b", [128, 1])
        em_ln_stats(p, bufs, x, S)
        em_ln_bcast(p, bufs, S, waits=[p.now("ve"), p.now("sc")])
        em_xn(p, bufs, x, S, xn_bf=xn_bf, max_col=mx, waits=[p.now("ve")])
        em_global_max(p, bufs, mx[:, 0:1], m1, waits=[p.now("ve")])
        p.flush()
    e23 = es.enter_context(ExitStack())
    WkF = e23.enter_context(nc.sbuf_tensor("WkF", [128, 8 * 2 * NF], BF16))
    WvF = e23.enter_context(nc.sbuf_tensor("WvF", [128, 8 * 2 * NF], BF16))
    with ExitStack() as e2:
        sb2 = lambda n, sh, dt=F32: e2.enter_context(
            nc.sbuf_tensor(n, list(sh), dt))
        bufs["Fpad"] = sb2("Fpad", [128, 8 * 2 * NF], BF16)
        wT_bf = sb2("wT_bf", [128, 8 * D], BF16)
        lf = p.sdma(bufs["Fpad"][:].rearrange("p (c f) -> p c f", c=8),
                    fpad_h[:].rearrange("(c p) f -> p c f", p=128))
        ldk = p.gdma(wT_bf[:].rearrange("p (c f) -> p c f", c=8),
                     wkT[:].rearrange("(c p) f -> p c f", p=128))
        ldbk = p.gdma(bc_bf[:], bkc[:])
        em_fold(p, bufs, wT_bf, WkF, bc_bf, bkF, waits=[lf, ldk, ldbk])
        ldv = p.gdma(wT_bf[:].rearrange("p (c f) -> p c f", c=8),
                     wvT[:].rearrange("(c p) f -> p c f", p=128),
                     waits=[p.now("te")])
        ldbv = p.gdma(bc_bf[:], bvc[:], waits=[p.now("te")])
        em_fold(p, bufs, wT_bf, WvF, bc_bf, bvF, waits=[ldv, ldbv])
        p.flush()
    with ExitStack() as e3:
        sb3 = lambda n, sh, dt=F32: e3.enter_context(
            nc.sbuf_tensor(n, list(sh), dt))
        Kbuf = sb3("Kbuf", [128, 10 * 512])
        Vbuf = sb3("Vbuf", [128, 10 * 512])
        prod = sb3("prod", [128, 512])
        pc1 = sb3("pc1", [128, 1])
        pc2 = sb3("pc2", [128, 1])
        for tc in range(TC):
            tk = em_rfft(p, bufs, WkF, bkF, xn_bf, tc, S, Kbuf,
                         waits=[p.now("ve")])
            tv = em_rfft(p, bufs, WvF, bvF, xn_bf, tc, S, Vbuf,
                         waits=[p.now("ve")])
            for m in range(5):
                Kr = Kbuf[:, m * 512:(m + 1) * 512]
                Ki = Kbuf[:, (m + 5) * 512:(m + 6) * 512]
                Vr = Vbuf[:, m * 512:(m + 1) * 512]
                Vi = Vbuf[:, (m + 5) * 512:(m + 6) * 512]
                t1 = p.v(lambda E, Kr=Kr, Vr=Vr: E.scalar_tensor_tensor(
                    prod[:], Kr, 1.0, Vr, ALU.mult, ALU.mult,
                    accum_out=pc1[:]), waits=[tk, tv, p.now("ve")])
                t2 = p.v(lambda E, Ki=Ki, Vi=Vi: E.scalar_tensor_tensor(
                    prod[:], Ki, 1.0, Vi, ALU.mult, ALU.mult,
                    accum_out=pc2[:]), waits=[t1])
                t3 = p.v(lambda E: E.tensor_tensor(pc1[:], pc1[:], pc2[:],
                                                   ALU.subtract), waits=[t2])
                t4 = p.v(lambda E, m=m: E.tensor_tensor(
                    Sacc[:, m:m + 1], Sacc[:, m:m + 1], pc1[:], ALU.add),
                    waits=[t3])
                t5 = p.v(lambda E, Kr=Kr, Vi=Vi: E.scalar_tensor_tensor(
                    prod[:], Kr, 1.0, Vi, ALU.mult, ALU.mult,
                    accum_out=pc1[:]), waits=[t4])
                t6 = p.v(lambda E, Ki=Ki, Vr=Vr: E.scalar_tensor_tensor(
                    prod[:], Ki, 1.0, Vr, ALU.mult, ALU.mult,
                    accum_out=pc2[:]), waits=[t5])
                t7 = p.v(lambda E: E.tensor_tensor(pc1[:], pc1[:], pc2[:],
                                                   ALU.add), waits=[t6])
                p.v(lambda E, m=m: E.tensor_tensor(
                    Sacc[:, m + 5:m + 6], Sacc[:, m + 5:m + 6], pc1[:],
                    ALU.add), waits=[t7])
        p.sdma(o_ab[0:1, :], bufs["a_row"][:],
               waits=[p.now("ve"), p.now("sc")], ch=3)
        p.sdma(o_ab[1:2, :], bufs["b_row"][:], waits=[p.now("ve")], ch=3)
        p.sdma(o_shat[:], Sacc[:], waits=[p.now("ve")], ch=3)
        p.gdma(o_mxn[:], m1[:], ch=3, waits=[p.now("ve")])
        p.wait_on("sync", p.alldma())
        p.flush()
    es.close()
    return nc


def em_scalar_cols(p, bufs, m1_ap, invs_col, s_col, waits=()):
    oner = bufs["oner"]
    pD = bufs["pD"]
    s1 = bufs["s1"]
    t1 = p.v(lambda E: E.tensor_scalar(s1[0:1, 0:1], m1_ap, 1.0 / 32767.0,
                                       1e-12, ALU.mult, ALU.max), waits=waits)
    mm = p.t(lambda E: E.matmul(pD[:, 0:1], oner[:], s1[0:1, 0:1], start=True,
                                stop=True), waits=[t1, p.now("ve")])
    t2 = p.v(lambda E: E.tensor_copy(s_col, pD[:, 0:1]), waits=[mm])
    t3 = p.v(lambda E: E.reciprocal(invs_col, s_col), waits=[t2])
    return t3


def em_lpbq_tile(p, bufs, w_hbm, it, I, lhsT_out, ident_bf, cons=(), ch=2,
                 pre="", nb_width=32, chain_tok=()):
    wtile, wq_bf = bufs[pre + "wtile"], bufs[pre + "wq_bf"]
    scl, invscl = bufs[pre + "scl"], bufs[pre + "invscl"]
    tmpw = bufs[pre + "tmpw"]
    pT = bufs["pT"]
    NB = I // 32
    ld = p.sdma(wtile[:, 0:I], w_hbm[it * 128:(it + 1) * 128, :],
                waits=list(cons) + list(chain_tok), ch=ch)
    wv = wtile[:, 0:I].rearrange("p (b i) -> p b i", i=32)
    sv = scl[:, 0:NB]
    t1 = p.v(lambda E: E.tensor_reduce(sv, wv, AXX, ALU.max,
                                       apply_absolute_value=True),
             waits=[ld] + list(chain_tok))
    t2 = p.v(lambda E: E.tensor_scalar(sv, sv, 1.0 / 7.0, 1e-12, ALU.mult,
                                       ALU.max), waits=[t1])
    t3 = p.v(lambda E: E.reciprocal(invscl[:, 0:NB], sv), waits=[t2])
    iv = bass.AP(invscl, 0, [[nb_width, 128], [1, NB], [0, 32]])
    svb = bass.AP(scl, 0, [[nb_width, 128], [1, NB], [0, 32]])
    tv = tmpw[:, 0:I].rearrange("p (b i) -> p b i", i=32)
    t4 = p.v(lambda E: E.tensor_tensor(tv, wv, iv, ALU.mult), waits=[t3])
    t5 = p.v(lambda E: E.tensor_scalar(tmpw[:, 0:I], tmpw[:, 0:I], MAGIC,
                                       -MAGIC, ALU.add, ALU.add), waits=[t4])
    t7 = p.v(lambda E: E.tensor_tensor(
        wq_bf[:, 0:I].rearrange("p (b i) -> p b i", i=32), tv, svb, ALU.mult),
        waits=[t5])
    pT2 = bufs.get("pT2")
    banks = [pT, pT2] if pT2 is not None else [pT]
    free = bufs.setdefault("_pt_free", [()] * len(banks))
    nkc = I // 128
    ev = None
    for kc in range(nkc):
        bk = kc % len(banks)
        tt = p.t(lambda E, kc=kc, bk=bk: E.transpose(
            banks[bk][:, 0:128], wq_bf[:, kc * 128:(kc + 1) * 128],
            ident_bf[:]), waits=[t7] + list(free[bk]))
        ew = [tt] + ([ev] if kc == nkc - 1 and ev is not None else [])
        if kc % 2 == 0:
            ev = p.v(lambda E, kc=kc, bk=bk: E.tensor_copy(
                lhsT_out[:, kc * 128:(kc + 1) * 128], banks[bk][:, 0:128]),
                waits=ew)
        else:
            ev = p.s(lambda E, kc=kc, bk=bk: E.activation(
                lhsT_out[:, kc * 128:(kc + 1) * 128], banks[bk][:, 0:128],
                AF.Identity), waits=ew)
        free[bk] = (ev,)
    return ev, t7


def qdq_chain(p, dst, src, invs_col, s_col, tmp, waits=(), magic_col=None,
              split=True):
    if split and magic_col is not None:
        t1 = p.s(lambda E: E.activation(tmp, src, AF.Identity,
                                        scale=invs_col, bias=magic_col),
                 waits=waits)
    else:
        t1 = p.v(lambda E: E.tensor_scalar(tmp, src, invs_col, MAGIC,
                                           ALU.mult, ALU.add), waits=waits)
    t2 = p.v(lambda E: E.tensor_scalar(dst, tmp, -MAGIC, s_col, ALU.add,
                                       ALU.mult), waits=[t1])
    return t2


def build_ffn_stage(stage, S, n_cores):
    nc = bass.Bass("TRN2", target_bir_lowering=False)
    TC = S // 512
    NC = n_cores
    x = nc.dram_tensor("x", [D, S], F32, kind="ExternalInput")
    ab = nc.dram_tensor("ab", [2, S], F32, kind="ExternalInput")
    mxn = nc.dram_tensor("mxn", [1, NC], F32, kind="ExternalInput")
    wup = nc.dram_tensor("wup", [3072, D], F32, kind="ExternalInput")
    wgate = nc.dram_tensor("wgate", [3072, D], F32, kind="ExternalInput")
    ident = nc.dram_tensor("ident", [128, 128], BF16, kind="ExternalInput")
    if stage in ("C", "D"):
        mgate_i = nc.dram_tensor("mgate", [1, NC], F32, kind="ExternalInput")
    if stage == "D":
        mup_i = nc.dram_tensor("mup", [1, NC], F32, kind="ExternalInput")
        mg2_i = nc.dram_tensor("mg2", [1, NC], F32, kind="ExternalInput")
    nout = {"B": 2, "C": 1, "D": 1}[stage]
    o_max = nc.dram_tensor("o_max", [1, nout], F32, kind="ExternalOutput")

    es = ExitStack()
    bufs, sb, ps = common_bufs(nc, es, S, stats=False, big=False)
    bufs["pT"] = es.enter_context(nc.psum_tensor("pT", [128, 128], BF16))
    bufs["pT2"] = es.enter_context(nc.psum_tensor("pT2", [128, 128], BF16))
    xnq_bf = sb("xnq_bf", [128, 8 * S], BF16)
    bufs["s1"] = sb("s1", [1, 2])
    ident_bf = sb("ident_bf", [128, 128], BF16)
    invs_xn = sb("invs_xn", [128, 1])
    s_xn = sb("s_xn", [128, 1])
    m1a = sb("m1a", [1, 1])
    m8row = sb("m8row", [1, NC])
    mxcol = sb("mxcol", [128, 2])
    magicc = sb("magicc", [128, 1])
    bufs["magicc"] = magicc
    mcol1 = bufs["mcol1"]
    cols = {}
    for nm in ("gate", "up", "g2"):
        cols["invs_" + nm] = sb("invs_" + nm, [128, 1])
        cols["s_" + nm] = sb("s_" + nm, [128, 1])

    p = Prog(nc, es)
    p.v(lambda E: E.memset(bufs["onec"][:], 1.0))
    p.v(lambda E: E.memset(bufs["oner"][:], 1.0))
    p.v(lambda E: E.memset(mxcol[:], 0.0))
    p.v(lambda E: E.memset(magicc[:], MAGIC))
    lid = p.sdma(ident_bf[:], ident[:])
    p.wait_on("tensor", [lid])
    lab = p.sdma(bufs["a_row"][:], ab[0:1, :], ch=1)
    lab2 = p.sdma(bufs["b_row"][:], ab[1:2, :], ch=1)
    p.flush()
    lmx = em_recv_max(p, mxn, m8row, m1a[0:1, 0:1])
    em_scalar_cols(p, bufs, m1a[0:1, 0:1], invs_xn[:], s_xn[:], waits=[lmx])
    for nm, hs in (("gate", "mgate"), ("up", "mup"), ("g2", "mg2")):
        need = (stage in ("C", "D") and nm == "gate") or                (stage == "D" and nm in ("up", "g2"))
        if need:
            hten = mgate_i if hs == "mgate" else (
                mup_i if hs == "mup" else mg2_i)
            lm = em_recv_max(p, hten, m8row, m1a[0:1, 0:1],
                             waits=[p.now("ve")])
            em_scalar_cols(p, bufs, m1a[0:1, 0:1], cols["invs_" + nm][:],
                           cols["s_" + nm][:], waits=[lm])
    with ExitStack() as e1:
        sb1 = lambda n, sh, dt=F32: e1.enter_context(
            nc.sbuf_tensor(n, list(sh), dt))
        bufs["a_bc"] = sb1("a_bc", [128, S])
        bufs["b_bc"] = sb1("b_bc", [128, S])
        bufs["xrow"] = sb1("xrow", [128, S])
        bufs["xrow_b"] = sb1("xrow_b", [128, S])
        bufs["tmprow"] = sb1("tmprow", [128, S])
        bufs["tmprow_b"] = sb1("tmprow_b", [128, S])
        em_ln_bcast(p, bufs, S, waits=[lab, lab2])
        em_xn(p, bufs, x, S, xnq_bf=xnq_bf, q_invs=invs_xn[:, 0:1],
              q_s=s_xn[:, 0:1], waits=[p.now("ve")])
        p.flush()

    e2s = es.enter_context(ExitStack())
    sb2 = lambda n, sh, dt=F32: e2s.enter_context(
        nc.sbuf_tensor(n, list(sh), dt))
    for pre in ("u0_", "u1_", "g0_", "g1_"):
        bufs[pre + "wtile"] = sb2(pre + "wtile", [128, D])
        bufs[pre + "wq_bf"] = sb2(pre + "wq_bf", [128, D], BF16)
        bufs[pre + "scl"] = sb2(pre + "scl", [128, 32])
        bufs[pre + "invscl"] = sb2(pre + "invscl", [128, 32])
        bufs[pre + "tmpw"] = sb2(pre + "tmpw", [128, D])
    lhsT = [sb2("lhsT%d" % i, [128, 8 * 128], BF16) for i in range(4)]
    raw32 = sb2("raw32", [128, 512])
    up32 = sb2("up32", [128, 512])
    c1 = sb2("c1", [128, 512])
    c2 = sb2("c2", [128, 512])
    c3 = sb2("c3", [128, 512])

    raw32b = [raw32, sb2("raw32_b", [128, 512])]
    up32b = [up32, sb2("up32_b", [128, 512])]
    c1b = [c1, sb2("c1_b", [128, 512])]
    c2b = [c2, sb2("c2_b", [128, 512])]
    c3b = [c3, sb2("c3_b", [128, 512])]
    mcol2 = [mcol1, sb2("mcol_b", [128, 1])]
    mg = magicc[:, 0:1]

    names = {"B": ("up", "gate"), "C": ("gate",), "D": ("up", "gate")}[stage]
    whbm = {"up": wup, "gate": wgate}
    ltok = {"up": {0: (), 1: ()}, "gate": {0: (), 1: ()}}
    chain_toks = {"up": {0: (), 1: ()}, "gate": {0: (), 1: ()}}
    lhsT2 = {"up": [lhsT[0], lhsT[1]], "gate": [lhsT[2], lhsT[3]]}
    pt = [dict(), dict()]
    psum_free = {"up": [(), ()], "gate": [(), ()]}
    r2_last = {0: (), 1: ()}
    for it in range(24):
        bb = it % 2
        tls = {}
        for nm in names:
            tls[nm], ct = em_lpbq_tile(
                p, bufs, whbm[nm], it, D, lhsT2[nm][bb], ident_bf,
                cons=ltok[nm][bb], ch=2 if nm == "up" else 3,
                pre=("u%d_" % bb) if nm == "up" else ("g%d_" % bb),
                chain_tok=chain_toks[nm][bb])
            chain_toks[nm][bb] = (ct,)
        mm_last = {nm: None for nm in names}
        for j in range(0, TC, 2):
            sts = []
            for tc in range(j, min(j + 2, TC)):
                bq = tc % 2
                st = {"b": bq}
                for nm in names:
                    ps_ = bufs["pA"] if nm == "up" else bufs["pB"]
                    mm = None
                    for kc in range(8):
                        w = ([tls[nm]] + list(psum_free[nm][bq])) \
                            if kc == 0 else ()
                        mm = p.t(lambda E, kc=kc, tc=tc, nm=nm, bb=bb,
                                 ps_=ps_, bq=bq: E.matmul(
                            ps_[:, bq * 512:bq * 512 + 512],
                            lhsT2[nm][bb][:, kc * 128:(kc + 1) * 128],
                            xnq_bf[:, kc * S + tc * 512:
                                   kc * S + (tc + 1) * 512],
                            start=(kc == 0), stop=(kc == 7)), waits=w)
                    st["mm_" + nm] = mm
                    mm_last[nm] = mm
                sts.append(st)
            if stage == "B":
                for sl, nm in enumerate(names):
                    ps_ = bufs["pA"] if nm == "up" else bufs["pB"]
                    for st in sts:
                        bq = st["b"]
                        r1 = p.v(lambda E, ps_=ps_, bq=bq: E.tensor_reduce(
                            mcol2[bq][:], ps_[:, bq * 512:bq * 512 + 512],
                            AXX, ALU.max, apply_absolute_value=True),
                            waits=[st["mm_" + nm]] + list(pt[bq].get("mc", ())))
                        psum_free[nm][bq] = (r1,)
                        r2 = p.v(lambda E, sl=sl, bq=bq: E.tensor_tensor(
                            mxcol[:, sl:sl + 1], mxcol[:, sl:sl + 1],
                            mcol2[bq][:], ALU.max),
                            waits=[r1] + list(r2_last[sl]))
                        r2_last[sl] = (r2,)
                        pt[bq]["mc"] = (r2,)
                continue
            if stage == "D":
                for st in sts:
                    bq = st["b"]
                    st["evu"] = p.s(lambda E, bq=bq: E.activation(
                        up32b[bq][:], bufs["pA"][:, bq * 512:bq * 512 + 512],
                        AF.Identity),
                        waits=[st["mm_up"]] + list(pt[bq].get("up32", ())))
                    psum_free["up"][bq] = (st["evu"],)
            for st in sts:
                bq = st["b"]
                st["evg"] = p.s(lambda E, bq=bq: E.activation(
                    raw32b[bq][:], bufs["pB"][:, bq * 512:bq * 512 + 512],
                    AF.Identity),
                    waits=[st["mm_gate"]] + list(pt[bq].get("raw32", ())))
                psum_free["gate"][bq] = (st["evg"],)
            for st in sts:
                bq = st["b"]
                st["g1a"] = p.s(lambda E, bq=bq: E.activation(
                    c1b[bq][:], raw32b[bq][:], AF.Identity,
                    scale=cols["invs_gate"][:, 0:1], bias=mg),
                    waits=[st["evg"]] + list(pt[bq].get("c1", ())))
            for st in sts:
                bq = st["b"]
                st["g1b"] = p.v(lambda E, bq=bq: E.tensor_scalar(
                    raw32b[bq][:], c1b[bq][:], -MAGIC,
                    cols["s_gate"][:, 0:1], ALU.add, ALU.mult),
                    waits=[st["g1a"]])
            for st in sts:
                bq = st["b"]
                st["sg"] = p.s(lambda E, bq=bq: E.activation(
                    c2b[bq][:], raw32b[bq][:], AF.Sigmoid),
                    waits=[st["g1b"]] + list(pt[bq].get("c2", ())))
            for st in sts:
                bq = st["b"]
                st["f1a"] = p.s(lambda E, bq=bq: E.activation(
                    c3b[bq][:], c2b[bq][:], AF.Identity, scale=65536.0,
                    bias=mg), waits=[st["sg"]] + list(pt[bq].get("c3", ())))
            for st in sts:
                bq = st["b"]
                st["f1b"] = p.v(lambda E, bq=bq: E.tensor_scalar(
                    c2b[bq][:], c3b[bq][:], -MAGIC, 65535.0, ALU.add,
                    ALU.min), waits=[st["f1a"]])
            for st in sts:
                bq = st["b"]
                st["pm"] = p.v(lambda E, bq=bq: E.scalar_tensor_tensor(
                    raw32b[bq][:], c2b[bq][:], 1.0 / 65536.0, raw32b[bq][:],
                    ALU.mult, ALU.mult), waits=[st["f1b"]])
                pt[bq]["c2"] = (st["pm"],)
            if stage == "C":
                for st in sts:
                    bq = st["b"]
                    pt[bq]["c1"] = (st["g1b"],)
                    pt[bq]["c3"] = (st["f1b"],)
                    r1 = p.v(lambda E, bq=bq: E.tensor_reduce(
                        mcol2[bq][:], raw32b[bq][:], AXX, ALU.max,
                        apply_absolute_value=True),
                        waits=[st["pm"]] + list(pt[bq].get("mc", ())))
                    pt[bq]["raw32"] = (r1,)
                    r2 = p.v(lambda E, bq=bq: E.tensor_tensor(
                        mxcol[:, 0:1], mxcol[:, 0:1], mcol2[bq][:],
                        ALU.max), waits=[r1] + list(r2_last[0]))
                    r2_last[0] = (r2,)
                    pt[bq]["mc"] = (r2,)
                continue
            for st in sts:
                bq = st["b"]
                st["g2a"] = p.s(lambda E, bq=bq: E.activation(
                    c1b[bq][:], raw32b[bq][:], AF.Identity,
                    scale=cols["invs_g2"][:, 0:1], bias=mg),
                    waits=[st["pm"]] + list(pt[bq].get("c1", ())))
            for st in sts:
                bq = st["b"]
                st["g2b"] = p.v(lambda E, bq=bq: E.tensor_scalar(
                    raw32b[bq][:], c1b[bq][:], -MAGIC, cols["s_g2"][:, 0:1],
                    ALU.add, ALU.mult), waits=[st["g2a"]])
                pt[bq]["c1"] = (st["g2b"],)
            for st in sts:
                bq = st["b"]
                st["uqa"] = p.s(lambda E, bq=bq: E.activation(
                    c3b[bq][:], up32b[bq][:], AF.Identity,
                    scale=cols["invs_up"][:, 0:1], bias=mg),
                    waits=[st["evu"], st["f1b"]])
                pt[bq]["up32"] = (st["uqa"],)
            for st in sts:
                bq = st["b"]
                st["uqb"] = p.v(lambda E, bq=bq: E.tensor_scalar(
                    c3b[bq][:], c3b[bq][:], -MAGIC, cols["s_up"][:, 0:1],
                    ALU.add, ALU.mult), waits=[st["uqa"]])
            for st in sts:
                bq = st["b"]
                st["orw"] = p.v(lambda E, bq=bq: E.tensor_tensor(
                    raw32b[bq][:], raw32b[bq][:], c3b[bq][:], ALU.mult),
                    waits=[st["uqb"], st["g2b"]])
                pt[bq]["c3"] = (st["orw"],)
            for st in sts:
                bq = st["b"]
                r1 = p.v(lambda E, bq=bq: E.tensor_reduce(
                    mcol2[bq][:], raw32b[bq][:], AXX, ALU.max,
                    apply_absolute_value=True),
                    waits=[st["orw"]] + list(pt[bq].get("mc", ())))
                pt[bq]["raw32"] = (r1,)
                r2 = p.v(lambda E, bq=bq: E.tensor_tensor(
                    mxcol[:, 0:1], mxcol[:, 0:1], mcol2[bq][:], ALU.max),
                    waits=[r1] + list(r2_last[0]))
                r2_last[0] = (r2,)
                pt[bq]["mc"] = (r2,)
        for nm in names:
            ltok[nm][bb] = (mm_last[nm],)
    mrow = bufs["mrow"]
    res = []
    for sl in range(nout):
        d1 = p.gdma(mrow[0:1, 0:128], mxcol[:, sl:sl + 1],
                    waits=[p.now("ve")], ch=1)
        r1 = p.v(lambda E, sl=sl: E.tensor_reduce(
            bufs["s1"][0:1, sl:sl + 1], mrow[0:1, 0:128], AXX, ALU.max),
            waits=[d1])
        res.append(r1)
    st = p.gdma(o_max[:], bufs["s1"][0:1, 0:nout], waits=res, ch=2)
    p.wait_on("sync", p.alldma())
    p.flush()
    es.close()
    return nc


def fq16_ops_inline(p, dst, src, tmp, waits=(), magic_col=None):
    if magic_col is not None:
        t1 = p.s(lambda E: E.activation(tmp[:], src[:], AF.Identity,
                                        scale=65536.0, bias=magic_col),
                 waits=waits)
    else:
        t1 = p.v(lambda E: E.tensor_scalar(tmp[:], src[:], 65536.0, MAGIC,
                                           ALU.mult, ALU.add), waits=waits)
    t2 = p.v(lambda E: E.tensor_scalar(dst[:], tmp[:], -MAGIC, 65535.0,
                                       ALU.add, ALU.min), waits=[t1])
    return t2




def build_E(S, n_cores):
    nc = bass.Bass("TRN2", target_bir_lowering=False)
    TC = S // 512
    TT = S // 128
    x = nc.dram_tensor("x", [D, S], F32, kind="ExternalInput")
    ab = nc.dram_tensor("ab", [2, S], F32, kind="ExternalInput")
    shat = nc.dram_tensor("shat", [128, 10], F32, kind="ExternalInput")
    NC = n_cores
    m_in = {}
    for nm in ("mxn", "mgate", "mup", "mg2", "mo"):
        m_in[nm] = nc.dram_tensor(nm, [1, NC], F32, kind="ExternalInput")
    wqT = nc.dram_tensor("wqT", [D, D], F32, kind="ExternalInput")
    bqc = nc.dram_tensor("bqc", [128, 8], F32, kind="ExternalInput")
    fpad_h = nc.dram_tensor("fpad", [D, 2 * NF], BF16, kind="ExternalInput")
    cmat_h = nc.dram_tensor("cmat", [NF, 2 * D], BF16, kind="ExternalInput")
    ident = nc.dram_tensor("ident", [128, 128], BF16, kind="ExternalInput")
    wup = nc.dram_tensor("wup", [3072, D], F32, kind="ExternalInput")
    wgate = nc.dram_tensor("wgate", [3072, D], F32, kind="ExternalInput")
    wdown = nc.dram_tensor("wdown", [D, 3072], F32, kind="ExternalInput")
    awq = nc.dram_tensor("awq", [D, 256], F32, kind="ExternalInput")
    awk = nc.dram_tensor("awk", [D, 256], F32, kind="ExternalInput")
    wvdT = nc.dram_tensor("wvdT", [256, D], F32, kind="ExternalInput")
    wvup = nc.dram_tensor("wvup", [256, D], F32, kind="ExternalInput")
    rwqT = nc.dram_tensor("rwqT", [D, D], F32, kind="ExternalInput")
    rwkT = nc.dram_tensor("rwkT", [D, D], F32, kind="ExternalInput")
    rbqc = nc.dram_tensor("rbqc", [128, 8], F32, kind="ExternalInput")
    rbkc = nc.dram_tensor("rbkc", [128, 8], F32, kind="ExternalInput")
    ewqT = nc.dram_tensor("ewqT", [128, 8 * 128], F32, kind="ExternalInput")
    ewkT = nc.dram_tensor("ewkT", [128, 8 * 128], F32, kind="ExternalInput")
    ebqc = nc.dram_tensor("ebqc", [128, 8], F32, kind="ExternalInput")
    ebkc = nc.dram_tensor("ebkc", [128, 8], F32, kind="ExternalInput")
    wexp = nc.dram_tensor("wexp", [8, D, D], F32, kind="ExternalInput")
    bexp = nc.dram_tensor("bexp", [8, D], F32, kind="ExternalInput")
    o_x2 = nc.dram_tensor("o_x2", [D, S], F32, kind="ExternalOutput")
    xn_d = nc.dram_tensor("xn_d", [128, 8 * S], BF16, kind="Internal")
    xnq_d = nc.dram_tensor("xnq_d", [128, 8 * S], BF16, kind="Internal")
    sfm_d = nc.dram_tensor("sfm_d", [D, S], F32, kind="Internal")
    wqlu_d = nc.dram_tensor("wqlu_d", [128, 24 * 1024], BF16, kind="Internal")
    wqlg_d = nc.dram_tensor("wqlg_d", [128, 24 * 1024], BF16, kind="Internal")
    wqld_d = nc.dram_tensor("wqld_d", [128, 8 * 24 * 128], BF16, kind="Internal")
    x1_d = nc.dram_tensor("x1_d", [D, S], F32, kind="Internal")
    xn1_d = nc.dram_tensor("xn1_d", [128, 8 * S], BF16, kind="Internal")
    s1_d = nc.dram_tensor("s1_d", [D, S], F32, kind="Internal")
    m_d = nc.dram_tensor("m_d", [8, S], F32, kind="Internal")

    es = ExitStack()
    sb = lambda n, sh, dt=F32: es.enter_context(nc.sbuf_tensor(n, list(sh), dt))
    ps_ = lambda n, sh, dt=F32: es.enter_context(nc.psum_tensor(n, list(sh), dt))
    bufs = {}
    bufs["pA"] = ps_("pA", [128, 1024])
    bufs["pB"] = ps_("pB", [128, 1024])
    bufs["pCD"] = ps_("pCD", [128, 1024])
    bufs["pC"] = bufs["pCD"][:, 0:512]
    bufs["pD"] = bufs["pCD"][:, 512:1024]
    pre_ps = ExitStack()
    bufs["pT"] = pre_ps.enter_context(nc.psum_tensor("pT", [128, 128], BF16))
    bufs["pT2"] = pre_ps.enter_context(nc.psum_tensor("pT2", [128, 128],
                                                      BF16))
    for nm in ("onec", "mcol1"):
        bufs[nm] = sb(nm, [128, 1])
    bufs["oner"] = sb("oner", [1, 128])
    bufs["eps1"] = sb("eps1", [1, 1])
    bufs["mrow"] = sb("mrow", [1, 128])
    bufs["s1"] = sb("s1", [1, 2])
    bufs["xt2"] = sb("xt2", [128, 1024])
    bufs["sq"] = sb("sq", [128, 512])
    m1a = sb("m1a", [1, 1])
    magicc = sb("magicc", [128, 1])
    bufs["magicc"] = magicc
    ident_bf = sb("ident_bf", [128, 128], BF16)
    cols = {}
    for nm in ("xn", "gate", "up", "g2", "o"):
        cols["invs_" + nm] = sb("invs_" + nm, [128, 1])
        cols["s_" + nm] = sb("s_" + nm, [128, 1])

    p = Prog(nc, es)
    p.v(lambda E: E.memset(bufs["onec"][:], 1.0))
    p.v(lambda E: E.memset(bufs["oner"][:], 1.0))
    p.v(lambda E: E.memset(magicc[:], MAGIC))
    lid = p.sdma(ident_bf[:], ident[:])
    p.wait_on("tensor", [lid])
    m8row = sb("m8row", [1, NC])
    for nm in ("xn", "gate", "up", "g2", "o"):
        lm = em_recv_max(p, m_in[{"xn": "mxn", "gate": "mgate", "up": "mup",
                                  "g2": "mg2", "o": "mo"}[nm]], m8row,
                         m1a[0:1, 0:1], waits=[p.now("ve")])
        em_scalar_cols(p, bufs, m1a[0:1, 0:1], cols["invs_" + nm][:],
                       cols["s_" + nm][:], waits=[lm])
    p.flush()

    with ExitStack() as e1:
        sb1 = lambda n, sh, dt=F32: e1.enter_context(
            nc.sbuf_tensor(n, list(sh), dt))
        bufs["a_row"] = sb1("a_row", [1, S])
        bufs["b_row"] = sb1("b_row", [1, S])
        bufs["a_bc"] = sb1("a_bc", [128, S])
        bufs["b_bc"] = sb1("b_bc", [128, S])
        xr2 = [sb1("xrow%d" % i, [128, S]) for i in range(2)]
        tp2 = [sb1("tmpr%d" % i, [128, S]) for i in range(2)]
        ob2 = [sb1("obf%d" % i, [128, S], BF16) for i in range(2)]
        oq2 = [sb1("obfq%d" % i, [128, S], BF16) for i in range(2)]
        lab = p.sdma(bufs["a_row"][:], ab[0:1, :], ch=1)
        lab2 = p.sdma(bufs["b_row"][:], ab[1:2, :], ch=1)
        em_ln_bcast(p, bufs, S, waits=[lab, lab2])
        bw = (p.now("ve"),)
        rd1 = [dict(), dict()]
        st1 = st2 = None
        for dc in range(8):
            bq = dc % 2
            xr = xr2[bq]
            ld = p.sdma(xr[:], x[dc * 128:(dc + 1) * 128, :],
                        waits=list(rd1[bq].get("xr", ())), ch=1)
            t1 = p.v(lambda E, xr=xr: E.tensor_tensor(
                xr[:], xr[:], bufs["a_bc"][:], ALU.mult),
                waits=[ld] + list(bw))
            t2 = p.v(lambda E, xr=xr: E.tensor_tensor(
                xr[:], xr[:], bufs["b_bc"][:], ALU.add), waits=[t1])
            c1t = p.v(lambda E, xr=xr, bq=bq: E.tensor_copy(
                ob2[bq][:], xr[:]),
                waits=[t2] + list(rd1[bq].get("obf", ())))
            st1 = p.sdma(xn_d[:, dc * S:(dc + 1) * S], ob2[bq][:],
                         waits=[c1t], ch=2)
            rd1[bq]["obf"] = (st1,)
            q3 = qdq_chain(p, oq2[bq][:], xr[:], cols["invs_xn"][:, 0:1],
                           cols["s_xn"][:, 0:1], tp2[bq][:],
                           waits=[t2] + list(rd1[bq].get("oq", ())))
            st2 = p.sdma(xnq_d[:, dc * S:(dc + 1) * S], oq2[bq][:],
                         waits=[q3], ch=3)
            rd1[bq]["oq"] = (st2,)
            rd1[bq]["xr"] = (c1t, q3)
        p.wait_on("vector", [st1, st2])
        p.flush()

    with ExitStack() as e2:
        sb2 = lambda n, sh, dt=F32: e2.enter_context(
            nc.sbuf_tensor(n, list(sh), dt))
        bufs["Fpad"] = sb2("Fpad", [128, 8 * 2 * NF], BF16)
        WqF = sb2("WqF", [128, 8 * 2 * NF], BF16)
        wT_bf = sb2("wT_bf", [128, 8 * D], BF16)
        bc_bf = sb2("bc_bf", [128, 8], BF16)
        bqF = sb2("bqF", [128, 10])
        Cmat = sb2("Cmat", [128, 5 * 2048], BF16)
        Scol = sb2("Scol", [128, 10])
        xnsl = sb2("xnsl", [128, 8 * 512], BF16)
        Q32 = sb2("Q32", [128, 10 * 512])
        Pbuf = sb2("Pbuf", [128, 10 * 512], BF16)
        ctmp = sb2("ctmp", [128, 512])
        sfmt = sb2("sfmt", [128, 512])
        lf = p.sdma(bufs["Fpad"][:].rearrange("p (c f) -> p c f", c=8),
                    fpad_h[:].rearrange("(c p) f -> p c f", p=128))
        lw = p.gdma(wT_bf[:].rearrange("p (c f) -> p c f", c=8),
                    wqT[:].rearrange("(c p) f -> p c f", p=128))
        lb = p.gdma(bc_bf[:], bqc[:])
        lc = p.sdma(Cmat[:].rearrange("p (c f) -> p c f", c=5),
                    cmat_h[:].rearrange("(c p) f -> p c f", p=128), ch=1)
        lsh = p.sdma(Scol[:], shat[:], ch=1)
        em_fold(p, bufs, wT_bf, WqF, bc_bf, bqF, waits=[lf, lw, lb])
        ldx = None
        for tc in range(TC):
            ldx = p.sdma(
                xnsl[:].rearrange("p (c s) -> p c s", c=8),
                xn_d[:].rearrange("p (c s) -> p c s", c=8)[
                    :, :, tc * 512:(tc + 1) * 512],
                waits=[p.now("te")], ch=2)
            em_rfft(p, bufs, WqF, bqF, xnsl, 0, 512, Q32,
                    waits=[ldx, p.now("ve")])
            tq = p.now("sc")
            last = []
            for m in range(5):
                Qr = Q32[:, m * 512:(m + 1) * 512]
                Qi = Q32[:, (m + 5) * 512:(m + 6) * 512]
                Pr = Pbuf[:, m * 512:(m + 1) * 512]
                Pi = Pbuf[:, (m + 5) * 512:(m + 6) * 512]
                u1 = p.v(lambda E, Qr=Qr, m=m: E.tensor_scalar(
                    ctmp[:], Qr, Scol[:, m:m + 1], None, ALU.mult),
                    waits=[tq, lsh, p.now("ve"), p.now("te")])
                u2 = p.v(lambda E, Qi=Qi, Pr=Pr, m=m: E.scalar_tensor_tensor(
                    Pr, Qi, Scol[:, m + 5:m + 6], ctmp[:], ALU.mult, ALU.add),
                    waits=[u1])
                u3 = p.v(lambda E, Qi=Qi, m=m: E.tensor_scalar(
                    ctmp[:], Qi, Scol[:, m:m + 1], -1.0, ALU.mult, ALU.mult),
                    waits=[u2])
                u4 = p.v(lambda E, Qr=Qr, Pi=Pi, m=m: E.scalar_tensor_tensor(
                    Pi, Qr, Scol[:, m + 5:m + 6], ctmp[:], ALU.mult, ALU.add),
                    waits=[u3])
                last.append(u4)
            for dm in range(8):
                mm = None
                for fc in range(5):
                    w = [last[-1]] if fc == 0 else ()
                    mm = p.t(lambda E, dm=dm, fc=fc: E.matmul(
                        bufs["pC"][:, 0:512],
                        Cmat[:, fc * 2048 + dm * 128:fc * 2048 + (dm + 1) * 128],
                        Pbuf[:, fc * 512:(fc + 1) * 512],
                        start=(fc == 0), stop=False), waits=w)
                for fc in range(5):
                    mm = p.t(lambda E, dm=dm, fc=fc: E.matmul(
                        bufs["pC"][:, 0:512],
                        Cmat[:, fc * 2048 + 1024 + dm * 128:
                             fc * 2048 + 1024 + (dm + 1) * 128],
                        Pbuf[:, (fc + 5) * 512:(fc + 6) * 512],
                        start=False, stop=(fc == 4)))
                ev = p.v(lambda E: E.tensor_copy(sfmt[:], bufs["pC"][:, 0:512]),
                         waits=[mm, p.now("sd3")])
                p.wait_on("tensor", [ev])
                p.sdma(sfm_d[dm * 128:(dm + 1) * 128, tc * 512:(tc + 1) * 512],
                       sfmt[:], waits=[ev], ch=3)
        p.flush()

    with ExitStack() as e3:
        sb3 = lambda n, sh, dt=F32: e3.enter_context(
            nc.sbuf_tensor(n, list(sh), dt))
        for pre, I in (("u0_", D), ("u1_", D), ("d0_", 3072), ("d1_", 3072)):
            bufs[pre + "wtile"] = sb3(pre + "wtile", [128, I])
            bufs[pre + "wq_bf"] = sb3(pre + "wq_bf", [128, I], BF16)
            bufs[pre + "scl"] = sb3(pre + "scl", [128, I // 32])
            bufs[pre + "invscl"] = sb3(pre + "invscl", [128, I // 32])
            bufs[pre + "tmpw"] = sb3(pre + "tmpw", [128, I])
        lhsTa = [sb3("lhsTa0", [128, 1024], BF16),
                 sb3("lhsTa1", [128, 1024], BF16)]
        lhsTd = [sb3("lhsTd0", [128, 3072], BF16),
                 sb3("lhsTd1", [128, 3072], BF16)]
        stok = {0: (), 1: ()}
        ctok = {0: (), 1: ()}
        k = 0
        for nm, wh, dst in (("up", wup, wqlu_d), ("gate", wgate, wqlg_d)):
            for it in range(24):
                bb = k % 2
                k += 1
                tl, ct = em_lpbq_tile(p, bufs, wh, it, D, lhsTa[bb], ident_bf,
                                      cons=stok[bb], ch=2 + bb,
                                      pre="u%d_" % bb, chain_tok=ctok[bb])
                ctok[bb] = (ct,)
                st = p.sdma(dst[:, it * 1024:(it + 1) * 1024], lhsTa[bb][:],
                            waits=[tl], ch=bb)
                stok[bb] = (st,)
        stok = {0: (), 1: ()}
        ctok = {0: (), 1: ()}
        for ot in range(8):
            bb = ot % 2
            tl, ct = em_lpbq_tile(p, bufs, wdown, ot, 3072, lhsTd[bb],
                                  ident_bf, cons=stok[bb], ch=2 + bb,
                                  pre="d%d_" % bb, nb_width=96,
                                  chain_tok=ctok[bb])
            ctok[bb] = (ct,)
            st = p.sdma(wqld_d[:, ot * 3072:(ot + 1) * 3072], lhsTd[bb][:],
                        waits=[tl], ch=bb)
            stok[bb] = (st,)
        p.wait_on("vector", p.alldma())
        p.flush()

    NQ = max(1, TC // 2)
    with ExitStack() as e4:
        sb4 = lambda n, sh, dt=F32: e4.enter_context(
            nc.sbuf_tensor(n, list(sh), dt))
        xnqq = sb4("xnqq", [128, 8 * 1024], BF16)
        oqq = sb4("oqq", [128, 24 * 1024], BF16)
        lu = [sb4("lu%d" % i, [128, 1024], BF16) for i in range(2)]
        lg = [sb4("lg%d" % i, [128, 1024], BF16) for i in range(2)]
        ldn = [sb4("ldn%d" % i, [128, 3072], BF16) for i in range(2)]
        up32b = [sb4("up32_%d" % i, [128, 512]) for i in range(2)]
        g32b = [sb4("g32_%d" % i, [128, 512]) for i in range(2)]
        c1b = [sb4("cc1_%d" % i, [128, 512]) for i in range(2)]
        c2b = [sb4("cc2_%d" % i, [128, 512]) for i in range(2)]
        c3b = [sb4("cc3_%d" % i, [128, 512]) for i in range(2)]
        sftb = [sb4("sft_%d" % i, [128, 512]) for i in range(2)]
        xtib = [sb4("xti_%d" % i, [128, 512]) for i in range(2)]
        x1tb = [sb4("x1t_%d" % i, [128, 512]) for i in range(2)]
        mg = magicc[:, 0:1]
        pt = [dict(), dict()]
        psum_free = {"up": [(), ()], "gate": [(), ()]}
        pt2 = [dict(), dict()]
        lx_prev = ()
        dn_prev = ()
        for q in range(NQ):
            nch = min(2, TC - q * 2)
            lx = p.sdma(
                xnqq[:].rearrange("p (c s) -> p c s", c=8)[:, :, 0:nch * 512],
                xnq_d[:].rearrange("p (c s) -> p c s", c=8)[
                    :, :, q * 1024:q * 1024 + nch * 512],
                waits=list(lx_prev), ch=1)
            ltoks = {0: (), 1: ()}
            mm_all = []
            for it in range(24):
                bb = it % 2
                l1 = p.sdma(lu[bb][:], wqlu_d[:, it * 1024:(it + 1) * 1024],
                            waits=ltoks[bb], ch=2)
                l2 = p.sdma(lg[bb][:], wqlg_d[:, it * 1024:(it + 1) * 1024],
                            waits=(), ch=2)
                sts = []
                for tcq in range(nch):
                    bq = tcq % 2
                    st = {"b": bq}
                    for nm, ps_, lt, lhs in (("up", bufs["pA"], l1, lu),
                                             ("gate", bufs["pB"], l2, lg)):
                        mm = None
                        for kc in range(8):
                            w = ([lt, lx] + list(psum_free[nm][bq])) \
                                if kc == 0 else ()
                            mm = p.t(lambda E, kc=kc, tcq=tcq, bb=bb, bq=bq,
                                     ps_=ps_, lhs=lhs: E.matmul(
                                ps_[:, bq * 512:bq * 512 + 512],
                                lhs[bb][:, kc * 128:(kc + 1) * 128],
                                xnqq[:, kc * 1024 + tcq * 512:
                                     kc * 1024 + (tcq + 1) * 512],
                                start=(kc == 0), stop=(kc == 7)), waits=w)
                        st["mm_" + nm] = mm
                        mm_all.append(mm)
                    sts.append(st)
                ltoks[bb] = (sts[-1]["mm_up"], sts[-1]["mm_gate"])
                for st in sts:
                    bq = st["b"]
                    st["evu"] = p.s(lambda E, bq=bq: E.activation(
                        up32b[bq][:], bufs["pA"][:, bq * 512:bq * 512 + 512],
                        AF.Identity),
                        waits=[st["mm_up"]] + list(pt[bq].get("up32", ())))
                    psum_free["up"][bq] = (st["evu"],)
                for st in sts:
                    bq = st["b"]
                    st["evg"] = p.s(lambda E, bq=bq: E.activation(
                        g32b[bq][:], bufs["pB"][:, bq * 512:bq * 512 + 512],
                        AF.Identity),
                        waits=[st["mm_gate"]] + list(pt[bq].get("g32", ())))
                    psum_free["gate"][bq] = (st["evg"],)
                for st in sts:
                    bq = st["b"]
                    st["g1a"] = p.s(lambda E, bq=bq: E.activation(
                        c1b[bq][:], g32b[bq][:], AF.Identity,
                        scale=cols["invs_gate"][:, 0:1], bias=mg),
                        waits=[st["evg"]] + list(pt[bq].get("c1", ())))
                for st in sts:
                    bq = st["b"]
                    st["g1b"] = p.v(lambda E, bq=bq: E.tensor_scalar(
                        g32b[bq][:], c1b[bq][:], -MAGIC,
                        cols["s_gate"][:, 0:1], ALU.add, ALU.mult),
                        waits=[st["g1a"]])
                for st in sts:
                    bq = st["b"]
                    st["sg"] = p.s(lambda E, bq=bq: E.activation(
                        c2b[bq][:], g32b[bq][:], AF.Sigmoid),
                        waits=[st["g1b"]] + list(pt[bq].get("c2", ())))
                for st in sts:
                    bq = st["b"]
                    st["f1a"] = p.s(lambda E, bq=bq: E.activation(
                        c3b[bq][:], c2b[bq][:], AF.Identity, scale=65536.0,
                        bias=mg),
                        waits=[st["sg"]] + list(pt[bq].get("c3", ())))
                for st in sts:
                    bq = st["b"]
                    st["f1b"] = p.v(lambda E, bq=bq: E.tensor_scalar(
                        c2b[bq][:], c3b[bq][:], -MAGIC, 65535.0, ALU.add,
                        ALU.min), waits=[st["f1a"]])
                for st in sts:
                    bq = st["b"]
                    st["pm"] = p.v(lambda E, bq=bq: E.scalar_tensor_tensor(
                        g32b[bq][:], c2b[bq][:], 1.0 / 65536.0, g32b[bq][:],
                        ALU.mult, ALU.mult), waits=[st["f1b"]])
                    pt[bq]["c2"] = (st["pm"],)
                for st in sts:
                    bq = st["b"]
                    st["g2a"] = p.s(lambda E, bq=bq: E.activation(
                        c1b[bq][:], g32b[bq][:], AF.Identity,
                        scale=cols["invs_g2"][:, 0:1], bias=mg),
                        waits=[st["pm"]])
                for st in sts:
                    bq = st["b"]
                    st["g2b"] = p.v(lambda E, bq=bq: E.tensor_scalar(
                        g32b[bq][:], c1b[bq][:], -MAGIC, cols["s_g2"][:, 0:1],
                        ALU.add, ALU.mult), waits=[st["g2a"]])
                for st in sts:
                    bq = st["b"]
                    st["uqa"] = p.s(lambda E, bq=bq: E.activation(
                        c3b[bq][:], up32b[bq][:], AF.Identity,
                        scale=cols["invs_up"][:, 0:1], bias=mg),
                        waits=[st["evu"], st["f1b"]])
                    pt[bq]["up32"] = (st["uqa"],)
                for st in sts:
                    bq = st["b"]
                    st["uqb"] = p.v(lambda E, bq=bq: E.tensor_scalar(
                        c3b[bq][:], c3b[bq][:], -MAGIC, cols["s_up"][:, 0:1],
                        ALU.add, ALU.mult), waits=[st["uqa"]])
                for st in sts:
                    bq = st["b"]
                    st["orw"] = p.v(lambda E, bq=bq: E.tensor_tensor(
                        g32b[bq][:], g32b[bq][:], c3b[bq][:], ALU.mult),
                        waits=[st["uqb"], st["g2b"]])
                    pt[bq]["c3"] = (st["orw"],)
                for st in sts:
                    bq = st["b"]
                    st["oqa"] = p.s(lambda E, bq=bq: E.activation(
                        c1b[bq][:], g32b[bq][:], AF.Identity,
                        scale=cols["invs_o"][:, 0:1], bias=mg),
                        waits=[st["orw"]])
                    pt[bq]["g32"] = (st["oqa"],)
                for st in sts:
                    bq = st["b"]
                    tcq = sts.index(st)
                    st["oqb"] = p.v(lambda E, bq=bq, it=it, tcq=tcq:
                                    E.tensor_scalar(
                        oqq[:, it * 1024 + tcq * 512:
                            it * 1024 + (tcq + 1) * 512],
                        c1b[bq][:], -MAGIC, cols["s_o"][:, 0:1], ALU.add,
                        ALU.mult), waits=[st["oqa"]] +
                        (list(dn_prev) if it == 0 else []))
                    pt[bq]["c1"] = (st["oqb"],)
            lx_prev = tuple(mm_all[-4:])
            stok2 = {0: (), 1: ()}
            oq_done = [st["oqb"] for st in sts]
            for ot in range(8):
                bb = ot % 2
                l3 = p.sdma(ldn[bb][:], wqld_d[:, ot * 3072:(ot + 1) * 3072],
                            waits=stok2[bb], ch=2)
                sts2 = []
                for tcq in range(nch):
                    bq = tcq % 2
                    s2 = {"b": bq}
                    mm = None
                    for ic in range(24):
                        w = ([l3] + oq_done + list(pt2[bq].get("ps", ()))) \
                            if ic == 0 else ()
                        mm = p.t(lambda E, ic=ic, tcq=tcq, bb=bb, bq=bq:
                                 E.matmul(
                            bufs["pA"][:, bq * 512:bq * 512 + 512],
                            ldn[bb][:, ic * 128:(ic + 1) * 128],
                            oqq[:, ic * 1024 + tcq * 512:
                                ic * 1024 + (tcq + 1) * 512],
                            start=(ic == 0), stop=(ic == 23)), waits=w)
                    s2["mm"] = mm
                    tcg = q * 2 + tcq
                    s2["ls"] = p.sdma(
                        sftb[bq][:], sfm_d[ot * 128:(ot + 1) * 128,
                                           tcg * 512:(tcg + 1) * 512],
                        waits=list(pt2[bq].get("sft", ())), ch=1)
                    s2["lxi"] = p.sdma(
                        xtib[bq][:], x[ot * 128:(ot + 1) * 128,
                                       tcg * 512:(tcg + 1) * 512],
                        waits=list(pt2[bq].get("xti", ())), ch=3)
                    s2["tcg"] = tcg
                    sts2.append(s2)
                stok2[bb] = (sts2[-1]["mm"],)
                for s2 in sts2:
                    bq = s2["b"]
                    s2["gv"] = p.s(lambda E, bq=bq: E.activation(
                        g32b[bq][:], bufs["pA"][:, bq * 512:bq * 512 + 512],
                        AF.Silu),
                        waits=[s2["mm"]] + list(pt2[bq].get("g32", ())))
                    pt2[bq]["ps"] = (s2["gv"],)
                for s2 in sts2:
                    bq = s2["b"]
                    s2["v1"] = p.v(lambda E, bq=bq: E.tensor_tensor(
                        x1tb[bq][:], sftb[bq][:], g32b[bq][:], ALU.mult),
                        waits=[s2["gv"], s2["ls"], s2["lxi"]] +
                        list(pt2[bq].get("x1t", ())))
                    pt2[bq]["g32"] = (s2["v1"],)
                    pt2[bq]["sft"] = (s2["v1"],)
                for s2 in sts2:
                    bq = s2["b"]
                    s2["v2"] = p.v(lambda E, bq=bq: E.tensor_tensor(
                        x1tb[bq][:], x1tb[bq][:], xtib[bq][:], ALU.add),
                        waits=[s2["v1"]])
                    pt2[bq]["xti"] = (s2["v2"],)
                for s2 in sts2:
                    bq = s2["b"]
                    so = p.sdma(x1_d[ot * 128:(ot + 1) * 128,
                                     s2["tcg"] * 512:(s2["tcg"] + 1) * 512],
                                x1tb[bq][:], waits=[s2["v2"]], ch=3)
                    pt2[bq]["x1t"] = (so,)
            dn_prev = (sts2[-1]["mm"],)
            psum_free["up"] = [tuple(pt2[bq2].get("ps", ()))
                               for bq2 in range(2)]
        p.wait_on("vector", p.alldma())
        p.flush()

    with ExitStack() as e5:
        sb5 = lambda n, sh, dt=F32: e5.enter_context(
            nc.sbuf_tensor(n, list(sh), dt))
        bufs["mu_row"] = sb5("mu_row", [1, S])
        bufs["sq_row"] = sb5("sq_row", [1, S])
        bufs["a_row"] = sb5("a_row5", [1, S])
        bufs["b_row"] = bufs["mu_row"]
        bufs["a_bc"] = sb5("a_bc2", [128, S])
        bufs["b_bc"] = sb5("b_bc2", [128, S])
        xr2 = [sb5("xrow5%d" % i, [128, S]) for i in range(2)]
        ob2 = [sb5("obf5%d" % i, [128, S], BF16) for i in range(2)]
        em_ln_stats(p, bufs, x1_d, S)
        em_ln_bcast(p, bufs, S, waits=[p.now("ve"), p.now("sc")])
        bw = (p.now("ve"),)
        rd5 = [dict(), dict()]
        st1 = None
        for dc in range(8):
            bq = dc % 2
            xr = xr2[bq]
            ld = p.sdma(xr[:], x1_d[dc * 128:(dc + 1) * 128, :],
                        waits=list(rd5[bq].get("xr", ())), ch=1)
            t1 = p.v(lambda E, xr=xr: E.tensor_tensor(
                xr[:], xr[:], bufs["a_bc"][:], ALU.mult),
                waits=[ld] + list(bw))
            t2 = p.v(lambda E, xr=xr: E.tensor_tensor(
                xr[:], xr[:], bufs["b_bc"][:], ALU.add), waits=[t1])
            c1t = p.v(lambda E, xr=xr, bq=bq: E.tensor_copy(
                ob2[bq][:], xr[:]),
                waits=[t2] + list(rd5[bq].get("obf", ())))
            st1 = p.sdma(xn1_d[:, dc * S:(dc + 1) * S], ob2[bq][:],
                         waits=[c1t], ch=2)
            rd5[bq]["obf"] = (st1,)
            rd5[bq]["xr"] = (c1t,)
        p.wait_on("vector", [st1])
        p.flush()

    with ExitStack() as e8:
        sb8 = lambda n, sh, dt=F32: e8.enter_context(
            nc.sbuf_tensor(n, list(sh), dt))
        Vtok = sb8("Vtok", [128, TT * 1024], BF16)
        Qf = sb8("Qf", [128, 2 * S], BF16)
        Kf = sb8("Kf", [128, 2 * S], BF16)
        with ExitStack() as e8v:
            sb8v = lambda n, sh, dt=F32: e8v.enter_context(
                nc.sbuf_tensor(n, list(sh), dt))
            veff = sb8v("veff", [128, 8 * 1024], BF16)
            with ExitStack() as e8w:
                sb8w = lambda n, sh, dt=F32: e8w.enter_context(
                    nc.sbuf_tensor(n, list(sh), dt))
                wvdT_bf = sb8w("wvdT_bf", [128, 2 * D], BF16)
                wvup_bf = sb8w("wvup_bf", [128, 2 * D], BF16)
                l3_ = p.gdma(wvdT_bf[:].rearrange("p (c f) -> p c f", c=2),
                             wvdT[:].rearrange("(c p) f -> p c f", p=128),
                             ch=1)
                l4_ = p.gdma(wvup_bf[:].rearrange("p (c f) -> p c f", c=2),
                             wvup[:].rearrange("(c p) f -> p c f", p=128),
                             ch=1)
                for dm in range(8):
                    for ocn in range(2):
                        mm = None
                        for rc in range(2):
                            w = [l3_, l4_, p.now("ve")] if rc == 0 else ()
                            mm = p.t(lambda E, dm=dm, ocn=ocn, rc=rc: E.matmul(
                                bufs["pD"][:, 0:512],
                                wvdT_bf[:, rc * D + dm * 128:
                                        rc * D + (dm + 1) * 128],
                                wvup_bf[:, rc * D + ocn * 512:
                                        rc * D + (ocn + 1) * 512],
                                start=(rc == 0), stop=(rc == 1)), waits=w)
                        p.v(lambda E, dm=dm, ocn=ocn: E.tensor_copy(
                            veff[:, dm * 1024 + ocn * 512:
                                 dm * 1024 + (ocn + 1) * 512],
                            bufs["pD"][:, 0:512]), waits=[mm])
                p.flush()
            with ExitStack() as e8x:
                sb8x = lambda n, sh, dt=F32: e8x.enter_context(
                    nc.sbuf_tensor(n, list(sh), dt))
                xn1 = sb8x("xn1", [128, 8 * S], BF16)
                awq_bf = sb8x("awq_bf", [128, 8 * 256], BF16)
                awk_bf = sb8x("awk_bf", [128, 8 * 256], BF16)
                lx1 = p.sdma(xn1[:], xn1_d[:], ch=1)
                l1_ = p.gdma(awq_bf[:].rearrange("p (c f) -> p c f", c=8),
                             awq[:].rearrange("(c p) f -> p c f", p=128))
                l2_ = p.gdma(awk_bf[:].rearrange("p (c f) -> p c f", c=8),
                             awk[:].rearrange("(c p) f -> p c f", p=128))
                for dst, wbf, ltok in ((Qf, awq_bf, l1_), (Kf, awk_bf, l2_)):
                    for ot in range(2):
                        for tc in range(S // 512):
                            mm = None
                            for kc in range(8):
                                w = [ltok, lx1, p.now("ve")] if kc == 0 else ()
                                mm = p.t(lambda E, kc=kc, ot=ot, tc=tc,
                                         wbf=wbf: E.matmul(
                                    bufs["pC"][:, 0:512],
                                    wbf[:, kc * 256 + ot * 128:
                                        kc * 256 + (ot + 1) * 128],
                                    xn1[:, kc * S + tc * 512:
                                        kc * S + (tc + 1) * 512],
                                    start=(kc == 0), stop=(kc == 7)), waits=w)
                            p.v(lambda E, dst=dst, ot=ot, tc=tc: E.tensor_copy(
                                dst[:, ot * S + tc * 512:
                                    ot * S + (tc + 1) * 512],
                                bufs["pC"][:, 0:512]), waits=[mm])
                for tm in range(TT):
                    for ocn in range(2):
                        mm = None
                        for kc in range(8):
                            w = [p.now("ve")] if kc == 0 else ()
                            mm = p.t(lambda E, tm=tm, ocn=ocn, kc=kc: E.matmul(
                                bufs["pC"][:, 0:512],
                                xn1[:, kc * S + tm * 128:kc * S + (tm + 1) * 128],
                                veff[:, kc * 1024 + ocn * 512:
                                     kc * 1024 + (ocn + 1) * 512],
                                start=(kc == 0), stop=(kc == 7)), waits=w)
                        p.v(lambda E, tm=tm, ocn=ocn: E.tensor_copy(
                            Vtok[:, tm * 1024 + ocn * 512:
                                 tm * 1024 + (ocn + 1) * 512],
                            bufs["pC"][:, 0:512]), waits=[mm])
                p.flush()
        logits2 = [sb8("logits_%d" % i, [128, S]) for i in range(2)]
        attnP2 = [sb8("attnP_%d" % i, [128, S], BF16) for i in range(2)]
        attnT2 = [sb8("attnT_%d" % i, [128, S], BF16) for i in range(2)]
        mtmp2 = [sb8("mtmp_%d" % i, [128, 512]) for i in range(2)]
        zb12 = [sb8("zb1_%d" % i, [128, S]) for i in range(2)]
        negmax2 = [sb8("negmax_%d" % i, [128, 1]) for i in range(2)]
        Zc2 = [sb8("Zc_%d" % i, [128, 1]) for i in range(2)]
        rZ2 = [sb8("rZ_%d" % i, [128, 1]) for i in range(2)]
        s1t2 = [sb8("s1t_%d" % i, [128, 128]) for i in range(2)]
        pT2 = [bufs["pT"], bufs["pT2"]]
        avps = [(bufs["pA"], 0), (bufs["pA"], 512), (bufs["pB"], 0),
                (bufs["pB"], 512)]
        pt8 = [dict(), dict()]
        qk_free = {0: (), 1: ()}
        av_free = {0: (), 1: (), 2: (), 3: ()}
        st_free = {0: (), 1: ()}
        ptr_free = {0: (), 1: ()}

        def emit_av(st, bi):
            ntt = st + 1
            ttoks = []
            avlast = list(pt8[bi].get("avlast", ()))
            for tt in range(ntt):
                sp = tt % 2
                w = [pt8[bi]["sc1"]] + list(ptr_free[sp])
                ttk = p.t(lambda E, tt=tt, sp=sp, bi=bi: E.transpose(
                    pT2[sp][:, 0:128], attnP2[bi][:, tt * 128:(tt + 1) * 128],
                    ident_bf[:]), waits=w)
                ev = p.v(lambda E, tt=tt, sp=sp, bi=bi: E.tensor_copy(
                    attnT2[bi][:, tt * 128:(tt + 1) * 128], pT2[sp][:, 0:128]),
                    waits=[ttk] + (avlast if tt == 0 else []))
                ptr_free[sp] = (ev,)
                ttoks.append(ev)
            pt8[bi]["attnP"] = (ttk,)
            mm = None
            for dm in range(8):
                ap = dm % 4
                ps2, off = avps[ap]
                mm = None
                for tt in range(ntt):
                    w = []
                    if tt == 0:
                        w += list(av_free[ap % 2 if False else ap])
                    if dm == 0:
                        w.append(ttoks[tt])
                    mm = p.t(lambda E, dm=dm, tt=tt, ps2=ps2, off=off,
                             ntt=ntt, bi=bi, st=st: E.matmul(
                        ps2[:, off:off + 128],
                        Vtok[:, tt * 1024 + dm * 128:tt * 1024 + (dm + 1) * 128],
                        attnT2[bi][:, tt * 128:(tt + 1) * 128],
                        start=(tt == 0), stop=(tt == ntt - 1)), waits=w)
                sp2 = dm % 2
                ev2 = p.v(lambda E, ps2=ps2, off=off, sp2=sp2: E.tensor_copy(
                    s1t2[sp2][:], ps2[:, off:off + 128]),
                    waits=[mm] + list(st_free[sp2]))
                av_free[ap] = (ev2,)
                so = p.sdma(s1_d[dm * 128:(dm + 1) * 128,
                                 st * 128:(st + 1) * 128],
                            s1t2[sp2][:], waits=[ev2], ch=3)
                st_free[sp2] = (so,)
            pt8[bi]["avlast"] = (mm,)

        pending = None
        for st in range(TT):
            bi = st % 2
            L = st * 128 + 128
            ncc = (L + 511) // 512
            evl = None
            for c in range(ncc):
                cp = c % 2
                ps2 = bufs["pC"] if cp == 0 else bufs["pD"]
                mm = None
                for ot in range(2):
                    w = list(qk_free[cp]) if ot == 0 else ()
                    mm = p.t(lambda E, st=st, c=c, ot=ot, ps2=ps2: E.matmul(
                        ps2[:, 0:512],
                        Qf[:, ot * S + st * 128:ot * S + (st + 1) * 128],
                        Kf[:, ot * S + c * 512:ot * S + (c + 1) * 512],
                        start=(ot == 0), stop=(ot == 1)), waits=w)
                evl = p.s(lambda E, c=c, ps2=ps2, bi=bi: E.activation(
                    logits2[bi][:, c * 512:(c + 1) * 512], ps2[:, 0:512],
                    AF.Identity, scale=0.0625),
                    waits=[mm] + (list(pt8[bi].get("log", ()))
                                  if c == 0 else []))
                qk_free[cp] = (evl,)
            cd = ncc - 1
            a1 = p.g(lambda E, st=st, cd=cd, bi=bi: E.affine_select(
                mtmp2[bi][:], logits2[bi][:, cd * 512:(cd + 1) * 512],
                [[-1, 512]], ALU.is_ge, NEG_FILL, base=st * 128 - cd * 512,
                channel_multiplier=1),
                waits=[evl] + list(pt8[bi].get("mtmp", ())))
            a2 = p.v(lambda E, cd=cd, bi=bi: E.tensor_copy(
                logits2[bi][:, cd * 512:(cd + 1) * 512], mtmp2[bi][:]),
                waits=[a1])
            pt8[bi]["mtmp"] = (a2,)
            Lr = cd * 512 + 512
            z1 = p.v(lambda E, Lr=Lr, bi=bi: E.tensor_scalar(
                zb12[bi][:, 0:Lr], logits2[bi][:, 0:Lr], 0.0, None,
                ALU.is_equal),
                waits=[a2, evl] + list(pt8[bi].get("zb1", ())))
            z2 = p.v(lambda E, Lr=Lr, bi=bi: E.scalar_tensor_tensor(
                logits2[bi][:, 0:Lr], zb12[bi][:, 0:Lr], NEG_FILL,
                logits2[bi][:, 0:Lr], ALU.mult, ALU.add), waits=[z1])
            pt8[bi]["zb1"] = (z2,)
            r1 = p.v(lambda E, Lr=Lr, bi=bi: E.tensor_reduce(
                negmax2[bi][:], logits2[bi][:, 0:Lr], AXX, ALU.max,
                negate=True), waits=[z2])
            ex = p.s(lambda E, Lr=Lr, bi=bi: E.activation(
                attnP2[bi][:, 0:Lr], logits2[bi][:, 0:Lr], AF.Exp,
                bias=negmax2[bi][:, 0:1], accum_out=Zc2[bi][:]),
                waits=[r1] + list(pt8[bi].get("attnP", ())))
            pt8[bi]["log"] = (ex,)
            rc1 = p.v(lambda E, bi=bi: E.reciprocal(rZ2[bi][:], Zc2[bi][:]),
                      waits=[ex])
            sc1 = p.v(lambda E, Lr=Lr, bi=bi: E.tensor_scalar(
                attnP2[bi][:, 0:Lr], attnP2[bi][:, 0:Lr], rZ2[bi][:, 0:1],
                None, ALU.mult), waits=[rc1])
            pt8[bi]["sc1"] = sc1
            if pending is not None:
                emit_av(*pending)
            pending = (st, bi)
        emit_av(*pending)
        p.wait_on("vector", p.alldma())
        p.flush()

    pre_ps.close()
    SCL = float(1.0 / np.sqrt(128.0))
    with ExitStack() as e6:
        sb6 = lambda n, sh, dt=F32: e6.enter_context(
            nc.sbuf_tensor(n, list(sh), dt))
        bufs["pSc"] = e6.enter_context(nc.psum_tensor("pSc", [8, 512], F32))
        pE = e6.enter_context(nc.psum_tensor("pE", [128, 512], F32))
        xn1 = sb6("xn1r", [128, 8 * S], BF16)
        rwq_bf = sb6("rwq_bf", [128, 8 * D], BF16)
        rwk_bf = sb6("rwk_bf", [128, 8 * D], BF16)
        rbq_sb = sb6("rbq_sb", [128, 8])
        rbk_sb = sb6("rbk_sb", [128, 8])
        ewq_bf = sb6("ewq_bf", [128, 8 * 128], BF16)
        ewk_bf = sb6("ewk_bf", [128, 8 * 128], BF16)
        ebqs = sb6("ebqs", [128, 8])
        ebk_sb = sb6("ebk_sb", [128, 8])
        qh = sb6("qh", [128, S], BF16)
        kh = sb6("kh", [128, S], BF16)
        qe = sb6("qe", [128, S], BF16)
        ke = sb6("ke", [128, S], BF16)
        expL = sb6("expL", [128, S], BF16)
        Zcol = sb6("Zcol", [128, 1])
        wcol = sb6("wcol", [128, 1])
        ones8 = sb6("ones8", [128, 8])
        scores_sb = sb6("scores_sb", [8, 8 * 512])
        m8 = scores_sb
        mx8 = sb6("mx8", [8, 512])
        tak8 = sb6("tak8", [8, 512])
        eq8 = sb6("eq8", [8, 512])
        ngr = max(1, TC // 2)
        gw = min(1024, S)
        lx1 = p.sdma(xn1[:], xn1_d[:], ch=1)
        lq = p.gdma(rwq_bf[:].rearrange("p (c f) -> p c f", c=8),
                    rwqT[:].rearrange("(c p) f -> p c f", p=128))
        lk = p.gdma(rwk_bf[:].rearrange("p (c f) -> p c f", c=8),
                    rwkT[:].rearrange("(c p) f -> p c f", p=128))
        leq = p.gdma(ewq_bf[:], ewqT[:], ch=1)
        lek = p.gdma(ewk_bf[:], ewkT[:], ch=1)
        lb1 = p.sdma(rbq_sb[:], rbqc[:], ch=2)
        lb2 = p.sdma(rbk_sb[:], rbkc[:], ch=2)
        lb3 = p.sdma(ebqs[:], ebqc[:], ch=2)
        lb4 = p.sdma(ebk_sb[:], ebkc[:], ch=2)
        w0 = p.v(lambda E: E.memset(ones8[:], 1.0))

        w2 = p.v(lambda E: E.memset(scores_sb[:], 0.0))

        expL2 = [expL, sb6("expL_b", [128, S], BF16),
                 sb6("expL_c", [128, S], BF16)]
        zpart2 = [sb6("zpart_%d" % i, [128, 4]) for i in range(2)]
        Zcol2 = [Zcol, sb6("Zcol_b", [128, 1])]
        wcol2 = [wcol, sb6("wcol_b", [128, 1])]
        wledAB = sb6("wledAB", [128, 128], BF16)
        wled2 = [wledAB[:, 0:64], wledAB[:, 64:128]]
        w1b = p.v(lambda E: E.memset(wledAB[:], 0.0))
        pairs3 = [bufs["pA"], bufs["pB"], bufs["pCD"]]
        bank_tok = {0: (), 1: (), 2: ()}
        bcnt = 0
        zp_tok = {0: (), 1: ()}
        mv_tok3 = {0: (), 1: (), 2: ()}
        mv_tokw = {0: (), 1: ()}
        prev_ea = ()
        pendq = []
        gcnt = 0

        def emit_matvecs(pend_info, prev_ea):
            ei, bi, wl_tok, e_, first, laststl, h_ = pend_info
            tok = None
            for c in range(TC):
                w = ([wl_tok] + list(prev_ea)) if c == 0 else ()
                tok = p.t(lambda E, c=c, ei=ei, bi=bi, first=first,
                          laststl=laststl: E.matmul(
                    bufs["pSc"][0:8, 0:512],
                    wled2[bi][:, c * 8:(c + 1) * 8],
                    expL2[ei][:, c * 512:(c + 1) * 512],
                    start=(first and c == 0),
                    stop=(laststl and c == TC - 1)), waits=w)
            mv_tok3[ei] = (tok,)
            mv_tokw[bi] = (tok,)
            if laststl:
                if h_ == 0:
                    ea = p.v(lambda E, e_=e_: E.tensor_copy(
                        scores_sb[0:8, e_ * 512:(e_ + 1) * 512],
                        bufs["pSc"][0:8, 0:512]), waits=[tok])
                else:
                    ea = p.v(lambda E, e_=e_: E.tensor_tensor(
                        scores_sb[0:8, e_ * 512:(e_ + 1) * 512],
                        scores_sb[0:8, e_ * 512:(e_ + 1) * 512],
                        bufs["pSc"][0:8, 0:512], ALU.add), waits=[tok])
                return (ea,)
            return prev_ea

        pc_tok = ()
        pd_tok = ()
        qh2 = [qh, qh]
        kh2 = [kh, kh]
        qe2 = [qe, sb6("qe_b", [128, S], BF16)]
        ke2 = [ke, sb6("ke_b", [128, S], BF16)]
        state = {"pe": ()}
        qh_done = {}
        qe_ready = {}
        qe_lastread = [(), ()]
        qh_lastread = [(), ()]

        def qhkh_steps(h):
            hb = h % 2
            out = []
            for tc in range(TC):
                for key in ("q", "k"):
                    def step(key=key, tc=tc, h=h, hb=hb):
                        dst = qh2[hb] if key == "q" else kh2[hb]
                        wbf = rwq_bf if key == "q" else rwk_bf
                        bias_sb = rbq_sb if key == "q" else rbk_sb
                        ltok = lq if key == "q" else lk
                        mm = None
                        for kc in range(8):
                            w = ([ltok, lx1, lb1, lb2] + list(state["pe"]) +
                                 list(qh_lastread[hb])) if kc == 0 else ()
                            mm = p.t(lambda E, kc=kc, tc=tc, h=h, wbf=wbf:
                                     E.matmul(
                                pE[:, 0:512],
                                wbf[:, kc * D + h * 128:
                                    kc * D + (h + 1) * 128],
                                xn1[:, kc * S + tc * 512:
                                    kc * S + (tc + 1) * 512],
                                start=(kc == 0), stop=(kc == 7)), waits=w)
                        ev = p.v(lambda E, dst=dst, tc=tc, h=h,
                                 bias_sb=bias_sb: E.tensor_scalar(
                            dst[:, tc * 512:(tc + 1) * 512], pE[:, 0:512],
                            bias_sb[:, h:h + 1], None, ALU.add), waits=[mm])
                        state["pe"] = (ev,)
                        cur = qh_done.get(h, ())
                        qh_done[h] = (cur + (ev,))[-2:]
                    out.append(step)
            return out

        def qeke_steps(h, e):
            hb = h % 2
            be = e % 2
            out = []
            for tc in range(TC):
                def stepQ(tc=tc, h=h, e=e, hb=hb, be=be):
                    qw = list(qh_done.get(h, ()))
                    mq = p.t(lambda E, e=e, tc=tc, hb=hb: E.matmul(
                        pE[:, 0:512],
                        ewq_bf[:, e * 128:(e + 1) * 128],
                        qh2[hb][:, tc * 512:(tc + 1) * 512],
                        start=True, stop=True),
                        waits=[leq] + qw + list(state["pe"]) +
                        list(qe_lastread[be]))
                    evq = p.v(lambda E, e=e, tc=tc, be=be: E.tensor_scalar(
                        qe2[be][:, tc * 512:(tc + 1) * 512],
                        pE[:, 0:512],
                        ebqs[:, e:e + 1], SCL, ALU.add, ALU.mult),
                        waits=[mq, lb3])
                    state["pe"] = (evq,)
                    qh_lastread[hb] = qh_lastread[hb] + (mq,)
                    qe_ready[(h, e)] = (evq,)

                def stepK(tc=tc, h=h, e=e, hb=hb, be=be):
                    qw = list(qh_done.get(h, ()))
                    mk = p.t(lambda E, e=e, tc=tc, hb=hb: E.matmul(
                        pE[:, 0:512],
                        ewk_bf[:, e * 128:(e + 1) * 128],
                        kh2[hb][:, tc * 512:(tc + 1) * 512],
                        start=True, stop=True),
                        waits=[lek] + qw + list(state["pe"]) +
                        list(qe_lastread[be]))
                    evk = p.v(lambda E, e=e, tc=tc, be=be: E.tensor_scalar(
                        ke2[be][:, tc * 512:(tc + 1) * 512],
                        pE[:, 0:512],
                        ebk_sb[:, e:e + 1], None, ALU.add), waits=[mk, lb4])
                    state["pe"] = (evk,)
                    qh_lastread[hb] = (mk,)
                    qe_ready[(h, e)] = qe_ready.get((h, e), ()) + (evk,)
                out.append(stepQ)
                out.append(stepK)
            return out

        bg = []
        for f in qhkh_steps(0):
            f()
        for f in qeke_steps(0, 0):
            f()
        for h in range(8):
            for e in range(8):
                be = e % 2
                if e < 7:
                    bg += qeke_steps(h, e + 1)
                if e == 6 and h < 7:
                    bg += qhkh_steps(h + 1)
                if e == 7 and h < 7:
                    bg += qeke_steps(h + 1, 0)
                lg_last = None
                for stl in range(TT):
                    bi = gcnt % 2
                    ei = gcnt % 3
                    gcnt += 1
                    exp_toks = []
                    for g in range(ngr):
                        gb = bcnt % 3
                        bcnt += 1
                        psg = pairs3[gb]
                        nsub = min(2, TC - g * 2)
                        mm = None
                        for sub in range(nsub):
                            w = list(bank_tok[gb]) if sub == 0 else []
                            if g == 0 and sub == 0 and stl == 0:
                                w = w + list(qe_ready[(h, e)])
                            mm = p.t(lambda E, stl=stl, g=g, sub=sub, psg=psg,
                                     be=be: E.matmul(
                                psg[:, sub * 512:(sub + 1) * 512],
                                qe2[be][:, stl * 128:(stl + 1) * 128],
                                ke2[be][:, (g * 2 + sub) * 512:
                                   (g * 2 + sub + 1) * 512],
                                start=True, stop=True), waits=w)
                        lg_last = mm
                        ew = [mm]
                        if g == 0:
                            ew += list(zp_tok[bi]) + list(mv_tok3[ei])
                        if g == 0:
                            ew += list(zp_tok[bi])
                        et = p.s(lambda E, g=g, psg=psg, nsub=nsub, ei=ei,
                                 bi=bi: E.activation(
                            expL2[ei][:, g * 1024:g * 1024 + nsub * 512],
                            psg[:, 0:nsub * 512], AF.Exp,
                            accum_out=zpart2[bi][:, g:g + 1]), waits=ew)
                        bank_tok[gb] = (et,)
                        exp_toks.append(et)
                        if g == min(1, ngr - 1) and len(pendq) >= 2:
                            prev_ea = emit_matvecs(pendq.pop(0), prev_ea)
                    zz = p.v(lambda E, bi=bi: E.tensor_reduce(
                        Zcol2[bi][:], zpart2[bi][:, 0:ngr], AXX, ALU.add),
                        waits=exp_toks)
                    zp_tok[bi] = (zz,)
                    zr = p.v(lambda E, bi=bi: E.reciprocal(
                        wcol2[bi][:], Zcol2[bi][:]), waits=[zz])
                    wl = p.v(lambda E, bi=bi: E.tensor_scalar(
                        bass.AP(wledAB, bi * 64, [[128, 128], [9, TC]]),
                        ones8[:, 0:TC], wcol2[bi][:, 0:1], None, ALU.mult),
                        waits=[zr, w0, w1b] + list(mv_tokw[bi]))
                    pendq.append((ei, bi, wl, e, stl == 0, stl == TT - 1, h))
                    if bg:
                        bg.pop(0)()
                qe_lastread[be] = (lg_last,)
        while pendq:
            prev_ea = emit_matvecs(pendq.pop(0), prev_ea)
        tvv = p.now("ve")
        sl = lambda t, e: t[0:TC, e * 512:(e + 1) * 512]
        mx1 = p.v(lambda E: E.tensor_copy(mx8[0:TC, :], sl(scores_sb, 0)),
                  waits=[tvv])
        for e in range(1, 8):
            mx1 = p.v(lambda E, e=e: E.tensor_tensor(
                mx8[0:TC, :], mx8[0:TC, :], sl(scores_sb, e), ALU.max),
                waits=[mx1])
        prev = p.v(lambda E: E.memset(tak8[:], 0.0), waits=[mx1])
        for e in range(8):
            q1 = p.v(lambda E, e=e: E.tensor_tensor(
                eq8[0:TC, :], sl(scores_sb, e), mx8[0:TC, :], ALU.is_equal),
                waits=[prev])
            q4 = p.v(lambda E, e=e: E.tensor_scalar(
                sl(m8, e), tak8[0:TC, :], -1.0, 1.0, ALU.mult, ALU.add),
                waits=[q1])
            q5 = p.v(lambda E, e=e: E.tensor_tensor(
                sl(m8, e), sl(m8, e), eq8[0:TC, :], ALU.mult), waits=[q4])
            prev = p.v(lambda E, e=e: E.tensor_tensor(
                tak8[0:TC, :], tak8[0:TC, :], sl(m8, e), ALU.add), waits=[q5])
        de = []
        for e in range(8):
            de.append(p.gdma(
                m_d[e:e + 1, :].rearrange("o (c s) -> (o c) s", c=TC),
                m8[0:TC, e * 512:(e + 1) * 512], waits=[prev], ch=2))
        p.wait_on("vector", [de[-1]])
        p.flush()

    with ExitStack() as e9:
        sb9 = lambda n, sh, dt=F32: e9.enter_context(
            nc.sbuf_tensor(n, list(sh), dt))
        m_fm = sb9("m_fm", [8, S], BF16)
        lmf = p.gdma(m_fm[:], m_d[:], ch=3)
        wexp_bf = [sb9("wexp%d" % e, [128, 8 * 512], BF16) for e in range(8)]
        bexp_bf = sb9("bexp_bf", [8, D], BF16)
        xn1sl = [sb9("xn1sl%d" % i, [128, 8 * 512], BF16) for i in range(2)]
        mrow_t = [sb9("mrow_t%d" % i, [1, 8 * 512]) for i in range(2)]
        maskb2 = [sb9("maskb%d" % i, [128, 512], BF16) for i in range(2)]
        xnm2 = [sb9("xnm%d" % i, [128, 8 * 512], BF16) for i in range(2)]
        g32b2 = [sb9("g32b%d" % i, [128, 512]) for i in range(2)]
        s1tt2 = [sb9("s1tt%d" % i, [128, 512]) for i in range(2)]
        x1tt2 = [sb9("x1tt%d" % i, [128, 512]) for i in range(2)]
        x2t2 = [sb9("x2t%d" % i, [128, 512]) for i in range(2)]
        lbx = p.gdma(bexp_bf[:], bexp[:], ch=2)
        pd_free = ()
        mask_rd = [(), ()]
        xnm_rd = [(), ()]
        xn_rd = [(), ()]
        mr_rd = [(), ()]
        ps_free = [()] * 4
        cmb = [dict(), dict()]
        lwe_rd = ()
        for grp in range(2):
            lwe = []
            for e in range(8):
                lwe.append(p.gdma(
                    wexp_bf[e][:].rearrange("p (c f) -> p c f", c=8),
                    wexp[e][:, grp * 512:(grp + 1) * 512].rearrange(
                        "(c p) f -> p c f", p=128),
                    waits=list(lwe_rd) if e == 0 else ()))
            for tc in range(TC):
                bb = tc % 2
                lxn = p.sdma(
                    xn1sl[bb][:].rearrange("p (c s) -> p c s", c=8),
                    xn1_d[:].rearrange("p (c s) -> p c s", c=8)[
                        :, :, tc * 512:(tc + 1) * 512],
                    waits=list(xn_rd[bb]), ch=1)
                lmr = p.gdma(
                    mrow_t[bb][0:1, :].rearrange("o (e s) -> o e s", e=8),
                    bass.AP(m_d, tc * 512,
                            [[S, 1], [S, 8], [1, 512]]),
                    waits=list(mr_rd[bb]), ch=2)
                mm_last = None
                for e in range(8):
                    eb = e % 2
                    bc = p.t(lambda E, e=e, tc=tc, bb=bb: E.matmul(
                        bufs["pD"][:, 0:512], bufs["oner"][:],
                        mrow_t[bb][0:1, e * 512:(e + 1) * 512],
                        start=True, stop=True),
                        waits=[lmr] + list(pd_free))
                    bv = p.v(lambda E, eb=eb: E.tensor_copy(
                        maskb2[eb][:], bufs["pD"][:, 0:512]),
                        waits=[bc] + list(mask_rd[eb]))
                    pd_free = (bv,)
                    xms = []
                    for kc in range(8):
                        xms.append(p.v(lambda E, kc=kc, bb=bb, eb=eb:
                                       E.tensor_tensor(
                            xnm2[eb][:, kc * 512:(kc + 1) * 512],
                            xn1sl[bb][:, kc * 512:(kc + 1) * 512],
                            maskb2[eb][:], ALU.mult),
                            waits=[bv, lxn] +
                            (list(xnm_rd[eb]) if kc == 0 else [])))
                    mask_rd[eb] = (xms[-1],)
                    for ot in range(4):
                        psg = bufs["pA"] if ot < 2 else bufs["pB"]
                        off = (ot % 2) * 512
                        for kc in range(8):
                            w = [xms[-1], lwe[e]] +                                 (list(ps_free[ot]) if e == 0 else [])                                 if kc == 0 else ()
                            mm_last = p.t(lambda E, e=e, ot=ot, kc=kc,
                                          psg=psg, off=off, eb=eb: E.matmul(
                                psg[:, off:off + 512],
                                wexp_bf[e][:, kc * 512 + ot * 128:
                                            kc * 512 + (ot + 1) * 128],
                                xnm2[eb][:, kc * 512:(kc + 1) * 512],
                                start=(e == 0 and kc == 0), stop=False),
                                waits=w)
                    xnm_rd[eb] = (mm_last,)
                xn_rd[bb] = (mm_last,)
                mr_rd[bb] = (mm_last,)
                for ot in range(4):
                    ob = ot % 2
                    psg = bufs["pA"] if ot < 2 else bufs["pB"]
                    off = (ot % 2) * 512
                    mb = p.t(lambda E, ot=ot, psg=psg, off=off, grp=grp,
                             tc=tc: E.matmul(
                        psg[:, off:off + 512],
                        bexp_bf[0:8, grp * 512 + ot * 128:
                                grp * 512 + (ot + 1) * 128],
                        m_fm[0:8, tc * 512:(tc + 1) * 512],
                        start=False, stop=True), waits=[lbx, lmf])
                    gv = p.s(lambda E, psg=psg, off=off, ob=ob: E.activation(
                        g32b2[ob][:], psg[:, off:off + 512], AF.Silu),
                        waits=[mb] + list(cmb[ob].get("g", ())))
                    ps_free[ot] = (gv,)
                    rr = grp * 512 + ot * 128
                    ls = p.sdma(s1tt2[ob][:],
                                s1_d[rr:rr + 128, tc * 512:(tc + 1) * 512],
                                waits=list(cmb[ob].get("s", ())), ch=2)
                    lx = p.sdma(x1tt2[ob][:],
                                x1_d[rr:rr + 128, tc * 512:(tc + 1) * 512],
                                waits=list(cmb[ob].get("x", ())), ch=3)
                    v1 = p.v(lambda E, ob=ob: E.tensor_tensor(
                        x2t2[ob][:], s1tt2[ob][:], g32b2[ob][:], ALU.mult),
                        waits=[gv, ls, lx] + list(cmb[ob].get("o", ())))
                    cmb[ob]["g"] = (v1,)
                    cmb[ob]["s"] = (v1,)
                    v2 = p.v(lambda E, ob=ob: E.tensor_tensor(
                        x2t2[ob][:], x2t2[ob][:], x1tt2[ob][:], ALU.add),
                        waits=[v1])
                    cmb[ob]["x"] = (v2,)
                    so = p.sdma(o_x2[rr:rr + 128, tc * 512:(tc + 1) * 512],
                                x2t2[ob][:], waits=[v2], ch=0)
                    cmb[ob]["o"] = (so,)
            lwe_rd = (mm_last,)
        p.wait_on("sync", p.alldma())
        p.flush()
    es.close()
    return nc


def em_scalar_cols(p, bufs, m1_ap, invs_col, s_col, waits=()):
    oner = bufs["oner"]
    pD = bufs["pD"]
    s1 = bufs["s1"]
    t1 = p.v(lambda E: E.tensor_scalar(s1[0:1, 0:1], m1_ap, 1.0 / 32767.0,
                                       1e-12, ALU.mult, ALU.max), waits=waits)
    mm = p.t(lambda E: E.matmul(pD[:, 0:1], oner[:], s1[0:1, 0:1], start=True,
                                stop=True), waits=[t1, p.now("ve")])
    t2 = p.v(lambda E: E.tensor_copy(s_col, pD[:, 0:1]), waits=[mm])
    t3 = p.v(lambda E: E.reciprocal(invs_col, s_col), waits=[t2])
    return t3


def em_lpbq_tile(p, bufs, w_hbm, it, I, lhsT_out, ident_bf, cons=(), ch=2,
                 pre="", nb_width=32, chain_tok=()):
    wtile, wq_bf = bufs[pre + "wtile"], bufs[pre + "wq_bf"]
    scl, invscl = bufs[pre + "scl"], bufs[pre + "invscl"]
    tmpw = bufs[pre + "tmpw"]
    pT = bufs["pT"]
    NB = I // 32
    ld = p.sdma(wtile[:, 0:I], w_hbm[it * 128:(it + 1) * 128, :],
                waits=list(cons) + list(chain_tok), ch=ch)
    wv = wtile[:, 0:I].rearrange("p (b i) -> p b i", i=32)
    sv = scl[:, 0:NB]
    t1 = p.v(lambda E: E.tensor_reduce(sv, wv, AXX, ALU.max,
                                       apply_absolute_value=True),
             waits=[ld] + list(chain_tok))
    t2 = p.v(lambda E: E.tensor_scalar(sv, sv, 1.0 / 7.0, 1e-12, ALU.mult,
                                       ALU.max), waits=[t1])
    t3 = p.v(lambda E: E.reciprocal(invscl[:, 0:NB], sv), waits=[t2])
    iv = bass.AP(invscl, 0, [[nb_width, 128], [1, NB], [0, 32]])
    svb = bass.AP(scl, 0, [[nb_width, 128], [1, NB], [0, 32]])
    tv = tmpw[:, 0:I].rearrange("p (b i) -> p b i", i=32)
    t4 = p.v(lambda E: E.tensor_tensor(tv, wv, iv, ALU.mult), waits=[t3])
    t5 = p.v(lambda E: E.tensor_scalar(tmpw[:, 0:I], tmpw[:, 0:I], MAGIC,
                                       -MAGIC, ALU.add, ALU.add), waits=[t4])
    t7 = p.v(lambda E: E.tensor_tensor(
        wq_bf[:, 0:I].rearrange("p (b i) -> p b i", i=32), tv, svb, ALU.mult),
        waits=[t5])
    pT2 = bufs.get("pT2")
    banks = [pT, pT2] if pT2 is not None else [pT]
    free = bufs.setdefault("_pt_free", [()] * len(banks))
    nkc = I // 128
    ev = None
    for kc in range(nkc):
        bk = kc % len(banks)
        tt = p.t(lambda E, kc=kc, bk=bk: E.transpose(
            banks[bk][:, 0:128], wq_bf[:, kc * 128:(kc + 1) * 128],
            ident_bf[:]), waits=[t7] + list(free[bk]))
        ew = [tt] + ([ev] if kc == nkc - 1 and ev is not None else [])
        if kc % 2 == 0:
            ev = p.v(lambda E, kc=kc, bk=bk: E.tensor_copy(
                lhsT_out[:, kc * 128:(kc + 1) * 128], banks[bk][:, 0:128]),
                waits=ew)
        else:
            ev = p.s(lambda E, kc=kc, bk=bk: E.activation(
                lhsT_out[:, kc * 128:(kc + 1) * 128], banks[bk][:, 0:128],
                AF.Identity), waits=ew)
        free[bk] = (ev,)
    return ev, t7


def qdq_chain(p, dst, src, invs_col, s_col, tmp, waits=(), magic_col=None,
              split=True):
    if split and magic_col is not None:
        t1 = p.s(lambda E: E.activation(tmp, src, AF.Identity,
                                        scale=invs_col, bias=magic_col),
                 waits=waits)
    else:
        t1 = p.v(lambda E: E.tensor_scalar(tmp, src, invs_col, MAGIC,
                                           ALU.mult, ALU.add), waits=waits)
    t2 = p.v(lambda E: E.tensor_scalar(dst, tmp, -MAGIC, s_col, ALU.add,
                                       ALU.mult), waits=[t1])
    return t2


def build_ffn_stage(stage, S, n_cores):
    nc = bass.Bass("TRN2", target_bir_lowering=False)
    TC = S // 512
    NC = n_cores
    x = nc.dram_tensor("x", [D, S], F32, kind="ExternalInput")
    ab = nc.dram_tensor("ab", [2, S], F32, kind="ExternalInput")
    mxn = nc.dram_tensor("mxn", [1, NC], F32, kind="ExternalInput")
    wup = nc.dram_tensor("wup", [3072, D], F32, kind="ExternalInput")
    wgate = nc.dram_tensor("wgate", [3072, D], F32, kind="ExternalInput")
    ident = nc.dram_tensor("ident", [128, 128], BF16, kind="ExternalInput")
    if stage in ("C", "D"):
        mgate_i = nc.dram_tensor("mgate", [1, NC], F32, kind="ExternalInput")
    if stage == "D":
        mup_i = nc.dram_tensor("mup", [1, NC], F32, kind="ExternalInput")
        mg2_i = nc.dram_tensor("mg2", [1, NC], F32, kind="ExternalInput")
    nout = {"B": 2, "C": 1, "D": 1}[stage]
    o_max = nc.dram_tensor("o_max", [1, nout], F32, kind="ExternalOutput")

    es = ExitStack()
    bufs, sb, ps = common_bufs(nc, es, S, stats=False, big=False)
    bufs["pT"] = es.enter_context(nc.psum_tensor("pT", [128, 128], BF16))
    bufs["pT2"] = es.enter_context(nc.psum_tensor("pT2", [128, 128], BF16))
    xnq_bf = sb("xnq_bf", [128, 8 * S], BF16)
    bufs["s1"] = sb("s1", [1, 2])
    ident_bf = sb("ident_bf", [128, 128], BF16)
    invs_xn = sb("invs_xn", [128, 1])
    s_xn = sb("s_xn", [128, 1])
    m1a = sb("m1a", [1, 1])
    m8row = sb("m8row", [1, NC])
    mxcol = sb("mxcol", [128, 2])
    magicc = sb("magicc", [128, 1])
    bufs["magicc"] = magicc
    mcol1 = bufs["mcol1"]
    cols = {}
    for nm in ("gate", "up", "g2"):
        cols["invs_" + nm] = sb("invs_" + nm, [128, 1])
        cols["s_" + nm] = sb("s_" + nm, [128, 1])

    p = Prog(nc, es)
    p.v(lambda E: E.memset(bufs["onec"][:], 1.0))
    p.v(lambda E: E.memset(bufs["oner"][:], 1.0))
    p.v(lambda E: E.memset(mxcol[:], 0.0))
    p.v(lambda E: E.memset(magicc[:], MAGIC))
    lid = p.sdma(ident_bf[:], ident[:])
    p.wait_on("tensor", [lid])
    lab = p.sdma(bufs["a_row"][:], ab[0:1, :], ch=1)
    lab2 = p.sdma(bufs["b_row"][:], ab[1:2, :], ch=1)
    p.flush()
    lmx = em_recv_max(p, mxn, m8row, m1a[0:1, 0:1])
    em_scalar_cols(p, bufs, m1a[0:1, 0:1], invs_xn[:], s_xn[:], waits=[lmx])
    for nm, hs in (("gate", "mgate"), ("up", "mup"), ("g2", "mg2")):
        need = (stage in ("C", "D") and nm == "gate") or                (stage == "D" and nm in ("up", "g2"))
        if need:
            hten = mgate_i if hs == "mgate" else (
                mup_i if hs == "mup" else mg2_i)
            lm = em_recv_max(p, hten, m8row, m1a[0:1, 0:1],
                             waits=[p.now("ve")])
            em_scalar_cols(p, bufs, m1a[0:1, 0:1], cols["invs_" + nm][:],
                           cols["s_" + nm][:], waits=[lm])
    with ExitStack() as e1:
        sb1 = lambda n, sh, dt=F32: e1.enter_context(
            nc.sbuf_tensor(n, list(sh), dt))
        bufs["a_bc"] = sb1("a_bc", [128, S])
        bufs["b_bc"] = sb1("b_bc", [128, S])
        bufs["xrow"] = sb1("xrow", [128, S])
        bufs["xrow_b"] = sb1("xrow_b", [128, S])
        bufs["tmprow"] = sb1("tmprow", [128, S])
        bufs["tmprow_b"] = sb1("tmprow_b", [128, S])
        em_ln_bcast(p, bufs, S, waits=[lab, lab2])
        em_xn(p, bufs, x, S, xnq_bf=xnq_bf, q_invs=invs_xn[:, 0:1],
              q_s=s_xn[:, 0:1], waits=[p.now("ve")])
        p.flush()

    e2s = es.enter_context(ExitStack())
    sb2 = lambda n, sh, dt=F32: e2s.enter_context(
        nc.sbuf_tensor(n, list(sh), dt))
    for pre in ("u0_", "u1_", "g0_", "g1_"):
        bufs[pre + "wtile"] = sb2(pre + "wtile", [128, D])
        bufs[pre + "wq_bf"] = sb2(pre + "wq_bf", [128, D], BF16)
        bufs[pre + "scl"] = sb2(pre + "scl", [128, 32])
        bufs[pre + "invscl"] = sb2(pre + "invscl", [128, 32])
        bufs[pre + "tmpw"] = sb2(pre + "tmpw", [128, D])
    lhsT = [sb2("lhsT%d" % i, [128, 8 * 128], BF16) for i in range(4)]
    raw32 = sb2("raw32", [128, 512])
    up32 = sb2("up32", [128, 512])
    c1 = sb2("c1", [128, 512])
    c2 = sb2("c2", [128, 512])
    c3 = sb2("c3", [128, 512])

    raw32b = [raw32, sb2("raw32_b", [128, 512])]
    up32b = [up32, sb2("up32_b", [128, 512])]
    c1b = [c1, sb2("c1_b", [128, 512])]
    c2b = [c2, sb2("c2_b", [128, 512])]
    c3b = [c3, sb2("c3_b", [128, 512])]
    mcol2 = [mcol1, sb2("mcol_b", [128, 1])]
    mg = magicc[:, 0:1]

    names = {"B": ("up", "gate"), "C": ("gate",), "D": ("up", "gate")}[stage]
    whbm = {"up": wup, "gate": wgate}
    ltok = {"up": {0: (), 1: ()}, "gate": {0: (), 1: ()}}
    chain_toks = {"up": {0: (), 1: ()}, "gate": {0: (), 1: ()}}
    lhsT2 = {"up": [lhsT[0], lhsT[1]], "gate": [lhsT[2], lhsT[3]]}
    pt = [dict(), dict()]
    psum_free = {"up": [(), ()], "gate": [(), ()]}
    r2_last = {0: (), 1: ()}
    for it in range(24):
        bb = it % 2
        tls = {}
        for nm in names:
            tls[nm], ct = em_lpbq_tile(
                p, bufs, whbm[nm], it, D, lhsT2[nm][bb], ident_bf,
                cons=ltok[nm][bb], ch=2 if nm == "up" else 3,
                pre=("u%d_" % bb) if nm == "up" else ("g%d_" % bb),
                chain_tok=chain_toks[nm][bb])
            chain_toks[nm][bb] = (ct,)
        mm_last = {nm: None for nm in names}
        for j in range(0, TC, 2):
            sts = []
            for tc in range(j, min(j + 2, TC)):
                bq = tc % 2
                st = {"b": bq}
                for nm in names:
                    ps_ = bufs["pA"] if nm == "up" else bufs["pB"]
                    mm = None
                    for kc in range(8):
                        w = ([tls[nm]] + list(psum_free[nm][bq])) \
                            if kc == 0 else ()
                        mm = p.t(lambda E, kc=kc, tc=tc, nm=nm, bb=bb,
                                 ps_=ps_, bq=bq: E.matmul(
                            ps_[:, bq * 512:bq * 512 + 512],
                            lhsT2[nm][bb][:, kc * 128:(kc + 1) * 128],
                            xnq_bf[:, kc * S + tc * 512:
                                   kc * S + (tc + 1) * 512],
                            start=(kc == 0), stop=(kc == 7)), waits=w)
                    st["mm_" + nm] = mm
                    mm_last[nm] = mm
                sts.append(st)
            if stage == "B":
                for sl, nm in enumerate(names):
                    ps_ = bufs["pA"] if nm == "up" else bufs["pB"]
                    for st in sts:
                        bq = st["b"]
                        r1 = p.v(lambda E, ps_=ps_, bq=bq: E.tensor_reduce(
                            mcol2[bq][:], ps_[:, bq * 512:bq * 512 + 512],
                            AXX, ALU.max, apply_absolute_value=True),
                            waits=[st["mm_" + nm]] + list(pt[bq].get("mc", ())))
                        psum_free[nm][bq] = (r1,)
                        r2 = p.v(lambda E, sl=sl, bq=bq: E.tensor_tensor(
                            mxcol[:, sl:sl + 1], mxcol[:, sl:sl + 1],
                            mcol2[bq][:], ALU.max),
                            waits=[r1] + list(r2_last[sl]))
                        r2_last[sl] = (r2,)
                        pt[bq]["mc"] = (r2,)
                continue
            if stage == "D":
                for st in sts:
                    bq = st["b"]
                    st["evu"] = p.s(lambda E, bq=bq: E.activation(
                        up32b[bq][:], bufs["pA"][:, bq * 512:bq * 512 + 512],
                        AF.Identity),
                        waits=[st["mm_up"]] + list(pt[bq].get("up32", ())))
                    psum_free["up"][bq] = (st["evu"],)
            for st in sts:
                bq = st["b"]
                st["evg"] = p.s(lambda E, bq=bq: E.activation(
                    raw32b[bq][:], bufs["pB"][:, bq * 512:bq * 512 + 512],
                    AF.Identity),
                    waits=[st["mm_gate"]] + list(pt[bq].get("raw32", ())))
                psum_free["gate"][bq] = (st["evg"],)
            for st in sts:
                bq = st["b"]
                st["g1a"] = p.s(lambda E, bq=bq: E.activation(
                    c1b[bq][:], raw32b[bq][:], AF.Identity,
                    scale=cols["invs_gate"][:, 0:1], bias=mg),
                    waits=[st["evg"]] + list(pt[bq].get("c1", ())))
            for st in sts:
                bq = st["b"]
                st["g1b"] = p.v(lambda E, bq=bq: E.tensor_scalar(
                    raw32b[bq][:], c1b[bq][:], -MAGIC,
                    cols["s_gate"][:, 0:1], ALU.add, ALU.mult),
                    waits=[st["g1a"]])
            for st in sts:
                bq = st["b"]
                st["sg"] = p.s(lambda E, bq=bq: E.activation(
                    c2b[bq][:], raw32b[bq][:], AF.Sigmoid),
                    waits=[st["g1b"]] + list(pt[bq].get("c2", ())))
            for st in sts:
                bq = st["b"]
                st["f1a"] = p.s(lambda E, bq=bq: E.activation(
                    c3b[bq][:], c2b[bq][:], AF.Identity, scale=65536.0,
                    bias=mg), waits=[st["sg"]] + list(pt[bq].get("c3", ())))
            for st in sts:
                bq = st["b"]
                st["f1b"] = p.v(lambda E, bq=bq: E.tensor_scalar(
                    c2b[bq][:], c3b[bq][:], -MAGIC, 65535.0, ALU.add,
                    ALU.min), waits=[st["f1a"]])
            for st in sts:
                bq = st["b"]
                st["pm"] = p.v(lambda E, bq=bq: E.scalar_tensor_tensor(
                    raw32b[bq][:], c2b[bq][:], 1.0 / 65536.0, raw32b[bq][:],
                    ALU.mult, ALU.mult), waits=[st["f1b"]])
                pt[bq]["c2"] = (st["pm"],)
            if stage == "C":
                for st in sts:
                    bq = st["b"]
                    pt[bq]["c1"] = (st["g1b"],)
                    pt[bq]["c3"] = (st["f1b"],)
                    r1 = p.v(lambda E, bq=bq: E.tensor_reduce(
                        mcol2[bq][:], raw32b[bq][:], AXX, ALU.max,
                        apply_absolute_value=True),
                        waits=[st["pm"]] + list(pt[bq].get("mc", ())))
                    pt[bq]["raw32"] = (r1,)
                    r2 = p.v(lambda E, bq=bq: E.tensor_tensor(
                        mxcol[:, 0:1], mxcol[:, 0:1], mcol2[bq][:],
                        ALU.max), waits=[r1] + list(r2_last[0]))
                    r2_last[0] = (r2,)
                    pt[bq]["mc"] = (r2,)
                continue
            for st in sts:
                bq = st["b"]
                st["g2a"] = p.s(lambda E, bq=bq: E.activation(
                    c1b[bq][:], raw32b[bq][:], AF.Identity,
                    scale=cols["invs_g2"][:, 0:1], bias=mg),
                    waits=[st["pm"]] + list(pt[bq].get("c1", ())))
            for st in sts:
                bq = st["b"]
                st["g2b"] = p.v(lambda E, bq=bq: E.tensor_scalar(
                    raw32b[bq][:], c1b[bq][:], -MAGIC, cols["s_g2"][:, 0:1],
                    ALU.add, ALU.mult), waits=[st["g2a"]])
                pt[bq]["c1"] = (st["g2b"],)
            for st in sts:
                bq = st["b"]
                st["uqa"] = p.s(lambda E, bq=bq: E.activation(
                    c3b[bq][:], up32b[bq][:], AF.Identity,
                    scale=cols["invs_up"][:, 0:1], bias=mg),
                    waits=[st["evu"], st["f1b"]])
                pt[bq]["up32"] = (st["uqa"],)
            for st in sts:
                bq = st["b"]
                st["uqb"] = p.v(lambda E, bq=bq: E.tensor_scalar(
                    c3b[bq][:], c3b[bq][:], -MAGIC, cols["s_up"][:, 0:1],
                    ALU.add, ALU.mult), waits=[st["uqa"]])
            for st in sts:
                bq = st["b"]
                st["orw"] = p.v(lambda E, bq=bq: E.tensor_tensor(
                    raw32b[bq][:], raw32b[bq][:], c3b[bq][:], ALU.mult),
                    waits=[st["uqb"], st["g2b"]])
                pt[bq]["c3"] = (st["orw"],)
            for st in sts:
                bq = st["b"]
                r1 = p.v(lambda E, bq=bq: E.tensor_reduce(
                    mcol2[bq][:], raw32b[bq][:], AXX, ALU.max,
                    apply_absolute_value=True),
                    waits=[st["orw"]] + list(pt[bq].get("mc", ())))
                pt[bq]["raw32"] = (r1,)
                r2 = p.v(lambda E, bq=bq: E.tensor_tensor(
                    mxcol[:, 0:1], mxcol[:, 0:1], mcol2[bq][:], ALU.max),
                    waits=[r1] + list(r2_last[0]))
                r2_last[0] = (r2,)
                pt[bq]["mc"] = (r2,)
        for nm in names:
            ltok[nm][bb] = (mm_last[nm],)
    mrow = bufs["mrow"]
    res = []
    for sl in range(nout):
        d1 = p.gdma(mrow[0:1, 0:128], mxcol[:, sl:sl + 1],
                    waits=[p.now("ve")], ch=1)
        r1 = p.v(lambda E, sl=sl: E.tensor_reduce(
            bufs["s1"][0:1, sl:sl + 1], mrow[0:1, 0:128], AXX, ALU.max),
            waits=[d1])
        res.append(r1)
    st = p.gdma(o_max[:], bufs["s1"][0:1, 0:nout], waits=res, ch=2)
    p.wait_on("sync", p.alldma())
    p.flush()
    es.close()
    return nc


def fq16_ops_inline(p, dst, src, tmp, waits=(), magic_col=None):
    if magic_col is not None:
        t1 = p.s(lambda E: E.activation(tmp[:], src[:], AF.Identity,
                                        scale=65536.0, bias=magic_col),
                 waits=waits)
    else:
        t1 = p.v(lambda E: E.tensor_scalar(tmp[:], src[:], 65536.0, MAGIC,
                                           ALU.mult, ALU.add), waits=waits)
    t2 = p.v(lambda E: E.tensor_scalar(dst[:], tmp[:], -MAGIC, 65535.0,
                                       ALU.add, ALU.min), waits=[t1])
    return t2




def build_E(S, n_cores):
    nc = bass.Bass("TRN2", target_bir_lowering=False)
    TC = S // 512
    TT = S // 128
    x = nc.dram_tensor("x", [D, S], F32, kind="ExternalInput")
    ab = nc.dram_tensor("ab", [2, S], F32, kind="ExternalInput")
    shat = nc.dram_tensor("shat", [128, 10], F32, kind="ExternalInput")
    NC = n_cores
    m_in = {}
    for nm in ("mxn", "mgate", "mup", "mg2", "mo"):
        m_in[nm] = nc.dram_tensor(nm, [1, NC], F32, kind="ExternalInput")
    wqT = nc.dram_tensor("wqT", [D, D], F32, kind="ExternalInput")
    bqc = nc.dram_tensor("bqc", [128, 8], F32, kind="ExternalInput")
    fpad_h = nc.dram_tensor("fpad", [D, 2 * NF], BF16, kind="ExternalInput")
    cmat_h = nc.dram_tensor("cmat", [NF, 2 * D], BF16, kind="ExternalInput")
    ident = nc.dram_tensor("ident", [128, 128], BF16, kind="ExternalInput")
    wup = nc.dram_tensor("wup", [3072, D], F32, kind="ExternalInput")
    wgate = nc.dram_tensor("wgate", [3072, D], F32, kind="ExternalInput")
    wdown = nc.dram_tensor("wdown", [D, 3072], F32, kind="ExternalInput")
    awq = nc.dram_tensor("awq", [D, 256], F32, kind="ExternalInput")
    awk = nc.dram_tensor("awk", [D, 256], F32, kind="ExternalInput")
    wvdT = nc.dram_tensor("wvdT", [256, D], F32, kind="ExternalInput")
    wvup = nc.dram_tensor("wvup", [256, D], F32, kind="ExternalInput")
    rwqT = nc.dram_tensor("rwqT", [D, D], F32, kind="ExternalInput")
    rwkT = nc.dram_tensor("rwkT", [D, D], F32, kind="ExternalInput")
    rbqc = nc.dram_tensor("rbqc", [128, 8], F32, kind="ExternalInput")
    rbkc = nc.dram_tensor("rbkc", [128, 8], F32, kind="ExternalInput")
    ewqT = nc.dram_tensor("ewqT", [128, 8 * 128], F32, kind="ExternalInput")
    ewkT = nc.dram_tensor("ewkT", [128, 8 * 128], F32, kind="ExternalInput")
    ebqc = nc.dram_tensor("ebqc", [128, 8], F32, kind="ExternalInput")
    ebkc = nc.dram_tensor("ebkc", [128, 8], F32, kind="ExternalInput")
    wexp = nc.dram_tensor("wexp", [8, D, D], F32, kind="ExternalInput")
    bexp = nc.dram_tensor("bexp", [8, D], F32, kind="ExternalInput")
    o_x2 = nc.dram_tensor("o_x2", [D, S], F32, kind="ExternalOutput")
    xn_d = nc.dram_tensor("xn_d", [128, 8 * S], BF16, kind="Internal")
    xnq_d = nc.dram_tensor("xnq_d", [128, 8 * S], BF16, kind="Internal")
    sfm_d = nc.dram_tensor("sfm_d", [D, S], F32, kind="Internal")
    wqlu_d = nc.dram_tensor("wqlu_d", [128, 24 * 1024], BF16, kind="Internal")
    wqlg_d = nc.dram_tensor("wqlg_d", [128, 24 * 1024], BF16, kind="Internal")
    wqld_d = nc.dram_tensor("wqld_d", [128, 8 * 24 * 128], BF16, kind="Internal")
    x1_d = nc.dram_tensor("x1_d", [D, S], F32, kind="Internal")
    xn1_d = nc.dram_tensor("xn1_d", [128, 8 * S], BF16, kind="Internal")
    s1_d = nc.dram_tensor("s1_d", [D, S], F32, kind="Internal")
    m_d = nc.dram_tensor("m_d", [8, S], F32, kind="Internal")

    es = ExitStack()
    sb = lambda n, sh, dt=F32: es.enter_context(nc.sbuf_tensor(n, list(sh), dt))
    ps_ = lambda n, sh, dt=F32: es.enter_context(nc.psum_tensor(n, list(sh), dt))
    bufs = {}
    bufs["pA"] = ps_("pA", [128, 1024])
    bufs["pB"] = ps_("pB", [128, 1024])
    bufs["pCD"] = ps_("pCD", [128, 1024])
    bufs["pC"] = bufs["pCD"][:, 0:512]
    bufs["pD"] = bufs["pCD"][:, 512:1024]
    pre_ps = ExitStack()
    bufs["pT"] = pre_ps.enter_context(nc.psum_tensor("pT", [128, 128], BF16))
    bufs["pT2"] = pre_ps.enter_context(nc.psum_tensor("pT2", [128, 128],
                                                      BF16))
    for nm in ("onec", "mcol1"):
        bufs[nm] = sb(nm, [128, 1])
    bufs["oner"] = sb("oner", [1, 128])
    bufs["eps1"] = sb("eps1", [1, 1])
    bufs["mrow"] = sb("mrow", [1, 128])
    bufs["s1"] = sb("s1", [1, 2])
    bufs["xt2"] = sb("xt2", [128, 1024])
    bufs["sq"] = sb("sq", [128, 512])
    m1a = sb("m1a", [1, 1])
    magicc = sb("magicc", [128, 1])
    bufs["magicc"] = magicc
    ident_bf = sb("ident_bf", [128, 128], BF16)
    cols = {}
    for nm in ("xn", "gate", "up", "g2", "o"):
        cols["invs_" + nm] = sb("invs_" + nm, [128, 1])
        cols["s_" + nm] = sb("s_" + nm, [128, 1])

    p = Prog(nc, es)
    p.v(lambda E: E.memset(bufs["onec"][:], 1.0))
    p.v(lambda E: E.memset(bufs["oner"][:], 1.0))
    p.v(lambda E: E.memset(magicc[:], MAGIC))
    lid = p.sdma(ident_bf[:], ident[:])
    p.wait_on("tensor", [lid])
    m8row = sb("m8row", [1, NC])
    for nm in ("xn", "gate", "up", "g2", "o"):
        lm = em_recv_max(p, m_in[{"xn": "mxn", "gate": "mgate", "up": "mup",
                                  "g2": "mg2", "o": "mo"}[nm]], m8row,
                         m1a[0:1, 0:1], waits=[p.now("ve")])
        em_scalar_cols(p, bufs, m1a[0:1, 0:1], cols["invs_" + nm][:],
                       cols["s_" + nm][:], waits=[lm])
    p.flush()

    with ExitStack() as e1:
        sb1 = lambda n, sh, dt=F32: e1.enter_context(
            nc.sbuf_tensor(n, list(sh), dt))
        bufs["a_row"] = sb1("a_row", [1, S])
        bufs["b_row"] = sb1("b_row", [1, S])
        bufs["a_bc"] = sb1("a_bc", [128, S])
        bufs["b_bc"] = sb1("b_bc", [128, S])
        xr2 = [sb1("xrow%d" % i, [128, S]) for i in range(2)]
        tp2 = [sb1("tmpr%d" % i, [128, S]) for i in range(2)]
        ob2 = [sb1("obf%d" % i, [128, S], BF16) for i in range(2)]
        oq2 = [sb1("obfq%d" % i, [128, S], BF16) for i in range(2)]
        lab = p.sdma(bufs["a_row"][:], ab[0:1, :], ch=1)
        lab2 = p.sdma(bufs["b_row"][:], ab[1:2, :], ch=1)
        em_ln_bcast(p, bufs, S, waits=[lab, lab2])
        bw = (p.now("ve"),)
        rd1 = [dict(), dict()]
        st1 = st2 = None
        for dc in range(8):
            bq = dc % 2
            xr = xr2[bq]
            ld = p.sdma(xr[:], x[dc * 128:(dc + 1) * 128, :],
                        waits=list(rd1[bq].get("xr", ())), ch=1)
            t1 = p.v(lambda E, xr=xr: E.tensor_tensor(
                xr[:], xr[:], bufs["a_bc"][:], ALU.mult),
                waits=[ld] + list(bw))
            t2 = p.v(lambda E, xr=xr: E.tensor_tensor(
                xr[:], xr[:], bufs["b_bc"][:], ALU.add), waits=[t1])
            c1t = p.v(lambda E, xr=xr, bq=bq: E.tensor_copy(
                ob2[bq][:], xr[:]),
                waits=[t2] + list(rd1[bq].get("obf", ())))
            st1 = p.sdma(xn_d[:, dc * S:(dc + 1) * S], ob2[bq][:],
                         waits=[c1t], ch=2)
            rd1[bq]["obf"] = (st1,)
            q3 = qdq_chain(p, oq2[bq][:], xr[:], cols["invs_xn"][:, 0:1],
                           cols["s_xn"][:, 0:1], tp2[bq][:],
                           waits=[t2] + list(rd1[bq].get("oq", ())))
            st2 = p.sdma(xnq_d[:, dc * S:(dc + 1) * S], oq2[bq][:],
                         waits=[q3], ch=3)
            rd1[bq]["oq"] = (st2,)
            rd1[bq]["xr"] = (c1t, q3)
        p.wait_on("vector", [st1, st2])
        p.flush()

    with ExitStack() as e2:
        sb2 = lambda n, sh, dt=F32: e2.enter_context(
            nc.sbuf_tensor(n, list(sh), dt))
        bufs["Fpad"] = sb2("Fpad", [128, 8 * 2 * NF], BF16)
        WqF = sb2("WqF", [128, 8 * 2 * NF], BF16)
        wT_bf = sb2("wT_bf", [128, 8 * D], BF16)
        bc_bf = sb2("bc_bf", [128, 8], BF16)
        bqF = sb2("bqF", [128, 10])
        Cmat = sb2("Cmat", [128, 5 * 2048], BF16)
        Scol = sb2("Scol", [128, 10])
        xnsl = sb2("xnsl", [128, 8 * 512], BF16)
        Q32 = sb2("Q32", [128, 10 * 512])
        Pbuf = sb2("Pbuf", [128, 10 * 512], BF16)
        ctmp = sb2("ctmp", [128, 512])
        sfmt = sb2("sfmt", [128, 512])
        lf = p.sdma(bufs["Fpad"][:].rearrange("p (c f) -> p c f", c=8),
                    fpad_h[:].rearrange("(c p) f -> p c f", p=128))
        lw = p.gdma(wT_bf[:].rearrange("p (c f) -> p c f", c=8),
                    wqT[:].rearrange("(c p) f -> p c f", p=128))
        lb = p.gdma(bc_bf[:], bqc[:])
        lc = p.sdma(Cmat[:].rearrange("p (c f) -> p c f", c=5),
                    cmat_h[:].rearrange("(c p) f -> p c f", p=128), ch=1)
        lsh = p.sdma(Scol[:], shat[:], ch=1)
        em_fold(p, bufs, wT_bf, WqF, bc_bf, bqF, waits=[lf, lw, lb])
        ldx = None
        for tc in range(TC):
            ldx = p.sdma(
                xnsl[:].rearrange("p (c s) -> p c s", c=8),
                xn_d[:].rearrange("p (c s) -> p c s", c=8)[
                    :, :, tc * 512:(tc + 1) * 512],
                waits=[p.now("te")], ch=2)
            em_rfft(p, bufs, WqF, bqF, xnsl, 0, 512, Q32,
                    waits=[ldx, p.now("ve")])
            tq = p.now("sc")
            last = []
            for m in range(5):
                Qr = Q32[:, m * 512:(m + 1) * 512]
                Qi = Q32[:, (m + 5) * 512:(m + 6) * 512]
                Pr = Pbuf[:, m * 512:(m + 1) * 512]
                Pi = Pbuf[:, (m + 5) * 512:(m + 6) * 512]
                u1 = p.v(lambda E, Qr=Qr, m=m: E.tensor_scalar(
                    ctmp[:], Qr, Scol[:, m:m + 1], None, ALU.mult),
                    waits=[tq, lsh, p.now("ve"), p.now("te")])
                u2 = p.v(lambda E, Qi=Qi, Pr=Pr, m=m: E.scalar_tensor_tensor(
                    Pr, Qi, Scol[:, m + 5:m + 6], ctmp[:], ALU.mult, ALU.add),
                    waits=[u1])
                u3 = p.v(lambda E, Qi=Qi, m=m: E.tensor_scalar(
                    ctmp[:], Qi, Scol[:, m:m + 1], -1.0, ALU.mult, ALU.mult),
                    waits=[u2])
                u4 = p.v(lambda E, Qr=Qr, Pi=Pi, m=m: E.scalar_tensor_tensor(
                    Pi, Qr, Scol[:, m + 5:m + 6], ctmp[:], ALU.mult, ALU.add),
                    waits=[u3])
                last.append(u4)
            for dm in range(8):
                mm = None
                for fc in range(5):
                    w = [last[-1]] if fc == 0 else ()
                    mm = p.t(lambda E, dm=dm, fc=fc: E.matmul(
                        bufs["pC"][:, 0:512],
                        Cmat[:, fc * 2048 + dm * 128:fc * 2048 + (dm + 1) * 128],
                        Pbuf[:, fc * 512:(fc + 1) * 512],
                        start=(fc == 0), stop=False), waits=w)
                for fc in range(5):
                    mm = p.t(lambda E, dm=dm, fc=fc: E.matmul(
                        bufs["pC"][:, 0:512],
                        Cmat[:, fc * 2048 + 1024 + dm * 128:
                             fc * 2048 + 1024 + (dm + 1) * 128],
                        Pbuf[:, (fc + 5) * 512:(fc + 6) * 512],
                        start=False, stop=(fc == 4)))
                ev = p.v(lambda E: E.tensor_copy(sfmt[:], bufs["pC"][:, 0:512]),
                         waits=[mm, p.now("sd3")])
                p.wait_on("tensor", [ev])
                p.sdma(sfm_d[dm * 128:(dm + 1) * 128, tc * 512:(tc + 1) * 512],
                       sfmt[:], waits=[ev], ch=3)
        p.flush()

    with ExitStack() as e3:
        sb3 = lambda n, sh, dt=F32: e3.enter_context(
            nc.sbuf_tensor(n, list(sh), dt))
        for pre, I in (("u0_", D), ("u1_", D), ("d0_", 3072), ("d1_", 3072)):
            bufs[pre + "wtile"] = sb3(pre + "wtile", [128, I])
            bufs[pre + "wq_bf"] = sb3(pre + "wq_bf", [128, I], BF16)
            bufs[pre + "scl"] = sb3(pre + "scl", [128, I // 32])
            bufs[pre + "invscl"] = sb3(pre + "invscl", [128, I // 32])
            bufs[pre + "tmpw"] = sb3(pre + "tmpw", [128, I])
        lhsTa = [sb3("lhsTa0", [128, 1024], BF16),
                 sb3("lhsTa1", [128, 1024], BF16)]
        lhsTd = [sb3("lhsTd0", [128, 3072], BF16),
                 sb3("lhsTd1", [128, 3072], BF16)]
        stok = {0: (), 1: ()}
        ctok = {0: (), 1: ()}
        k = 0
        for nm, wh, dst in (("up", wup, wqlu_d), ("gate", wgate, wqlg_d)):
            for it in range(24):
                bb = k % 2
                k += 1
                tl, ct = em_lpbq_tile(p, bufs, wh, it, D, lhsTa[bb], ident_bf,
                                      cons=stok[bb], ch=2 + bb,
                                      pre="u%d_" % bb, chain_tok=ctok[bb])
                ctok[bb] = (ct,)
                st = p.sdma(dst[:, it * 1024:(it + 1) * 1024], lhsTa[bb][:],
                            waits=[tl], ch=bb)
                stok[bb] = (st,)
        stok = {0: (), 1: ()}
        ctok = {0: (), 1: ()}
        for ot in range(8):
            bb = ot % 2
            tl, ct = em_lpbq_tile(p, bufs, wdown, ot, 3072, lhsTd[bb],
                                  ident_bf, cons=stok[bb], ch=2 + bb,
                                  pre="d%d_" % bb, nb_width=96,
                                  chain_tok=ctok[bb])
            ctok[bb] = (ct,)
            st = p.sdma(wqld_d[:, ot * 3072:(ot + 1) * 3072], lhsTd[bb][:],
                        waits=[tl], ch=bb)
            stok[bb] = (st,)
        p.wait_on("vector", p.alldma())
        p.flush()

    NQ = max(1, TC // 2)
    with ExitStack() as e4:
        sb4 = lambda n, sh, dt=F32: e4.enter_context(
            nc.sbuf_tensor(n, list(sh), dt))
        xnqq = sb4("xnqq", [128, 8 * 1024], BF16)
        oqq = sb4("oqq", [128, 24 * 1024], BF16)
        lu = [sb4("lu%d" % i, [128, 1024], BF16) for i in range(2)]
        lg = [sb4("lg%d" % i, [128, 1024], BF16) for i in range(2)]
        ldn = [sb4("ldn%d" % i, [128, 3072], BF16) for i in range(2)]
        up32b = [sb4("up32_%d" % i, [128, 512]) for i in range(2)]
        g32b = [sb4("g32_%d" % i, [128, 512]) for i in range(2)]
        c1b = [sb4("cc1_%d" % i, [128, 512]) for i in range(2)]
        c2b = [sb4("cc2_%d" % i, [128, 512]) for i in range(2)]
        c3b = [sb4("cc3_%d" % i, [128, 512]) for i in range(2)]
        sftb = [sb4("sft_%d" % i, [128, 512]) for i in range(2)]
        xtib = [sb4("xti_%d" % i, [128, 512]) for i in range(2)]
        x1tb = [sb4("x1t_%d" % i, [128, 512]) for i in range(2)]
        mg = magicc[:, 0:1]
        pt = [dict(), dict()]
        psum_free = {"up": [(), ()], "gate": [(), ()]}
        pt2 = [dict(), dict()]
        lx_prev = ()
        dn_prev = ()
        for q in range(NQ):
            nch = min(2, TC - q * 2)
            lx = p.sdma(
                xnqq[:].rearrange("p (c s) -> p c s", c=8)[:, :, 0:nch * 512],
                xnq_d[:].rearrange("p (c s) -> p c s", c=8)[
                    :, :, q * 1024:q * 1024 + nch * 512],
                waits=list(lx_prev), ch=1)
            ltoks = {0: (), 1: ()}
            mm_all = []
            for it in range(24):
                bb = it % 2
                l1 = p.sdma(lu[bb][:], wqlu_d[:, it * 1024:(it + 1) * 1024],
                            waits=ltoks[bb], ch=2)
                l2 = p.sdma(lg[bb][:], wqlg_d[:, it * 1024:(it + 1) * 1024],
                            waits=(), ch=2)
                sts = []
                for tcq in range(nch):
                    bq = tcq % 2
                    st = {"b": bq}
                    for nm, ps_, lt, lhs in (("up", bufs["pA"], l1, lu),
                                             ("gate", bufs["pB"], l2, lg)):
                        mm = None
                        for kc in range(8):
                            w = ([lt, lx] + list(psum_free[nm][bq])) \
                                if kc == 0 else ()
                            mm = p.t(lambda E, kc=kc, tcq=tcq, bb=bb, bq=bq,
                                     ps_=ps_, lhs=lhs: E.matmul(
                                ps_[:, bq * 512:bq * 512 + 512],
                                lhs[bb][:, kc * 128:(kc + 1) * 128],
                                xnqq[:, kc * 1024 + tcq * 512:
                                     kc * 1024 + (tcq + 1) * 512],
                                start=(kc == 0), stop=(kc == 7)), waits=w)
                        st["mm_" + nm] = mm
                        mm_all.append(mm)
                    sts.append(st)
                ltoks[bb] = (sts[-1]["mm_up"], sts[-1]["mm_gate"])
                for st in sts:
                    bq = st["b"]
                    st["evu"] = p.s(lambda E, bq=bq: E.activation(
                        up32b[bq][:], bufs["pA"][:, bq * 512:bq * 512 + 512],
                        AF.Identity),
                        waits=[st["mm_up"]] + list(pt[bq].get("up32", ())))
                    psum_free["up"][bq] = (st["evu"],)
                for st in sts:
                    bq = st["b"]
                    st["evg"] = p.s(lambda E, bq=bq: E.activation(
                        g32b[bq][:], bufs["pB"][:, bq * 512:bq * 512 + 512],
                        AF.Identity),
                        waits=[st["mm_gate"]] + list(pt[bq].get("g32", ())))
                    psum_free["gate"][bq] = (st["evg"],)
                for st in sts:
                    bq = st["b"]
                    st["g1a"] = p.s(lambda E, bq=bq: E.activation(
                        c1b[bq][:], g32b[bq][:], AF.Identity,
                        scale=cols["invs_gate"][:, 0:1], bias=mg),
                        waits=[st["evg"]] + list(pt[bq].get("c1", ())))
                for st in sts:
                    bq = st["b"]
                    st["g1b"] = p.v(lambda E, bq=bq: E.tensor_scalar(
                        g32b[bq][:], c1b[bq][:], -MAGIC,
                        cols["s_gate"][:, 0:1], ALU.add, ALU.mult),
                        waits=[st["g1a"]])
                for st in sts:
                    bq = st["b"]
                    st["sg"] = p.s(lambda E, bq=bq: E.activation(
                        c2b[bq][:], g32b[bq][:], AF.Sigmoid),
                        waits=[st["g1b"]] + list(pt[bq].get("c2", ())))
                for st in sts:
                    bq = st["b"]
                    st["f1a"] = p.s(lambda E, bq=bq: E.activation(
                        c3b[bq][:], c2b[bq][:], AF.Identity, scale=65536.0,
                        bias=mg),
                        waits=[st["sg"]] + list(pt[bq].get("c3", ())))
                for st in sts:
                    bq = st["b"]
                    st["f1b"] = p.v(lambda E, bq=bq: E.tensor_scalar(
                        c2b[bq][:], c3b[bq][:], -MAGIC, 65535.0, ALU.add,
                        ALU.min), waits=[st["f1a"]])
                for st in sts:
                    bq = st["b"]
                    st["pm"] = p.v(lambda E, bq=bq: E.scalar_tensor_tensor(
                        g32b[bq][:], c2b[bq][:], 1.0 / 65536.0, g32b[bq][:],
                        ALU.mult, ALU.mult), waits=[st["f1b"]])
                    pt[bq]["c2"] = (st["pm"],)
                for st in sts:
                    bq = st["b"]
                    st["g2a"] = p.s(lambda E, bq=bq: E.activation(
                        c1b[bq][:], g32b[bq][:], AF.Identity,
                        scale=cols["invs_g2"][:, 0:1], bias=mg),
                        waits=[st["pm"]])
                for st in sts:
                    bq = st["b"]
                    st["g2b"] = p.v(lambda E, bq=bq: E.tensor_scalar(
                        g32b[bq][:], c1b[bq][:], -MAGIC, cols["s_g2"][:, 0:1],
                        ALU.add, ALU.mult), waits=[st["g2a"]])
                for st in sts:
                    bq = st["b"]
                    st["uqa"] = p.s(lambda E, bq=bq: E.activation(
                        c3b[bq][:], up32b[bq][:], AF.Identity,
                        scale=cols["invs_up"][:, 0:1], bias=mg),
                        waits=[st["evu"], st["f1b"]])
                    pt[bq]["up32"] = (st["uqa"],)
                for st in sts:
                    bq = st["b"]
                    st["uqb"] = p.v(lambda E, bq=bq: E.tensor_scalar(
                        c3b[bq][:], c3b[bq][:], -MAGIC, cols["s_up"][:, 0:1],
                        ALU.add, ALU.mult), waits=[st["uqa"]])
                for st in sts:
                    bq = st["b"]
                    st["orw"] = p.v(lambda E, bq=bq: E.tensor_tensor(
                        g32b[bq][:], g32b[bq][:], c3b[bq][:], ALU.mult),
                        waits=[st["uqb"], st["g2b"]])
                    pt[bq]["c3"] = (st["orw"],)
                for st in sts:
                    bq = st["b"]
                    st["oqa"] = p.s(lambda E, bq=bq: E.activation(
                        c1b[bq][:], g32b[bq][:], AF.Identity,
                        scale=cols["invs_o"][:, 0:1], bias=mg),
                        waits=[st["orw"]])
                    pt[bq]["g32"] = (st["oqa"],)
                for st in sts:
                    bq = st["b"]
                    tcq = sts.index(st)
                    st["oqb"] = p.v(lambda E, bq=bq, it=it, tcq=tcq:
                                    E.tensor_scalar(
                        oqq[:, it * 1024 + tcq * 512:
                            it * 1024 + (tcq + 1) * 512],
                        c1b[bq][:], -MAGIC, cols["s_o"][:, 0:1], ALU.add,
                        ALU.mult), waits=[st["oqa"]] +
                        (list(dn_prev) if it == 0 else []))
                    pt[bq]["c1"] = (st["oqb"],)
            lx_prev = tuple(mm_all[-4:])
            stok2 = {0: (), 1: ()}
            oq_done = [st["oqb"] for st in sts]
            for ot in range(8):
                bb = ot % 2
                l3 = p.sdma(ldn[bb][:], wqld_d[:, ot * 3072:(ot + 1) * 3072],
                            waits=stok2[bb], ch=2)
                sts2 = []
                for tcq in range(nch):
                    bq = tcq % 2
                    s2 = {"b": bq}
                    mm = None
                    for ic in range(24):
                        w = ([l3] + oq_done + list(pt2[bq].get("ps", ()))) \
                            if ic == 0 else ()
                        mm = p.t(lambda E, ic=ic, tcq=tcq, bb=bb, bq=bq:
                                 E.matmul(
                            bufs["pA"][:, bq * 512:bq * 512 + 512],
                            ldn[bb][:, ic * 128:(ic + 1) * 128],
                            oqq[:, ic * 1024 + tcq * 512:
                                ic * 1024 + (tcq + 1) * 512],
                            start=(ic == 0), stop=(ic == 23)), waits=w)
                    s2["mm"] = mm
                    tcg = q * 2 + tcq
                    s2["ls"] = p.sdma(
                        sftb[bq][:], sfm_d[ot * 128:(ot + 1) * 128,
                                           tcg * 512:(tcg + 1) * 512],
                        waits=list(pt2[bq].get("sft", ())), ch=1)
                    s2["lxi"] = p.sdma(
                        xtib[bq][:], x[ot * 128:(ot + 1) * 128,
                                       tcg * 512:(tcg + 1) * 512],
                        waits=list(pt2[bq].get("xti", ())), ch=3)
                    s2["tcg"] = tcg
                    sts2.append(s2)
                stok2[bb] = (sts2[-1]["mm"],)
                for s2 in sts2:
                    bq = s2["b"]
                    s2["gv"] = p.s(lambda E, bq=bq: E.activation(
                        g32b[bq][:], bufs["pA"][:, bq * 512:bq * 512 + 512],
                        AF.Silu),
                        waits=[s2["mm"]] + list(pt2[bq].get("g32", ())))
                    pt2[bq]["ps"] = (s2["gv"],)
                for s2 in sts2:
                    bq = s2["b"]
                    s2["v1"] = p.v(lambda E, bq=bq: E.tensor_tensor(
                        x1tb[bq][:], sftb[bq][:], g32b[bq][:], ALU.mult),
                        waits=[s2["gv"], s2["ls"], s2["lxi"]] +
                        list(pt2[bq].get("x1t", ())))
                    pt2[bq]["g32"] = (s2["v1"],)
                    pt2[bq]["sft"] = (s2["v1"],)
                for s2 in sts2:
                    bq = s2["b"]
                    s2["v2"] = p.v(lambda E, bq=bq: E.tensor_tensor(
                        x1tb[bq][:], x1tb[bq][:], xtib[bq][:], ALU.add),
                        waits=[s2["v1"]])
                    pt2[bq]["xti"] = (s2["v2"],)
                for s2 in sts2:
                    bq = s2["b"]
                    so = p.sdma(x1_d[ot * 128:(ot + 1) * 128,
                                     s2["tcg"] * 512:(s2["tcg"] + 1) * 512],
                                x1tb[bq][:], waits=[s2["v2"]], ch=3)
                    pt2[bq]["x1t"] = (so,)
            dn_prev = (sts2[-1]["mm"],)
            psum_free["up"] = [tuple(pt2[bq2].get("ps", ()))
                               for bq2 in range(2)]
        p.wait_on("vector", p.alldma())
        p.flush()

    with ExitStack() as e5:
        sb5 = lambda n, sh, dt=F32: e5.enter_context(
            nc.sbuf_tensor(n, list(sh), dt))
        bufs["mu_row"] = sb5("mu_row", [1, S])
        bufs["sq_row"] = sb5("sq_row", [1, S])
        bufs["a_row"] = sb5("a_row5", [1, S])
        bufs["b_row"] = bufs["mu_row"]
        bufs["a_bc"] = sb5("a_bc2", [128, S])
        bufs["b_bc"] = sb5("b_bc2", [128, S])
        xr2 = [sb5("xrow5%d" % i, [128, S]) for i in range(2)]
        ob2 = [sb5("obf5%d" % i, [128, S], BF16) for i in range(2)]
        em_ln_stats(p, bufs, x1_d, S)
        em_ln_bcast(p, bufs, S, waits=[p.now("ve"), p.now("sc")])
        bw = (p.now("ve"),)
        rd5 = [dict(), dict()]
        st1 = None
        for dc in range(8):
            bq = dc % 2
            xr = xr2[bq]
            ld = p.sdma(xr[:], x1_d[dc * 128:(dc + 1) * 128, :],
                        waits=list(rd5[bq].get("xr", ())), ch=1)
            t1 = p.v(lambda E, xr=xr: E.tensor_tensor(
                xr[:], xr[:], bufs["a_bc"][:], ALU.mult),
                waits=[ld] + list(bw))
            t2 = p.v(lambda E, xr=xr: E.tensor_tensor(
                xr[:], xr[:], bufs["b_bc"][:], ALU.add), waits=[t1])
            c1t = p.v(lambda E, xr=xr, bq=bq: E.tensor_copy(
                ob2[bq][:], xr[:]),
                waits=[t2] + list(rd5[bq].get("obf", ())))
            st1 = p.sdma(xn1_d[:, dc * S:(dc + 1) * S], ob2[bq][:],
                         waits=[c1t], ch=2)
            rd5[bq]["obf"] = (st1,)
            rd5[bq]["xr"] = (c1t,)
        p.wait_on("vector", [st1])
        p.flush()

    with ExitStack() as e8:
        sb8 = lambda n, sh, dt=F32: e8.enter_context(
            nc.sbuf_tensor(n, list(sh), dt))
        Vtok = sb8("Vtok", [128, TT * 1024], BF16)
        Qf = sb8("Qf", [128, 2 * S], BF16)
        Kf = sb8("Kf", [128, 2 * S], BF16)
        with ExitStack() as e8v:
            sb8v = lambda n, sh, dt=F32: e8v.enter_context(
                nc.sbuf_tensor(n, list(sh), dt))
            veff = sb8v("veff", [128, 8 * 1024], BF16)
            with ExitStack() as e8w:
                sb8w = lambda n, sh, dt=F32: e8w.enter_context(
                    nc.sbuf_tensor(n, list(sh), dt))
                wvdT_bf = sb8w("wvdT_bf", [128, 2 * D], BF16)
                wvup_bf = sb8w("wvup_bf", [128, 2 * D], BF16)
                l3_ = p.gdma(wvdT_bf[:].rearrange("p (c f) -> p c f", c=2),
                             wvdT[:].rearrange("(c p) f -> p c f", p=128),
                             ch=1)
                l4_ = p.gdma(wvup_bf[:].rearrange("p (c f) -> p c f", c=2),
                             wvup[:].rearrange("(c p) f -> p c f", p=128),
                             ch=1)
                for dm in range(8):
                    for ocn in range(2):
                        mm = None
                        for rc in range(2):
                            w = [l3_, l4_, p.now("ve")] if rc == 0 else ()
                            mm = p.t(lambda E, dm=dm, ocn=ocn, rc=rc: E.matmul(
                                bufs["pD"][:, 0:512],
                                wvdT_bf[:, rc * D + dm * 128:
                                        rc * D + (dm + 1) * 128],
                                wvup_bf[:, rc * D + ocn * 512:
                                        rc * D + (ocn + 1) * 512],
                                start=(rc == 0), stop=(rc == 1)), waits=w)
                        p.v(lambda E, dm=dm, ocn=ocn: E.tensor_copy(
                            veff[:, dm * 1024 + ocn * 512:
                                 dm * 1024 + (ocn + 1) * 512],
                            bufs["pD"][:, 0:512]), waits=[mm])
                p.flush()
            with ExitStack() as e8x:
                sb8x = lambda n, sh, dt=F32: e8x.enter_context(
                    nc.sbuf_tensor(n, list(sh), dt))
                xn1 = sb8x("xn1", [128, 8 * S], BF16)
                awq_bf = sb8x("awq_bf", [128, 8 * 256], BF16)
                awk_bf = sb8x("awk_bf", [128, 8 * 256], BF16)
                lx1 = p.sdma(xn1[:], xn1_d[:], ch=1)
                l1_ = p.gdma(awq_bf[:].rearrange("p (c f) -> p c f", c=8),
                             awq[:].rearrange("(c p) f -> p c f", p=128))
                l2_ = p.gdma(awk_bf[:].rearrange("p (c f) -> p c f", c=8),
                             awk[:].rearrange("(c p) f -> p c f", p=128))
                for dst, wbf, ltok in ((Qf, awq_bf, l1_), (Kf, awk_bf, l2_)):
                    for ot in range(2):
                        for tc in range(S // 512):
                            mm = None
                            for kc in range(8):
                                w = [ltok, lx1, p.now("ve")] if kc == 0 else ()
                                mm = p.t(lambda E, kc=kc, ot=ot, tc=tc,
                                         wbf=wbf: E.matmul(
                                    bufs["pC"][:, 0:512],
                                    wbf[:, kc * 256 + ot * 128:
                                        kc * 256 + (ot + 1) * 128],
                                    xn1[:, kc * S + tc * 512:
                                        kc * S + (tc + 1) * 512],
                                    start=(kc == 0), stop=(kc == 7)), waits=w)
                            p.v(lambda E, dst=dst, ot=ot, tc=tc: E.tensor_copy(
                                dst[:, ot * S + tc * 512:
                                    ot * S + (tc + 1) * 512],
                                bufs["pC"][:, 0:512]), waits=[mm])
                for tm in range(TT):
                    for ocn in range(2):
                        mm = None
                        for kc in range(8):
                            w = [p.now("ve")] if kc == 0 else ()
                            mm = p.t(lambda E, tm=tm, ocn=ocn, kc=kc: E.matmul(
                                bufs["pC"][:, 0:512],
                                xn1[:, kc * S + tm * 128:kc * S + (tm + 1) * 128],
                                veff[:, kc * 1024 + ocn * 512:
                                     kc * 1024 + (ocn + 1) * 512],
                                start=(kc == 0), stop=(kc == 7)), waits=w)
                        p.v(lambda E, tm=tm, ocn=ocn: E.tensor_copy(
                            Vtok[:, tm * 1024 + ocn * 512:
                                 tm * 1024 + (ocn + 1) * 512],
                            bufs["pC"][:, 0:512]), waits=[mm])
                p.flush()
        logits2 = [sb8("logits_%d" % i, [128, S]) for i in range(2)]
        attnP2 = [sb8("attnP_%d" % i, [128, S], BF16) for i in range(2)]
        attnT2 = [sb8("attnT_%d" % i, [128, S], BF16) for i in range(2)]
        mtmp2 = [sb8("mtmp_%d" % i, [128, 512]) for i in range(2)]
        zb12 = [sb8("zb1_%d" % i, [128, S]) for i in range(2)]
        negmax2 = [sb8("negmax_%d" % i, [128, 1]) for i in range(2)]
        Zc2 = [sb8("Zc_%d" % i, [128, 1]) for i in range(2)]
        rZ2 = [sb8("rZ_%d" % i, [128, 1]) for i in range(2)]
        s1t2 = [sb8("s1t_%d" % i, [128, 128]) for i in range(2)]
        pT2 = [bufs["pT"], bufs["pT2"]]
        avps = [(bufs["pA"], 0), (bufs["pA"], 512), (bufs["pB"], 0),
                (bufs["pB"], 512)]
        pt8 = [dict(), dict()]
        qk_free = {0: (), 1: ()}
        av_free = {0: (), 1: (), 2: (), 3: ()}
        st_free = {0: (), 1: ()}
        ptr_free = {0: (), 1: ()}

        def emit_av(st, bi):
            ntt = st + 1
            ttoks = []
            avlast = list(pt8[bi].get("avlast", ()))
            for tt in range(ntt):
                sp = tt % 2
                w = [pt8[bi]["sc1"]] + list(ptr_free[sp])
                ttk = p.t(lambda E, tt=tt, sp=sp, bi=bi: E.transpose(
                    pT2[sp][:, 0:128], attnP2[bi][:, tt * 128:(tt + 1) * 128],
                    ident_bf[:]), waits=w)
                ev = p.v(lambda E, tt=tt, sp=sp, bi=bi: E.tensor_copy(
                    attnT2[bi][:, tt * 128:(tt + 1) * 128], pT2[sp][:, 0:128]),
                    waits=[ttk] + (avlast if tt == 0 else []))
                ptr_free[sp] = (ev,)
                ttoks.append(ev)
            pt8[bi]["attnP"] = (ttk,)
            mm = None
            for dm in range(8):
                ap = dm % 4
                ps2, off = avps[ap]
                mm = None
                for tt in range(ntt):
                    w = []
                    if tt == 0:
                        w += list(av_free[ap % 2 if False else ap])
                    if dm == 0:
                        w.append(ttoks[tt])
                    mm = p.t(lambda E, dm=dm, tt=tt, ps2=ps2, off=off,
                             ntt=ntt, bi=bi, st=st: E.matmul(
                        ps2[:, off:off + 128],
                        Vtok[:, tt * 1024 + dm * 128:tt * 1024 + (dm + 1) * 128],
                        attnT2[bi][:, tt * 128:(tt + 1) * 128],
                        start=(tt == 0), stop=(tt == ntt - 1)), waits=w)
                sp2 = dm % 2
                ev2 = p.v(lambda E, ps2=ps2, off=off, sp2=sp2: E.tensor_copy(
                    s1t2[sp2][:], ps2[:, off:off + 128]),
                    waits=[mm] + list(st_free[sp2]))
                av_free[ap] = (ev2,)
                so = p.sdma(s1_d[dm * 128:(dm + 1) * 128,
                                 st * 128:(st + 1) * 128],
                            s1t2[sp2][:], waits=[ev2], ch=3)
                st_free[sp2] = (so,)
            pt8[bi]["avlast"] = (mm,)

        pending = None
        for st in range(TT):
            bi = st % 2
            L = st * 128 + 128
            ncc = (L + 511) // 512
            evl = None
            for c in range(ncc):
                cp = c % 2
                ps2 = bufs["pC"] if cp == 0 else bufs["pD"]
                mm = None
                for ot in range(2):
                    w = list(qk_free[cp]) if ot == 0 else ()
                    mm = p.t(lambda E, st=st, c=c, ot=ot, ps2=ps2: E.matmul(
                        ps2[:, 0:512],
                        Qf[:, ot * S + st * 128:ot * S + (st + 1) * 128],
                        Kf[:, ot * S + c * 512:ot * S + (c + 1) * 512],
                        start=(ot == 0), stop=(ot == 1)), waits=w)
                evl = p.s(lambda E, c=c, ps2=ps2, bi=bi: E.activation(
                    logits2[bi][:, c * 512:(c + 1) * 512], ps2[:, 0:512],
                    AF.Identity, scale=0.0625),
                    waits=[mm] + (list(pt8[bi].get("log", ()))
                                  if c == 0 else []))
                qk_free[cp] = (evl,)
            cd = ncc - 1
            a1 = p.g(lambda E, st=st, cd=cd, bi=bi: E.affine_select(
                mtmp2[bi][:], logits2[bi][:, cd * 512:(cd + 1) * 512],
                [[-1, 512]], ALU.is_ge, NEG_FILL, base=st * 128 - cd * 512,
                channel_multiplier=1),
                waits=[evl] + list(pt8[bi].get("mtmp", ())))
            a2 = p.v(lambda E, cd=cd, bi=bi: E.tensor_copy(
                logits2[bi][:, cd * 512:(cd + 1) * 512], mtmp2[bi][:]),
                waits=[a1])
            pt8[bi]["mtmp"] = (a2,)
            Lr = cd * 512 + 512
            z1 = p.v(lambda E, Lr=Lr, bi=bi: E.tensor_scalar(
                zb12[bi][:, 0:Lr], logits2[bi][:, 0:Lr], 0.0, None,
                ALU.is_equal),
                waits=[a2, evl] + list(pt8[bi].get("zb1", ())))
            z2 = p.v(lambda E, Lr=Lr, bi=bi: E.scalar_tensor_tensor(
                logits2[bi][:, 0:Lr], zb12[bi][:, 0:Lr], NEG_FILL,
                logits2[bi][:, 0:Lr], ALU.mult, ALU.add), waits=[z1])
            pt8[bi]["zb1"] = (z2,)
            r1 = p.v(lambda E, Lr=Lr, bi=bi: E.tensor_reduce(
                negmax2[bi][:], logits2[bi][:, 0:Lr], AXX, ALU.max,
                negate=True), waits=[z2])
            ex = p.s(lambda E, Lr=Lr, bi=bi: E.activation(
                attnP2[bi][:, 0:Lr], logits2[bi][:, 0:Lr], AF.Exp,
                bias=negmax2[bi][:, 0:1], accum_out=Zc2[bi][:]),
                waits=[r1] + list(pt8[bi].get("attnP", ())))
            pt8[bi]["log"] = (ex,)
            rc1 = p.v(lambda E, bi=bi: E.reciprocal(rZ2[bi][:], Zc2[bi][:]),
                      waits=[ex])
            sc1 = p.v(lambda E, Lr=Lr, bi=bi: E.tensor_scalar(
                attnP2[bi][:, 0:Lr], attnP2[bi][:, 0:Lr], rZ2[bi][:, 0:1],
                None, ALU.mult), waits=[rc1])
            pt8[bi]["sc1"] = sc1
            if pending is not None:
                emit_av(*pending)
            pending = (st, bi)
        emit_av(*pending)
        p.wait_on("vector", p.alldma())
        p.flush()

    pre_ps.close()
    SCL = float(1.0 / np.sqrt(128.0))
    with ExitStack() as e6:
        sb6 = lambda n, sh, dt=F32: e6.enter_context(
            nc.sbuf_tensor(n, list(sh), dt))
        bufs["pSc"] = e6.enter_context(nc.psum_tensor("pSc", [8, 512], F32))
        pE = e6.enter_context(nc.psum_tensor("pE", [128, 512], F32))
        xn1 = sb6("xn1r", [128, 8 * S], BF16)
        rwq_bf = sb6("rwq_bf", [128, 8 * D], BF16)
        rwk_bf = sb6("rwk_bf", [128, 8 * D], BF16)
        rbq_sb = sb6("rbq_sb", [128, 8])
        rbk_sb = sb6("rbk_sb", [128, 8])
        ewq_bf = sb6("ewq_bf", [128, 8 * 128], BF16)
        ewk_bf = sb6("ewk_bf", [128, 8 * 128], BF16)
        ebqs = sb6("ebqs", [128, 8])
        ebk_sb = sb6("ebk_sb", [128, 8])
        qh = sb6("qh", [128, S], BF16)
        kh = sb6("kh", [128, S], BF16)
        qe = sb6("qe", [128, S], BF16)
        ke = sb6("ke", [128, S], BF16)
        expL = sb6("expL", [128, S], BF16)
        Zcol = sb6("Zcol", [128, 1])
        wcol = sb6("wcol", [128, 1])
        ones8 = sb6("ones8", [128, 8])
        scores_sb = sb6("scores_sb", [8, 8 * 512])
        m8 = scores_sb
        mx8 = sb6("mx8", [8, 512])
        tak8 = sb6("tak8", [8, 512])
        eq8 = sb6("eq8", [8, 512])
        ngr = max(1, TC // 2)
        gw = min(1024, S)
        lx1 = p.sdma(xn1[:], xn1_d[:], ch=1)
        lq = p.gdma(rwq_bf[:].rearrange("p (c f) -> p c f", c=8),
                    rwqT[:].rearrange("(c p) f -> p c f", p=128))
        lk = p.gdma(rwk_bf[:].rearrange("p (c f) -> p c f", c=8),
                    rwkT[:].rearrange("(c p) f -> p c f", p=128))
        leq = p.gdma(ewq_bf[:], ewqT[:], ch=1)
        lek = p.gdma(ewk_bf[:], ewkT[:], ch=1)
        lb1 = p.sdma(rbq_sb[:], rbqc[:], ch=2)
        lb2 = p.sdma(rbk_sb[:], rbkc[:], ch=2)
        lb3 = p.sdma(ebqs[:], ebqc[:], ch=2)
        lb4 = p.sdma(ebk_sb[:], ebkc[:], ch=2)
        w0 = p.v(lambda E: E.memset(ones8[:], 1.0))

        w2 = p.v(lambda E: E.memset(scores_sb[:], 0.0))

        expL2 = [expL, sb6("expL_b", [128, S], BF16),
                 sb6("expL_c", [128, S], BF16)]
        zpart2 = [sb6("zpart_%d" % i, [128, 4]) for i in range(2)]
        Zcol2 = [Zcol, sb6("Zcol_b", [128, 1])]
        wcol2 = [wcol, sb6("wcol_b", [128, 1])]
        wledAB = sb6("wledAB", [128, 128], BF16)
        wled2 = [wledAB[:, 0:64], wledAB[:, 64:128]]
        w1b = p.v(lambda E: E.memset(wledAB[:], 0.0))
        pairs3 = [bufs["pA"], bufs["pB"], bufs["pCD"]]
        bank_tok = {0: (), 1: (), 2: ()}
        bcnt = 0
        zp_tok = {0: (), 1: ()}
        mv_tok3 = {0: (), 1: (), 2: ()}
        mv_tokw = {0: (), 1: ()}
        prev_ea = ()
        pendq = []
        gcnt = 0

        def emit_matvecs(pend_info, prev_ea):
            ei, bi, wl_tok, e_, first, laststl, h_ = pend_info
            tok = None
            for c in range(TC):
                w = ([wl_tok] + list(prev_ea)) if c == 0 else ()
                tok = p.t(lambda E, c=c, ei=ei, bi=bi, first=first,
                          laststl=laststl: E.matmul(
                    bufs["pSc"][0:8, 0:512],
                    wled2[bi][:, c * 8:(c + 1) * 8],
                    expL2[ei][:, c * 512:(c + 1) * 512],
                    start=(first and c == 0),
                    stop=(laststl and c == TC - 1)), waits=w)
            mv_tok3[ei] = (tok,)
            mv_tokw[bi] = (tok,)
            if laststl:
                if h_ == 0:
                    ea = p.v(lambda E, e_=e_: E.tensor_copy(
                        scores_sb[0:8, e_ * 512:(e_ + 1) * 512],
                        bufs["pSc"][0:8, 0:512]), waits=[tok])
                else:
                    ea = p.v(lambda E, e_=e_: E.tensor_tensor(
                        scores_sb[0:8, e_ * 512:(e_ + 1) * 512],
                        scores_sb[0:8, e_ * 512:(e_ + 1) * 512],
                        bufs["pSc"][0:8, 0:512], ALU.add), waits=[tok])
                return (ea,)
            return prev_ea

        pc_tok = ()
        pd_tok = ()
        qh2 = [qh, qh]
        kh2 = [kh, kh]
        qe2 = [qe, sb6("qe_b", [128, S], BF16)]
        ke2 = [ke, sb6("ke_b", [128, S], BF16)]
        state = {"pe": ()}
        qh_done = {}
        qe_ready = {}
        qe_lastread = [(), ()]
        qh_lastread = [(), ()]

        def qhkh_steps(h):
            hb = h % 2
            out = []
            for tc in range(TC):
                for key in ("q", "k"):
                    def step(key=key, tc=tc, h=h, hb=hb):
                        dst = qh2[hb] if key == "q" else kh2[hb]
                        wbf = rwq_bf if key == "q" else rwk_bf
                        bias_sb = rbq_sb if key == "q" else rbk_sb
                        ltok = lq if key == "q" else lk
                        mm = None
                        for kc in range(8):
                            w = ([ltok, lx1, lb1, lb2] + list(state["pe"]) +
                                 list(qh_lastread[hb])) if kc == 0 else ()
                            mm = p.t(lambda E, kc=kc, tc=tc, h=h, wbf=wbf:
                                     E.matmul(
                                pE[:, 0:512],
                                wbf[:, kc * D + h * 128:
                                    kc * D + (h + 1) * 128],
                                xn1[:, kc * S + tc * 512:
                                    kc * S + (tc + 1) * 512],
                                start=(kc == 0), stop=(kc == 7)), waits=w)
                        ev = p.v(lambda E, dst=dst, tc=tc, h=h,
                                 bias_sb=bias_sb: E.tensor_scalar(
                            dst[:, tc * 512:(tc + 1) * 512], pE[:, 0:512],
                            bias_sb[:, h:h + 1], None, ALU.add), waits=[mm])
                        state["pe"] = (ev,)
                        cur = qh_done.get(h, ())
                        qh_done[h] = (cur + (ev,))[-2:]
                    out.append(step)
            return out

        def qeke_steps(h, e):
            hb = h % 2
            be = e % 2
            out = []
            for tc in range(TC):
                def stepQ(tc=tc, h=h, e=e, hb=hb, be=be):
                    qw = list(qh_done.get(h, ()))
                    mq = p.t(lambda E, e=e, tc=tc, hb=hb: E.matmul(
                        pE[:, 0:512],
                        ewq_bf[:, e * 128:(e + 1) * 128],
                        qh2[hb][:, tc * 512:(tc + 1) * 512],
                        start=True, stop=True),
                        waits=[leq] + qw + list(state["pe"]) +
                        list(qe_lastread[be]))
                    evq = p.v(lambda E, e=e, tc=tc, be=be: E.tensor_scalar(
                        qe2[be][:, tc * 512:(tc + 1) * 512],
                        pE[:, 0:512],
                        ebqs[:, e:e + 1], SCL, ALU.add, ALU.mult),
                        waits=[mq, lb3])
                    state["pe"] = (evq,)
                    qh_lastread[hb] = qh_lastread[hb] + (mq,)
                    qe_ready[(h, e)] = (evq,)

                def stepK(tc=tc, h=h, e=e, hb=hb, be=be):
                    qw = list(qh_done.get(h, ()))
                    mk = p.t(lambda E, e=e, tc=tc, hb=hb: E.matmul(
                        pE[:, 0:512],
                        ewk_bf[:, e * 128:(e + 1) * 128],
                        kh2[hb][:, tc * 512:(tc + 1) * 512],
                        start=True, stop=True),
                        waits=[lek] + qw + list(state["pe"]) +
                        list(qe_lastread[be]))
                    evk = p.v(lambda E, e=e, tc=tc, be=be: E.tensor_scalar(
                        ke2[be][:, tc * 512:(tc + 1) * 512],
                        pE[:, 0:512],
                        ebk_sb[:, e:e + 1], None, ALU.add), waits=[mk, lb4])
                    state["pe"] = (evk,)
                    qh_lastread[hb] = (mk,)
                    qe_ready[(h, e)] = qe_ready.get((h, e), ()) + (evk,)
                out.append(stepQ)
                out.append(stepK)
            return out

        bg = []
        for f in qhkh_steps(0):
            f()
        for f in qeke_steps(0, 0):
            f()
        for h in range(8):
            for e in range(8):
                be = e % 2
                if e < 7:
                    bg += qeke_steps(h, e + 1)
                if e == 6 and h < 7:
                    bg += qhkh_steps(h + 1)
                if e == 7 and h < 7:
                    bg += qeke_steps(h + 1, 0)
                lg_last = None
                for stl in range(TT):
                    bi = gcnt % 2
                    ei = gcnt % 3
                    gcnt += 1
                    exp_toks = []
                    for g in range(ngr):
                        gb = bcnt % 3
                        bcnt += 1
                        psg = pairs3[gb]
                        nsub = min(2, TC - g * 2)
                        mm = None
                        for sub in range(nsub):
                            w = list(bank_tok[gb]) if sub == 0 else []
                            if g == 0 and sub == 0 and stl == 0:
                                w = w + list(qe_ready[(h, e)])
                            mm = p.t(lambda E, stl=stl, g=g, sub=sub, psg=psg,
                                     be=be: E.matmul(
                                psg[:, sub * 512:(sub + 1) * 512],
                                qe2[be][:, stl * 128:(stl + 1) * 128],
                                ke2[be][:, (g * 2 + sub) * 512:
                                   (g * 2 + sub + 1) * 512],
                                start=True, stop=True), waits=w)
                        lg_last = mm
                        ew = [mm]
                        if g == 0:
                            ew += list(zp_tok[bi]) + list(mv_tok3[ei])
                        if g == 0:
                            ew += list(zp_tok[bi])
                        et = p.s(lambda E, g=g, psg=psg, nsub=nsub, ei=ei,
                                 bi=bi: E.activation(
                            expL2[ei][:, g * 1024:g * 1024 + nsub * 512],
                            psg[:, 0:nsub * 512], AF.Exp,
                            accum_out=zpart2[bi][:, g:g + 1]), waits=ew)
                        bank_tok[gb] = (et,)
                        exp_toks.append(et)
                        if g == min(1, ngr - 1) and len(pendq) >= 2:
                            prev_ea = emit_matvecs(pendq.pop(0), prev_ea)
                    zz = p.v(lambda E, bi=bi: E.tensor_reduce(
                        Zcol2[bi][:], zpart2[bi][:, 0:ngr], AXX, ALU.add),
                        waits=exp_toks)
                    zp_tok[bi] = (zz,)
                    zr = p.v(lambda E, bi=bi: E.reciprocal(
                        wcol2[bi][:], Zcol2[bi][:]), waits=[zz])
                    wl = p.v(lambda E, bi=bi: E.tensor_scalar(
                        bass.AP(wledAB, bi * 64, [[128, 128], [9, TC]]),
                        ones8[:, 0:TC], wcol2[bi][:, 0:1], None, ALU.mult),
                        waits=[zr, w0, w1b] + list(mv_tokw[bi]))
                    pendq.append((ei, bi, wl, e, stl == 0, stl == TT - 1, h))
                    if bg:
                        bg.pop(0)()
                qe_lastread[be] = (lg_last,)
        while pendq:
            prev_ea = emit_matvecs(pendq.pop(0), prev_ea)
        tvv = p.now("ve")
        sl = lambda t, e: t[0:TC, e * 512:(e + 1) * 512]
        mx1 = p.v(lambda E: E.tensor_copy(mx8[0:TC, :], sl(scores_sb, 0)),
                  waits=[tvv])
        for e in range(1, 8):
            mx1 = p.v(lambda E, e=e: E.tensor_tensor(
                mx8[0:TC, :], mx8[0:TC, :], sl(scores_sb, e), ALU.max),
                waits=[mx1])
        prev = p.v(lambda E: E.memset(tak8[:], 0.0), waits=[mx1])
        for e in range(8):
            q1 = p.v(lambda E, e=e: E.tensor_tensor(
                eq8[0:TC, :], sl(scores_sb, e), mx8[0:TC, :], ALU.is_equal),
                waits=[prev])
            q4 = p.v(lambda E, e=e: E.tensor_scalar(
                sl(m8, e), tak8[0:TC, :], -1.0, 1.0, ALU.mult, ALU.add),
                waits=[q1])
            q5 = p.v(lambda E, e=e: E.tensor_tensor(
                sl(m8, e), sl(m8, e), eq8[0:TC, :], ALU.mult), waits=[q4])
            prev = p.v(lambda E, e=e: E.tensor_tensor(
                tak8[0:TC, :], tak8[0:TC, :], sl(m8, e), ALU.add), waits=[q5])
        de = []
        for e in range(8):
            de.append(p.gdma(
                m_d[e:e + 1, :].rearrange("o (c s) -> (o c) s", c=TC),
                m8[0:TC, e * 512:(e + 1) * 512], waits=[prev], ch=2))
        p.wait_on("vector", [de[-1]])
        p.flush()

    with ExitStack() as e9:
        sb9 = lambda n, sh, dt=F32: e9.enter_context(
            nc.sbuf_tensor(n, list(sh), dt))
        m_fm = sb9("m_fm", [8, S], BF16)
        lmf = p.gdma(m_fm[:], m_d[:], ch=3)
        wexp_bf = [sb9("wexp%d" % e, [128, 8 * 512], BF16) for e in range(8)]
        bexp_bf = sb9("bexp_bf", [8, D], BF16)
        xn1sl = [sb9("xn1sl%d" % i, [128, 8 * 512], BF16) for i in range(2)]
        mrow_t = sb9("mrow_t", [1, 8 * 512])
        maskb = sb9("maskb", [128, 512], BF16)
        xnm = sb9("xnm", [128, 8 * 512], BF16)
        g32b = sb9("g32b", [128, 512])
        s1tt = sb9("s1tt", [128, 512])
        x1tt = sb9("x1tt", [128, 512])
        x2t = sb9("x2t", [128, 512])
        lbx = p.gdma(bexp_bf[:], bexp[:], ch=2)
        for grp in range(2):
            lwe = []
            for e in range(8):
                lwe.append(p.gdma(
                    wexp_bf[e][:].rearrange("p (c f) -> p c f", c=8),
                    wexp[e][:, grp * 512:(grp + 1) * 512].rearrange(
                        "(c p) f -> p c f", p=128),
                    waits=[p.now("te")] if e == 0 else ()))
            for tc in range(TC):
                bb = tc % 2
                lxn = p.sdma(
                    xn1sl[bb][:].rearrange("p (c s) -> p c s", c=8),
                    xn1_d[:].rearrange("p (c s) -> p c s", c=8)[
                        :, :, tc * 512:(tc + 1) * 512],
                    waits=[p.now("te")], ch=1)
                lmr = p.gdma(
                    mrow_t[0:1, :].rearrange("o (e s) -> o e s", e=8),
                    m_d[:, tc * 512:(tc + 1) * 512].rearrange(
                        "e s -> z e s", z=1) if False else
                    bass.AP(m_d, tc * 512,
                            [[S, 1], [S, 8], [1, 512]]),
                    waits=[p.now("te")], ch=2)
                for e in range(8):
                    bc = p.t(lambda E, e=e, tc=tc: E.matmul(
                        bufs["pD"][:, 0:512], bufs["oner"][:],
                        mrow_t[0:1, e * 512:(e + 1) * 512],
                        start=True, stop=True),
                        waits=[lmr, p.now("ve"), p.now("sc")])
                    bv = p.v(lambda E: E.tensor_copy(maskb[:],
                                                     bufs["pD"][:, 0:512]),
                             waits=[bc])
                    xms = []
                    for kc in range(8):
                        xms.append(p.v(lambda E, kc=kc, bb=bb: E.tensor_tensor(
                            xnm[:, kc * 512:(kc + 1) * 512],
                            xn1sl[bb][:, kc * 512:(kc + 1) * 512],
                            maskb[:], ALU.mult),
                            waits=[bv, lxn, p.now("te")]))
                    for ot in range(4):
                        psg = bufs["pA"] if ot < 2 else bufs["pB"]
                        off = (ot % 2) * 512
                        for kc in range(8):
                            w = [xms[-1], lwe[e], p.now("sc")] \
                                if kc == 0 else ()
                            p.t(lambda E, e=e, ot=ot, kc=kc, psg=psg, off=off:
                                E.matmul(
                                psg[:, off:off + 512],
                                wexp_bf[e][:, kc * 512 + ot * 128:
                                            kc * 512 + (ot + 1) * 128],
                                xnm[:, kc * 512:(kc + 1) * 512],
                                start=(e == 0 and kc == 0), stop=False),
                                waits=w)
                for ot in range(4):
                    psg = bufs["pA"] if ot < 2 else bufs["pB"]
                    off = (ot % 2) * 512
                    mb = p.t(lambda E, ot=ot, psg=psg, off=off, grp=grp, tc=tc:
                             E.matmul(
                        psg[:, off:off + 512],
                        bexp_bf[0:8, grp * 512 + ot * 128:
                                grp * 512 + (ot + 1) * 128],
                        m_fm[0:8, tc * 512:(tc + 1) * 512],
                        start=False, stop=True), waits=[lbx, lmf])
                    gv = p.s(lambda E, psg=psg, off=off: E.activation(
                        g32b[:], psg[:, off:off + 512], AF.Silu),
                        waits=[mb, p.now("ve")])
                    rr = grp * 512 + ot * 128
                    ls = p.sdma(s1tt[:], s1_d[rr:rr + 128,
                                              tc * 512:(tc + 1) * 512],
                                waits=[p.now("ve")], ch=2)
                    lx = p.sdma(x1tt[:], x1_d[rr:rr + 128,
                                              tc * 512:(tc + 1) * 512],
                                waits=[p.now("ve")], ch=3)
                    v1 = p.v(lambda E: E.tensor_tensor(x2t[:], s1tt[:],
                                                       g32b[:], ALU.mult),
                             waits=[gv, ls, lx, p.now("sd0")])
                    v2 = p.v(lambda E: E.tensor_tensor(x2t[:], x2t[:],
                                                       x1tt[:], ALU.add),
                             waits=[v1])
                    so = p.sdma(o_x2[rr:rr + 128, tc * 512:(tc + 1) * 512],
                                x2t[:], waits=[v2], ch=0)
                    p.wait_on("tensor", [v2])
        p.wait_on("sync", p.alldma())
        p.flush()
    es.close()
    return nc


def _host_prep(inputs):
    T = lambda a: np.ascontiguousarray(np.asarray(a, np.float32).T)
    st = {}
    x = np.asarray(inputs["L0_attn_x"], np.float32)
    st["x"] = [np.ascontiguousarray(x[b].T) for b in range(x.shape[0])]
    st["wkT"] = T(inputs["L0_attn_Wk"])
    st["wvT"] = T(inputs["L0_attn_Wv"])
    st["wqT"] = T(inputs["L0_attn_Wq"])
    st["bkc"] = _col(inputs["L0_attn_bk"])
    st["bvc"] = _col(inputs["L0_attn_bv"])
    st["bqc"] = _col(inputs["L0_attn_bq"])
    st["fpad"], st["cmat"] = _dft_consts()
    st["ident"] = np.eye(128, dtype=np.float32).astype(BF)
    st["wup"] = np.asarray(inputs["L0_ffn_w_up"], np.float32)
    st["wgate"] = np.asarray(inputs["L0_ffn_w_gate"], np.float32)
    st["wdown"] = np.asarray(inputs["L0_ffn_w_down"], np.float32)
    st["awq"] = np.asarray(inputs["L1_attn_Wq"], np.float32)
    st["awk"] = np.asarray(inputs["L1_attn_Wk"], np.float32)
    st["wvdT"] = T(inputs["L1_attn_Wvdown"])
    st["wvup"] = np.asarray(inputs["L1_attn_Wvup"], np.float32)
    st["rwqT"] = T(inputs["L1_ffn_Wq"])
    st["rwkT"] = T(inputs["L1_ffn_Wk"])
    st["rbqc"] = _col(inputs["L1_ffn_bq"])
    st["rbkc"] = _col(inputs["L1_ffn_bk"])
    st["ewqT"] = np.ascontiguousarray(
        np.asarray(inputs["L1_ffn_eWq"], np.float32).transpose(2, 0, 1)
        .reshape(128, 8 * 128))
    st["ewkT"] = np.ascontiguousarray(
        np.asarray(inputs["L1_ffn_eWk"], np.float32).transpose(2, 0, 1)
        .reshape(128, 8 * 128))
    st["ebqc"] = np.ascontiguousarray(
        np.asarray(inputs["L1_ffn_ebq"], np.float32).T)
    st["ebkc"] = np.ascontiguousarray(
        np.asarray(inputs["L1_ffn_ebk"], np.float32).T)
    st["wexp"] = np.asarray(inputs["L1_ffn_Wexp"], np.float32)
    st["bexp"] = np.asarray(inputs["L1_ffn_bexp"], np.float32)
    return st


def kernel(**inputs):
    st = _host_prep(inputs)
    n = len(st["x"])
    S = st["x"][0].shape[1]
    shared_A = {k: st[k] for k in ("wkT", "wvT", "bkc", "bvc", "fpad")}
    rA = run_bass_kernel_spmd(
        build_A(S, n), [dict(shared_A, x=st["x"][b]) for b in range(n)],
        core_ids=list(range(n)))
    ab = [rA.results[b]["o_ab"] for b in range(n)]
    sh = [rA.results[b]["o_shat"] for b in range(n)]
    mxn = np.concatenate([rA.results[b]["o_mxn"] for b in range(n)], axis=1)
    shared_B = {k: st[k] for k in ("wup", "wgate", "ident")}
    rB = run_bass_kernel_spmd(
        build_ffn_stage("B", S, n),
        [dict(shared_B, x=st["x"][b], ab=ab[b], mxn=mxn) for b in range(n)],
        core_ids=list(range(n)))
    mup = np.concatenate([rB.results[b]["o_max"][:, 0:1] for b in range(n)],
                         axis=1)
    mgate = np.concatenate([rB.results[b]["o_max"][:, 1:2] for b in range(n)],
                           axis=1)
    rC = run_bass_kernel_spmd(
        build_ffn_stage("C", S, n),
        [dict(shared_B, x=st["x"][b], ab=ab[b], mxn=mxn, mgate=mgate)
         for b in range(n)], core_ids=list(range(n)))
    mg2 = np.concatenate([rC.results[b]["o_max"] for b in range(n)], axis=1)
    rD = run_bass_kernel_spmd(
        build_ffn_stage("D", S, n),
        [dict(shared_B, x=st["x"][b], ab=ab[b], mxn=mxn, mgate=mgate,
              mup=mup, mg2=mg2) for b in range(n)], core_ids=list(range(n)))
    mo = np.concatenate([rD.results[b]["o_max"] for b in range(n)], axis=1)
    shared_E = {k: st[k] for k in (
        "wqT", "bqc", "fpad", "cmat", "ident", "wup", "wgate", "wdown",
        "awq", "awk", "wvdT", "wvup", "rwqT", "rwkT", "rbqc", "rbkc",
        "ewqT", "ewkT", "ebqc", "ebkc", "wexp", "bexp")}
    rE = run_bass_kernel_spmd(
        build_E(S, n),
        [dict(shared_E, x=st["x"][b], ab=ab[b], shat=sh[b], mxn=mxn,
              mgate=mgate, mup=mup, mg2=mg2, mo=mo) for b in range(n)],
        core_ids=list(range(n)))
    out = np.stack([np.asarray(rE.results[b]["o_x2"]).T for b in range(n)])
    return np.ascontiguousarray(out.astype(np.float32))
